# version 03
# baseline (speedup 1.0000x reference)
import math
import os
import numpy as np
from contextlib import ExitStack

import concourse.bass as bass
import concourse.mybir as mybir
from concourse.bass_utils import run_bass_kernel_spmd
from concourse.bass import IndirectOffsetOnAxis

F32 = mybir.dt.float32
BF16 = mybir.dt.bfloat16
I32 = mybir.dt.int32
AF = mybir.ActivationFunctionType
ALU = mybir.AluOpType
AX = mybir.AxisListType

B, T, D = 2, 8192, 2048
NCH = 16
H, G, DH = 16, 4, 128
KVD = 512
NE, INTER, ICH = 16, 1408, 11
SINTER, SCH = 2816, 22
QB = 512
NV = 4
TOKC = 2048
TAU = 1024
KSW = T + 128
EXP_SHIFT = -12.0
SM_SCALE = 1.0 / math.sqrt(DH)
EPS = 1e-5
BIG = 1.0e9

PPREF = [2048 * (v + 1) for v in range(NV)]
NPAIR = [p // 128 for p in PPREF]
PAIR_OFF = [0, 16, 48, 96]
NPAIR_TOT = 160

def q0tok(w, v):
    j = w % 4
    return (j + 4 * v) * QB

ENG2BLK = {"pe": "tensor", "dve": "vector", "act": "scalar", "pool": "gpsimd", "sp": "sync"}

class _Op:
    __slots__ = ("eng", "fn", "deps", "inc", "event", "clock", "stream", "waits", "name")
    def __init__(self):
        self.deps = []
        self.inc = False
        self.event = None
        self.clock = None
        self.stream = None
        self.waits = None
        self.name = ""

class Em:
    def __init__(self, nc, es, tag):
        self.nc = nc
        self.es = es
        self.tag = tag
        self.ops = []
        self.lw = {}
        self.rd = {}
        self.sems = {}
        self.scnt = {}

    def _sem(self, key):
        s = self.sems.get(key)
        if s is None:
            s = self.es.enter_context(self.nc.semaphore(f"{self.tag}_{key}".replace(":", "_")))
            self.sems[key] = s
        return s

    def add(self, eng, fn, reads=(), writes=(), name=""):
        op = _Op()
        op.eng = eng
        op.fn = fn
        op.name = name
        deps = {}
        for r in reads:
            w = self.lw.get(r)
            if w is not None:
                deps[id(w)] = w
        for r in writes:
            w = self.lw.get(r)
            if w is not None:
                deps[id(w)] = w
            for rr in self.rd.get(r, ()):
                deps[id(rr)] = rr
        if deps.pop(id(op), None) is not None:
            pass
        op.deps = list(deps.values())
        for r in reads:
            self.rd.setdefault(r, []).append(op)
        for r in writes:
            self.lw[r] = op
            self.rd[r] = [op]
        self.ops.append(op)
        return op

    def dma(self, eng, stream, fn, reads=(), writes=(), name=""):
        op = self.add(eng, fn, reads, writes, name=name)
        op.stream = stream
        k = self.scnt.get(stream, 0) + 1
        self.scnt[stream] = k
        op.event = ("s:" + stream, 16 * k)
        return op

    def stream_count(self, stream):
        return self.scnt.get(stream, 0)

    def emit(self, block):
        for op in self.ops:
            for d in op.deps:
                if d.stream is not None:
                    continue
                if d.eng == "pe" and op.eng == "pe":
                    continue
                d.inc = True
        ctr = {}
        for op in self.ops:
            if op.stream is None and op.inc:
                c = ctr.get(op.eng, 0) + 1
                ctr[op.eng] = c
                op.event = ("e:" + op.eng, c)
        known = {}
        for op in self.ops:
            st = known.setdefault(op.eng, {})
            clk = {}
            need = {}
            for d in op.deps:
                if d.clock:
                    for s2, v2 in d.clock.items():
                        if clk.get(s2, 0) < v2:
                            clk[s2] = v2
                ev = d.event
                if ev is not None:
                    if clk.get(ev[0], 0) < ev[1]:
                        clk[ev[0]] = ev[1]
                    if need.get(ev[0], 0) < ev[1]:
                        need[ev[0]] = ev[1]
            waits = []
            for semk, val in need.items():
                if st.get(semk, 0) < val:
                    waits.append((semk, val))
            if waits:
                for d in op.deps:
                    ev = d.event
                    if ev is not None and st.get(ev[0], 0) < ev[1]:
                        st[ev[0]] = ev[1]
                        if d.clock:
                            for s2, v2 in d.clock.items():
                                if st.get(s2, 0) < v2:
                                    st[s2] = v2
            for s2, v2 in st.items():
                if clk.get(s2, 0) < v2:
                    clk[s2] = v2
            if op.event is not None:
                clk[op.event[0]] = op.event[1]
            op.clock = clk
            op.waits = waits
        per = {}
        for op in self.ops:
            per.setdefault(op.eng, []).append(op)

        def mk_runner(oplist):
            def run(eh):
                for op in oplist:
                    for (semk, val) in op.waits:
                        eh.wait_ge(self._sem(semk), val)
                    try:
                        ins = op.fn(eh)
                    except Exception as ex:
                        raise RuntimeError(f"op failed: {self.tag}/{op.eng}/{op.name} waits={op.waits}") from ex
                    if op.stream is not None:
                        ins.then_inc(self._sem("s:" + op.stream), 16)
                    elif op.inc:
                        ins.then_inc(self._sem("e:" + op.eng), 1)
            return run

        for eng, oplist in per.items():
            getattr(block, ENG2BLK[eng])(mk_runner(oplist))

    def drain(self, eng_for_stream):
        for stream, cnt in list(self.scnt.items()):
            eng = eng_for_stream[stream]
            sem = ("s:" + stream, 16 * cnt)
            op = _Op()
            op.eng = eng
            op.fn = lambda eh: eh.nop()
            op.waits = None
            fake = _Op()
            fake.eng = "?"
            fake.event = sem
            fake.clock = {}
            op.deps = [fake]
            self.ops.append(op)


_CACHED = {}

def _build():
    if "nc" in _CACHED:
        return _CACHED["nc"]
    PHASES = int(os.environ.get("KN_PHASES", "4"))
    nc = bass.Bass("TRN2", target_bir_lowering=False)

    dx = nc.dram_tensor("x", [B, T, D], F32, kind="ExternalInput")
    dwq = nc.dram_tensor("wq", [D, D], F32, kind="ExternalInput")
    dwk = nc.dram_tensor("wk", [D, KVD], F32, kind="ExternalInput")
    dwv = nc.dram_tensor("wv", [D, KVD], F32, kind="ExternalInput")
    dwo = nc.dram_tensor("wo", [D, D], F32, kind="ExternalInput")
    dbo = nc.dram_tensor("bo", [D], F32, kind="ExternalInput")
    dgw = nc.dram_tensor("gw", [NE, D], F32, kind="ExternalInput")
    deb = nc.dram_tensor("eb", [NE], F32, kind="ExternalInput")
    dw1 = nc.dram_tensor("w1", [NE, D, INTER], F32, kind="ExternalInput")
    dw2 = nc.dram_tensor("w2", [NE, INTER, D], F32, kind="ExternalInput")
    dw3 = nc.dram_tensor("w3", [NE, D, INTER], F32, kind="ExternalInput")
    dsw1 = nc.dram_tensor("sw1", [D, SINTER], F32, kind="ExternalInput")
    dsw2 = nc.dram_tensor("sw2", [SINTER, D], F32, kind="ExternalInput")
    dsw3 = nc.dram_tensor("sw3", [D, SINTER], F32, kind="ExternalInput")

    dout = nc.dram_tensor("out_my", [TOKC, D], F32, kind="ExternalOutput")
    doid = nc.dram_tensor("oid", [1, 1], I32, kind="ExternalOutput")
    DEBUG = int(os.environ.get("KN_DEBUG", "0"))
    if DEBUG:
        d_ctx0 = nc.dram_tensor("d_ctx0", [128, H * 512], F32, kind="ExternalOutput")
        d_l0 = nc.dram_tensor("d_l0", [1, 512], F32, kind="ExternalOutput")
        d_ksb0 = nc.dram_tensor("d_ksb0", [128, 16 * 128], F32, kind="ExternalOutput")
        d_vsb0 = nc.dram_tensor("d_vsb0", [128, 16 * 128], F32, kind="ExternalOutput")
        d_q0 = nc.dram_tensor("d_q0", [128, 512], F32, kind="ExternalOutput")
        d_p0 = nc.dram_tensor("d_p0", [128, 1024], F32, kind="ExternalOutput")
        d_x1 = nc.dram_tensor("d_x1", [128, NCH * TOKC], F32, kind="ExternalOutput")
        d_usb0 = nc.dram_tensor("d_usb0", [128, NCH * 512], F32, kind="ExternalOutput")
        d_vdix = nc.dram_tensor("d_vdix", [128, 64], I32, kind="ExternalOutput")
        d_kdix = nc.dram_tensor("d_kdix", [128, 64], I32, kind="ExternalOutput")

    kscr = nc.dram_tensor("kscr", [KVD, KSW], BF16)
    vscr = nc.dram_tensor("vscr", [KSW, KVD], BF16)

    top = ExitStack()
    _CACHED["top"] = top
    E_ = top.enter_context

    sched_i = E_(nc.sbuf_tensor("sched_i", [1, 8], I32))
    sched_f = E_(nc.sbuf_tensor("sched_f", [1, 16], F32))
    id_bf = E_(nc.sbuf_tensor("id_bf", [128, 128], BF16))
    id_f = E_(nc.sbuf_tensor("id_f", [128, 128], F32))
    ones_col_bf = E_(nc.sbuf_tensor("ones_col_bf", [128, 1], BF16))
    ones_row_bf = E_(nc.sbuf_tensor("ones_row_bf", [1, 128], BF16))
    ones_row_f = E_(nc.sbuf_tensor("ones_row_f", [1, 128], F32))
    c_scale = E_(nc.sbuf_tensor("c_scale", [128, 1], F32))
    c_shift = E_(nc.sbuf_tensor("c_shift", [128, 1], F32))
    c_eps = E_(nc.sbuf_tensor("c_eps", [128, 1], F32))
    c_nhalf = E_(nc.sbuf_tensor("c_nhalf", [128, 1], F32))
    bo_sb = E_(nc.sbuf_tensor("bo_sb", [128, NCH], F32))
    ebias_b = E_(nc.sbuf_tensor("ebias_b", [128, NE], F32))
    gw_fm = E_(nc.sbuf_tensor("gw_fm", [128, NCH * NE], BF16))
    oneh = E_(nc.sbuf_tensor("oneh", [16, NE * 128], BF16))
    kdix = E_(nc.sbuf_tensor("kdix", [128, 64], I32))
    vdix = E_(nc.sbuf_tensor("vdix", [128, 64], I32))
    mcol = E_(nc.sbuf_tensor("mcol", [128, 48], F32))
    X1 = E_(nc.sbuf_tensor("X1", [128, NCH * TOKC], BF16))

    ps_s1 = E_(nc.psum_tensor("ps_s1", [128, 1024], F32))
    ps_s2 = E_(nc.psum_tensor("ps_s2", [128, 1024], F32))
    ps_c1 = E_(nc.psum_tensor("ps_c1", [128, 512], F32))
    ps_c2 = E_(nc.psum_tensor("ps_c2", [128, 512], F32))
    ps_c3 = E_(nc.psum_tensor("ps_c3", [128, 512], F32))
    ps_tb = E_(nc.psum_tensor("ps_tb", [128, 1024], BF16))

    r_g1 = nc.gpsimd.alloc_register("r_g1")
    r_g2 = nc.gpsimd.alloc_register("r_g2")

    for arm in nc.switch_core_id([nc.vector], n=128):
        w = int(arm) % 8
        j, b = w % 4, w // 4
        nc.vector.memset(sched_i[0:1, 0:1], w)
        nc.vector.memset(sched_i[0:1, 1:2], b * T * D)
        for v in range(NV):
            nc.vector.memset(sched_i[0:1, 2 + v:3 + v], (b * T + q0tok(w, v)) * D)
        for v in range(NV):
            nc.vector.memset(sched_f[0:1, v:v + 1], float(q0tok(w, v)))


    with ExitStack() as ph:
        Ep = ph.enter_context
        bo_t = Ep(nc.sbuf_tensor("bo_t", [16, 128], F32))
        eb_t = Ep(nc.sbuf_tensor("eb_t", [1, NE], F32))
        gw_t = Ep(nc.sbuf_tensor("gw_t", [16, D], BF16))
        itf = Ep(nc.sbuf_tensor("itf", [1, 16], F32))
        m1f = Ep(nc.sbuf_tensor("m1f", [1, 16], F32))
        m2f = Ep(nc.sbuf_tensor("m2f", [1, 16], F32))
        iop = Ep(nc.sbuf_tensor("iop", [128, 1], F32))
        io65 = Ep(nc.sbuf_tensor("io65", [128, 1], F32))
        gb = Ep(nc.sbuf_tensor("gb", [128, 4], F32))
        q0b = Ep(nc.sbuf_tensor("q0b", [128, 4], F32))
        q0tb = Ep(nc.sbuf_tensor("q0tb", [128, 4], F32))
        q128r = Ep(nc.sbuf_tensor("q128r", [1, 4], F32))
        kdf = Ep(nc.sbuf_tensor("kdf", [128, 64], F32))
        vdf = Ep(nc.sbuf_tensor("vdf", [128, 64], F32))
        vtmp = Ep(nc.sbuf_tensor("vtmp", [128, 1], F32))
        em = Em(nc, top, "p0")

        em.dma("pool", "ld", lambda e: e.dma_start(bo_t[:, :], dbo.ap().rearrange("(c p) -> c p", p=128)), writes=["bo_t"])
        em.dma("pool", "ld", lambda e: e.dma_start(eb_t[:, :], deb.ap().unsqueeze(0)), writes=["eb_t"])
        em.dma("pool", "ld", lambda e: e.dma_start(gw_t[:, :], dgw.ap()), writes=["gw_t"])

        em.add("pool", lambda e: e.memset(id_bf[:, :], 0.0), writes=["id_bf"])
        em.add("pool", lambda e: e.affine_select(out=id_bf[:, :], in_=id_bf[:, :], compare_op=ALU.not_equal,
                                                 fill=1.0, base=0, pattern=[[-1, 128]], channel_multiplier=1),
               reads=["id_bf"], writes=["id_bf"])
        em.add("pool", lambda e: e.memset(id_f[:, :], 0.0), writes=["id_f"])
        em.add("pool", lambda e: e.affine_select(out=id_f[:, :], in_=id_f[:, :], compare_op=ALU.not_equal,
                                                 fill=1.0, base=0, pattern=[[-1, 128]], channel_multiplier=1),
               reads=["id_f"], writes=["id_f"])
        em.add("pool", lambda e: e.memset(ones_col_bf[:, :], 1.0), writes=["ones_col"])
        em.add("pool", lambda e: e.memset(ones_row_bf[:, :], 1.0), writes=["ones_row"])
        em.add("pool", lambda e: e.memset(ones_row_f[:, :], 1.0), writes=["ones_row_f"])
        em.add("pool", lambda e: e.memset(c_scale[:, :], SM_SCALE), writes=["c_scale"])
        em.add("pool", lambda e: e.memset(c_shift[:, :], EXP_SHIFT), writes=["c_shift"])
        em.add("pool", lambda e: e.memset(c_eps[:, :], EPS), writes=["c_eps"])
        em.add("pool", lambda e: e.memset(c_nhalf[:, :], -0.5), writes=["c_nhalf"])
        em.add("pool", lambda e: e.memset(oneh[:, :], 0.0), writes=["oneh"])
        em.add("pool", lambda e: e.affine_select(out=oneh[:, :], in_=oneh[:, :], compare_op=ALU.not_equal,
                                                 fill=1.0, base=0, pattern=[[-1, NE], [0, 128]], channel_multiplier=1),
               reads=["oneh"], writes=["oneh"])

        em.add("pe", lambda e: e.transpose(ps_c1[:, 0:16], bo_t[:, :], id_f[0:16, 0:16]),
               reads=["bo_t", "id_f"], writes=["c1"])
        em.add("dve", lambda e: e.tensor_copy(bo_sb[:, :], ps_c1[:, 0:16]), reads=["c1"], writes=["bo_sb"])
        em.add("pe", lambda e: e.matmul(ps_c2[:, 0:NE], ones_row_f[:, :], eb_t[:, :],
                                        start=True, stop=True),
               reads=["ones_row_f", "eb_t"], writes=["c2"])
        em.add("dve", lambda e: e.tensor_copy(ebias_b[:, :], ps_c2[:, 0:NE]), reads=["c2"], writes=["ebias_b"])
        for c in range(NCH):
            em.add("pe", (lambda c: lambda e: e.transpose(ps_tb[:, c * 16:(c + 1) * 16],
                                                          gw_t[:, c * 128:(c + 1) * 128],
                                                          id_bf[0:16, 0:16]))(c),
                   reads=["gw_t", "id_bf"], writes=["tb"])
        em.add("dve", lambda e: e.tensor_copy(gw_fm[:, 0:NCH * NE], ps_tb[:, 0:NCH * NE]),
               reads=["tb"], writes=["gw_fm"])

        em.add("pool", lambda e: e.iota(itf[0:1, 0:12], [[1, 12]], base=0, channel_multiplier=0,
                                        allow_small_or_imprecise_dtypes=True), writes=["i12"])
        em.add("pool", lambda e: e.iota(iop[:, :], [[0, 1]], base=0, channel_multiplier=1,
                                        allow_small_or_imprecise_dtypes=True), writes=["iop"])
        em.add("pool", lambda e: e.iota(io65[:, :], [[0, 1]], base=0, channel_multiplier=65,
                                        allow_small_or_imprecise_dtypes=True), writes=["io65"])
        em.add("dve", lambda e: e.tensor_scalar(q128r[0:1, 0:4], sched_f[0:1, 0:4], 1.0 / 128.0, None, ALU.mult),
               reads=["sched"], writes=["q128r"])
        em.add("pe", lambda e: e.matmul(ps_c3[:, 0:4], ones_row_f[:, :], sched_f[0:1, 0:4], start=True, stop=True),
               reads=["ones_row_f", "sched"], writes=["c3"])
        em.add("dve", lambda e: e.tensor_copy(q0b[:, 0:4], ps_c3[:, 0:4]), reads=["c3"], writes=["q0b"])
        em.add("pe", lambda e: e.matmul(ps_c3[:, 4:8], ones_row_f[:, :], q128r[0:1, 0:4], start=True, stop=True),
               reads=["ones_row_f", "q128r"], writes=["c3"])
        em.add("dve", lambda e: e.tensor_copy(q0tb[:, 0:4], ps_c3[:, 4:8]), reads=["c3"], writes=["q0tb"])
        for g in range(G):
            em.add("dve", (lambda g: lambda e: e.tensor_scalar(
                gb[:, g:g + 1], io65[:, :], float(g * 128 * 65), None, ALU.add))(g),
                reads=["io65"], writes=["gb"])
        for v in range(NV):
            for g in range(G):
                for p in range(4):
                    col = (v * 4 + g) * 4 + p
                    em.add("dve", (lambda col, g, v, p: lambda e: e.tensor_scalar(
                        kdf[:, col:col + 1], gb[:, g:g + 1], q0tb[:, v:v + 1], float(p), ALU.add, ALU.add))(col, g, v, p),
                        reads=["gb", "q0tb"], writes=["kdf"])
        em.add("dve", lambda e: e.tensor_copy(kdix[:, :], kdf[:, :]), reads=["kdf"], writes=["kdix"])
        for v in range(NV):
            for g in range(G):
                for p in range(4):
                    col = (v * 4 + g) * 4 + p
                    em.add("dve", (lambda col, v, p: lambda e: e.tensor_scalar(
                        vdf[:, col:col + 1], iop[:, :], q0b[:, v:v + 1], float(128 * p), ALU.add, ALU.add))(col, v, p),
                        reads=["iop", "q0b"], writes=["vdf"])
                    em.add("dve", (lambda col: lambda e: e.tensor_scalar(
                        vdf[:, col:col + 1], vdf[:, col:col + 1], 4.0, None, ALU.mult))(col),
                        reads=["vdf"], writes=["vdf"])
                    em.add("dve", (lambda col, g: lambda e: e.tensor_scalar(
                        vdf[:, col:col + 1], vdf[:, col:col + 1], float(g), None, ALU.add))(col, g),
                        reads=["vdf"], writes=["vdf"])
        em.add("dve", lambda e: e.tensor_copy(vdix[:, :], vdf[:, :]), reads=["vdf"], writes=["vdix"])
        for v in range(NV):
            n = NPAIR[v]
            em.add("dve", (lambda v, n: lambda e: e.tensor_scalar(
                m1f[0:1, 0:12], itf[0:1, 0:12], float(n - 16), None, ALU.add))(v, n),
                reads=["i12"], writes=["m1f"])
            em.add("dve", (lambda v: lambda e: e.tensor_scalar(
                m1f[0:1, 0:12], m1f[0:1, 0:12], q128r[0:1, v:v + 1], None, ALU.is_lt))(v),
                reads=["m1f", "q128r"], writes=["m1f"])
            em.add("dve", lambda e: e.tensor_copy(m2f[0:1, 0:12], m1f[0:1, 0:12]),
                   reads=["m1f"], writes=["m2f"])
            em.add("pe", (lambda v: lambda e: e.matmul(
                ps_c3[:, 0:12], ones_row_f[:, :], m2f[0:1, 0:12], start=True, stop=True))(v),
                reads=["ones_row_f", "m2f"], writes=["c3"])
            em.add("dve", (lambda v: lambda e: e.tensor_copy(mcol[:, v * 12:(v + 1) * 12], ps_c3[:, 0:12]))(v),
                   reads=["c3"], writes=["mcol"])

        if DEBUG:
            em.dma("pool", "dbg0", lambda e: e.dma_start(d_vdix.ap(), vdix[:, :]), reads=["vdix"], name="dbgvd")
            em.dma("pool", "dbg0", lambda e: e.dma_start(d_kdix.ap(), kdix[:, :]), reads=["kdix"], name="dbgkd")
        em.drain({"ld": "pool", "dbg0": "pool"})
        with nc.Block() as blk:
            em.emit(blk)

    if PHASES >= 2:
        with ExitStack() as ph:
            Ep = ph.enter_context
            wk_sb = Ep(nc.sbuf_tensor("wk_sb", [128, NCH * KVD], BF16))
            wv_sb = Ep(nc.sbuf_tensor("wv_sb", [128, NCH * KVD], BF16))
            x_tm = [Ep(nc.sbuf_tensor(f"x_tm{i}", [128, 8 * D], BF16)) for i in range(2)]
            xfm = [Ep(nc.sbuf_tensor(f"xfm{i}", [128, NCH * 512], BF16)) for i in range(2)]
            kst = [Ep(nc.sbuf_tensor(f"kst{i}", [128, 512], BF16)) for i in range(2)]
            vst = [Ep(nc.sbuf_tensor(f"vst{i}", [128, 512], BF16)) for i in range(2)]
            em = Em(nc, top, "a1")

            em.dma("pool", "wkv", lambda e: e.dma_start(
                wk_sb[:, :].rearrange("p (c n) -> p c n", c=NCH),
                dwk.ap().rearrange("(c p) n -> p c n", p=128)), writes=["wk"])
            em.dma("pool", "wkv", lambda e: e.dma_start(
                wv_sb[:, :].rearrange("p (c n) -> p c n", c=NCH),
                dwv.ap().rearrange("(c p) n -> p c n", p=128)), writes=["wv"])

            em.add("pool", lambda e: e.reg_load(r_g1, sched_i[0:1, 1:2]), reads=["sched"], name="ldxb")

            for st0 in range(16):
                pb = st0 % 2
                if st0 % 2 == 0:
                    ld = st0 // 2
                    lb = ld % 2
                    def xload(e, ld=ld, lb=lb):
                        e.reg_mov(r_g2, ld * 1024 * D)
                        e.reg_add(r_g2, r_g2, r_g1)
                        return e.dma_start(x_tm[lb][:, :].rearrange("p (t c) -> p t c", t=8),
                                           bass.AP(dx, r_g2, [[D, 128], [128 * D, 8], [1, D]]))
                    em.dma("pool", "xin", xload, reads=[], writes=[f"x8_{lb}"], name=f"xl{ld}")
                lb = (st0 // 2) % 2
                for ti in range(4):
                    toff = (st0 % 2) * 4 + ti
                    for half in range(2):
                        for cc in range(8):
                            c = half * 8 + cc
                            em.add("pe", (lambda lb, c, cc, toff: lambda e: e.transpose(
                                ps_tb[:, cc * 128:(cc + 1) * 128],
                                x_tm[lb][:, toff * D + c * 128: toff * D + (c + 1) * 128], id_bf[:, :]))(lb, c, cc, toff),
                                reads=[f"x8_{lb}", "id_bf"], writes=["tb"], name=f"xt{st0}_{ti}")
                        em.add("dve", (lambda pb, ti, half: lambda e: e.tensor_copy(
                            xfm[pb][:, :].rearrange("p (c t) -> p c t", c=NCH)[:, half * 8:(half + 1) * 8, ti * 128:(ti + 1) * 128],
                            ps_tb[:, :].rearrange("p (c t) -> p c t", c=8)))(pb, ti, half),
                            reads=["tb"], writes=[f"xfm{pb}:{ti}"], name=f"xc{st0}_{ti}")
                for g in range(G):
                    kb = g % 2
                    for c in range(NCH):
                        em.add("pe", (lambda pb, g, c, kb: lambda e: e.matmul(
                            (ps_c1 if kb == 0 else ps_c2)[:, :],
                            wk_sb[:, c * KVD + g * 128: c * KVD + (g + 1) * 128],
                            xfm[pb][:, c * 512:(c + 1) * 512],
                            start=(c == 0), stop=(c == NCH - 1)))(pb, g, c, kb),
                            reads=["wk", f"xfm{pb}:0", f"xfm{pb}:1", f"xfm{pb}:2", f"xfm{pb}:3"],
                            writes=["c1" if kb == 0 else "c2"], name=f"kmm{st0}g{g}")
                    em.add("dve", (lambda kb, g: lambda e: e.tensor_copy(kst[kb][:, :], (ps_c1 if kb == 0 else ps_c2)[:, :]))(kb, g),
                           reads=["c1" if kb == 0 else "c2"], writes=[f"kst{kb}"], name=f"kev{st0}g{g}")
                    em.dma("sp", "kout", (lambda kb, g, st0: lambda e: e.dma_start(
                        bass.AP(kscr, g * 128 * KSW + st0 * 512, [[KSW, 128], [1, 512]]), kst[kb][:, :]))(kb, g, st0),
                        reads=[f"kst{kb}"], name=f"kw{st0}g{g}")
                for ti in range(4):
                    vb = ti % 2
                    for c in range(NCH):
                        em.add("pe", (lambda pb, ti, c, vb: lambda e: e.matmul(
                            (ps_s1 if vb == 0 else ps_s2)[:, 0:512],
                            xfm[pb][:, c * 512 + ti * 128: c * 512 + (ti + 1) * 128],
                            wv_sb[:, c * KVD:(c + 1) * KVD],
                            start=(c == 0), stop=(c == NCH - 1)))(pb, ti, c, vb),
                            reads=["wv", f"xfm{pb}:{ti}"], writes=["s1" if vb == 0 else "s2"], name=f"vmm{st0}t{ti}")
                    em.add("dve", (lambda vb: lambda e: e.tensor_copy(vst[vb][:, :], (ps_s1 if vb == 0 else ps_s2)[:, 0:512]))(vb),
                           reads=["s1" if vb == 0 else "s2"], writes=[f"vst{vb}"], name=f"vev{st0}t{ti}")
                    em.dma("act", "vout", (lambda vb, st0, ti: lambda e: e.dma_start(
                        bass.AP(vscr, (st0 * 512 + ti * 128) * KVD, [[KVD, 128], [1, KVD]]), vst[vb][:, :]))(vb, st0, ti),
                        reads=[f"vst{vb}"], name=f"vw{st0}t{ti}")

            em.drain({"wkv": "pool", "xin": "pool", "kout": "sp", "vout": "act"})
            with nc.Block() as blk:
                em.emit(blk)


    if PHASES >= 3:
        with ExitStack() as ph:
            Ep = ph.enter_context
            x_tm2 = Ep(nc.sbuf_tensor("x_tm2", [128, 4 * D], BF16))
            xfq = Ep(nc.sbuf_tensor("xfq", [128, NCH * 512], BF16))
            wcol = [Ep(nc.sbuf_tensor(f"wcol{i}", [128, NCH * 128], BF16)) for i in range(2)]
            qh = [Ep(nc.sbuf_tensor(f"qh{i}", [128, 512], BF16)) for i in range(2)]
            ksb = Ep(nc.sbuf_tensor("ksb", [128, 64 * 128], BF16))
            vsb = Ep(nc.sbuf_tensor("vsb", [128, 64 * 128], BF16))
            pbuf = [Ep(nc.sbuf_tensor(f"pbuf{i}", [128, 1024], BF16)) for i in range(2)]
            ctx_sb = Ep(nc.sbuf_tensor("ctx_sb", [128, H * 512], BF16))
            usb = Ep(nc.sbuf_tensor("usb", [128, NCH * 512], BF16))
            tmpf = Ep(nc.sbuf_tensor("tmpf", [128, 512], F32))
            tmpb = Ep(nc.sbuf_tensor("tmpb", [128, 512], BF16))
            lrow = Ep(nc.sbuf_tensor("lrow", [1, 512], F32))
            lrow_bf = Ep(nc.sbuf_tensor("lrow_bf", [1, 512], BF16))
            mrow = Ep(nc.sbuf_tensor("mrow", [1, 512], F32))
            mrow_bf = Ep(nc.sbuf_tensor("mrow_bf", [1, 512], BF16))
            rrow = Ep(nc.sbuf_tensor("rrow", [1, 512], F32))
            rrow_bf = Ep(nc.sbuf_tensor("rrow_bf", [1, 512], BF16))
            vrow = Ep(nc.sbuf_tensor("vrow", [1, 512], F32))
            em = Em(nc, top, "a2")

            for v in range(NV):
                n = NPAIR[v]
                em.add("pool", (lambda v: lambda e: e.reg_load(r_g1, sched_i[0:1, 2 + v:3 + v]))(v),
                       reads=["sched"], name=f"ldxmy{v}")
                def xload2(e):
                    return e.dma_start(x_tm2[:, :].rearrange("p (t c) -> p t c", t=4),
                                       bass.AP(dx, r_g1, [[D, 128], [128 * D, 4], [1, D]]))
                em.dma("pool", "xq", xload2, writes=["x2"], name=f"xq{v}")
                for i in range(4):
                    for half in range(2):
                        for cc in range(8):
                            c = half * 8 + cc
                            em.add("pe", (lambda c, cc, i: lambda e: e.transpose(
                                ps_tb[:, cc * 128:(cc + 1) * 128],
                                x_tm2[:, i * D + c * 128: i * D + (c + 1) * 128], id_bf[:, :]))(c, cc, i),
                                reads=["x2", "id_bf"], writes=["tb"], name=f"qt{v}_{i}")
                        em.add("dve", (lambda i, half: lambda e: e.tensor_copy(
                            xfq[:, :].rearrange("p (c t) -> p c t", c=NCH)[:, half * 8:(half + 1) * 8, i * 128:(i + 1) * 128],
                            ps_tb[:, :].rearrange("p (c t) -> p c t", c=8)))(i, half),
                            reads=["tb"], writes=["xfq"], name=f"qc{v}_{i}")

                ctx_regions = []
                for g in range(G):
                    for p in range(4):
                        kcol = (v * 4 + g) * 4 + p
                        vcol = (v * 4 + g) * 4 + p
                        em.dma("pool", "kvd", (lambda p, kcol: lambda e: e.indirect_dma_start(
                            ksb[:, p * 128:(p + 1) * 128], None,
                            bass.AP(kscr, 0, [[128, KVD * (KSW // 128)], [1, 128]]),
                            IndirectOffsetOnAxis(kdix[:, kcol:kcol + 1], 0)))(p, kcol),
                            reads=["kdix"], writes=[f"ks:{p}"], name=f"kd{v}g{g}p{p}")
                        em.dma("pool", "kvd", (lambda p, vcol: lambda e: e.indirect_dma_start(
                            vsb[:, p * 128:(p + 1) * 128], None,
                            bass.AP(vscr, 0, [[128, KSW * 4], [1, 128]]),
                            IndirectOffsetOnAxis(vdix[:, vcol:vcol + 1], 0)))(p, vcol),
                            reads=["vdix"], writes=[f"vs:{p}"], name=f"vd{v}g{g}p{p}")
                    if DEBUG and v == 0 and g == 0:
                        em.dma("pool", "dbg", lambda e: e.dma_start(d_ksb0.ap(), ksb[:, 0:16 * 128]),
                               reads=[f"ks:{sl}" for sl in range(16)], name="dbgk")
                        em.dma("pool", "dbg", lambda e: e.dma_start(d_vsb0.ap(), vsb[:, 0:16 * 128]),
                               reads=[f"vs:{sl}" for sl in range(16)], name="dbgv")
                    nq4 = (n - 4) // 4
                    for qq in range(4):
                        s0 = 4 + qq * nq4
                        em.dma("sp", "kv", (lambda g, qq, s0, nq4: lambda e: e.dma_start(
                            ksb[:, s0 * 128:(s0 + nq4) * 128],
                            bass.AP(kscr, g * 128 * KSW + qq * nq4 * 128, [[KSW, 128], [1, nq4 * 128]])))(g, qq, s0, nq4),
                            writes=[f"ks:{sl}" for sl in range(s0, s0 + nq4)], name=f"kb{v}g{g}q{qq}")
                        em.dma("act", "kv2", (lambda g, qq, s0, nq4: lambda e: e.dma_start(
                            vsb[:, s0 * 128:(s0 + nq4) * 128].rearrange("p (s c) -> p s c", s=nq4),
                            bass.AP(vscr, (qq * nq4 * 128) * KVD + g * 128, [[KVD, 128], [128 * KVD, nq4], [1, 128]])))(g, qq, s0, nq4),
                            writes=[f"vs:{sl}" for sl in range(s0, s0 + nq4)], name=f"vb{v}g{g}q{qq}")

                    for hh in range(4):
                        h = g * 4 + hh
                        qb_ = h % 2
                        em.dma("pool", "wq", (lambda h, qb_: lambda e: e.dma_start(
                            wcol[qb_][:, :].rearrange("p (c m) -> p c m", c=NCH),
                            bass.AP(dwq, h * 128, [[D, 128], [128 * D, NCH], [1, 128]])))(h, qb_),
                            writes=[f"wcol{qb_}"], name=f"wq{h}")
                        for c in range(NCH):
                            em.add("pe", (lambda c, qb_: lambda e: e.matmul(
                                ps_c3[:, :], wcol[qb_][:, c * 128:(c + 1) * 128], xfq[:, c * 512:(c + 1) * 512],
                                start=(c == 0), stop=(c == NCH - 1)))(c, qb_),
                                reads=[f"wcol{qb_}", "xfq"], writes=["c3"], name=f"qmm{h}")
                        em.add("dve", (lambda qb_: lambda e: e.tensor_copy(qh[qb_][:, :], ps_c3[:, :]))(qb_),
                               reads=["c3"], writes=[f"qh{qb_}"], name=f"qev{h}")

                        M = n // 2
                        for m in range(M):
                            sb_ps = ps_s1 if (m % 2 == 0) else ps_s2
                            sreg = "s1" if (m % 2 == 0) else "s2"
                            pb_ = m % 2
                            for u in range(2):
                                p = 2 * m + u
                                em.add("pe", (lambda p, u, sb_ps, qb_: lambda e: e.matmul(
                                    sb_ps[:, u * 512:(u + 1) * 512],
                                    ksb[:, p * 128:(p + 1) * 128], qh[qb_][:, :],
                                    start=True, stop=True))(p, u, sb_ps, qb_),
                                    reads=[f"ks:{p}", f"qh{qb_}"], writes=[sreg], name=f"smm{h}p{p}")
                            em.add("act", (lambda sb_ps, pb_: lambda e: e.activation(
                                pbuf[pb_][:, :], sb_ps[:, :], AF.Exp, bias=c_shift[:, :], scale=c_scale[:, :]))(sb_ps, pb_),
                                reads=[sreg, "c_shift", "c_scale"], writes=[f"pb{pb_}"], name=f"exp{h}m{m}")
                            for u in range(2):
                                p = 2 * m + u
                                if p < 4:
                                    em.add("pool", (lambda p, u, pb_: lambda e: e.affine_select(
                                        out=pbuf[pb_][:, u * 512:(u + 1) * 512], in_=pbuf[pb_][:, u * 512:(u + 1) * 512],
                                        compare_op=ALU.is_ge, fill=0.0, base=-128 * p,
                                        pattern=[[1, 512]], channel_multiplier=-1))(p, u, pb_),
                                        reads=[f"pb{pb_}"], writes=[f"pb{pb_}"], name=f"diag{h}p{p}")
                                if p >= n - 12:
                                    mc = v * 12 + (p - (n - 12))
                                    em.add("dve", (lambda p, u, pb_, mc: lambda e: e.tensor_scalar(
                                        pbuf[pb_][:, u * 512:(u + 1) * 512], pbuf[pb_][:, u * 512:(u + 1) * 512],
                                        mcol[:, mc:mc + 1], None, ALU.mult))(p, u, pb_, mc),
                                        reads=[f"pb{pb_}", "mcol"], writes=[f"pb{pb_}"], name=f"gmask{h}p{p}")
                            for u in range(2):
                                p = 2 * m + u
                                em.add("pe", (lambda p, u, pb_: lambda e: e.matmul(
                                    ps_c1[:, :], vsb[:, p * 128:(p + 1) * 128], pbuf[pb_][:, u * 512:(u + 1) * 512],
                                    start=(p == 0), stop=(p == n - 1)))(p, u, pb_),
                                    reads=[f"vs:{p}", f"pb{pb_}"], writes=["c1"], name=f"pv{h}p{p}")
                                em.add("pe", (lambda p, u, pb_: lambda e: e.matmul(
                                    ps_c2[0:1, :], ones_col_bf[:, :], pbuf[pb_][:, u * 512:(u + 1) * 512],
                                    start=(p == 0), stop=(p == n - 1)))(p, u, pb_),
                                    reads=["ones_col", f"pb{pb_}"], writes=["c2"], name=f"lmm{h}p{p}")

                        if DEBUG and v == 0 and h == 0:
                            em.dma("pool", "dbg", (lambda qb_: lambda e: e.dma_start(d_q0.ap(), qh[qb_][:, :]))(qb_),
                                   reads=[f"qh{qb_}"], name="dbgq")
                            em.dma("pool", "dbg", lambda e: e.dma_start(d_p0.ap(), pbuf[0][:, :]),
                                   reads=["pb0"], name="dbgp")
                        em.add("dve", lambda e: e.reciprocal(lrow[0:1, :], ps_c2[0:1, :]),
                               reads=["c2"], writes=["lrow"], name=f"li{h}")
                        em.add("dve", lambda e: e.tensor_copy(lrow_bf[0:1, :], lrow[0:1, :]),
                               reads=["lrow"], writes=["lrow_bf"], name=f"lb{h}")
                        em.add("pe", lambda e: e.matmul(ps_c3[:, :], ones_row_bf[:, :], lrow_bf[0:1, :], start=True, stop=True),
                               reads=["ones_row", "lrow_bf"], writes=["c3"], name=f"lbc{h}")
                        em.add("dve", lambda e: e.tensor_copy(tmpf[:, :], ps_c1[:, :]),
                               reads=["c1"], writes=["tmpf"], name=f"cev{h}")
                        em.add("dve", (lambda h: lambda e: e.tensor_tensor(
                            ctx_sb[:, h * 512:(h + 1) * 512], tmpf[:, :], ps_c3[:, :], ALU.mult))(h),
                            reads=["tmpf", "c3"], writes=["ctx"], name=f"cscl{h}")

                if DEBUG and v == 0:
                    em.dma("pool", "dbg", lambda e: e.dma_start(d_l0.ap(), lrow[0:1, :]),
                           reads=["lrow"], name="dbgl")
                    em.dma("pool", "dbg", lambda e: e.dma_start(d_ctx0.ap(), ctx_sb[:, :]),
                           reads=["ctx"], name="dbgctx")
                for dch in range(NCH):
                    wb = dch % 2
                    em.dma("pool", "wo", (lambda dch, wb: lambda e: e.dma_start(
                        wcol[wb][:, :].rearrange("p (c m) -> p c m", c=NCH),
                        bass.AP(dwo, dch * 128, [[D, 128], [128 * D, NCH], [1, 128]])))(dch, wb),
                        writes=[f"wcol{wb}"], name=f"wo{v}_{dch}")
                    for hc in range(NCH):
                        em.add("pe", (lambda dch, hc, wb: lambda e: e.matmul(
                            ps_s1[:, 0:512], wcol[wb][:, hc * 128:(hc + 1) * 128], ctx_sb[:, hc * 512:(hc + 1) * 512],
                            start=(hc == 0), stop=(hc == NCH - 1)))(dch, hc, wb),
                            reads=[f"wcol{wb}", "ctx"], writes=["s1"], name=f"womm{v}_{dch}")
                    em.add("dve", (lambda dch: lambda e: e.scalar_tensor_tensor(
                        usb[:, dch * 512:(dch + 1) * 512], ps_s1[:, 0:512], bo_sb[:, dch:dch + 1],
                        xfq[:, dch * 512:(dch + 1) * 512], ALU.add, ALU.add))(dch),
                        reads=["s1", "bo_sb", "xfq"], writes=["usb"], name=f"ures{v}_{dch}")

                for c in range(NCH):
                    em.add("pe", (lambda c: lambda e: e.matmul(
                        ps_s1[0:1, 0:512], ones_col_bf[:, :], usb[:, c * 512:(c + 1) * 512],
                        start=(c == 0), stop=(c == NCH - 1)))(c),
                        reads=["ones_col", "usb"], writes=["s1"], name=f"lns{v}")
                for c in range(NCH):
                    em.add("dve", (lambda c: lambda e: e.tensor_mul(
                        tmpb[:, :], usb[:, c * 512:(c + 1) * 512], usb[:, c * 512:(c + 1) * 512]))(c),
                        reads=["usb"], writes=["tmpb"], name=f"sq{v}_{c}")
                    em.add("pe", (lambda c: lambda e: e.matmul(
                        ps_s2[0:1, 0:512], ones_col_bf[:, :], tmpb[:, :],
                        start=(c == 0), stop=(c == NCH - 1)))(c),
                        reads=["ones_col", "tmpb"], writes=["s2"], name=f"sqs{v}")
                em.add("dve", lambda e: e.tensor_scalar(mrow[0:1, :], ps_s1[0:1, 0:512], 1.0 / D, None, ALU.mult),
                       reads=["s1"], writes=["mrow"], name=f"mean{v}")
                em.add("dve", lambda e: e.tensor_scalar(vrow[0:1, :], ps_s2[0:1, 0:512], 1.0 / D, None, ALU.mult),
                       reads=["s2"], writes=["vrow"], name=f"msq{v}")
                em.add("dve", lambda e: e.tensor_mul(lrow[0:1, :], mrow[0:1, :], mrow[0:1, :]),
                       reads=["mrow"], writes=["lrow"], name=f"m2{v}")
                em.add("dve", lambda e: e.tensor_sub(vrow[0:1, :], vrow[0:1, :], lrow[0:1, :]),
                       reads=["vrow", "lrow"], writes=["vrow"], name=f"var{v}")
                em.add("act", lambda e: e.activation(vrow[0:1, :], vrow[0:1, :], AF.Ln, bias=c_eps[0:1, :]),
                       reads=["vrow", "c_eps"], writes=["vrow"], name=f"lnv{v}")
                em.add("act", lambda e: e.activation(rrow[0:1, :], vrow[0:1, :], AF.Exp, scale=c_nhalf[0:1, :]),
                       reads=["vrow", "c_nhalf"], writes=["rrow"], name=f"rstd{v}")
                em.add("dve", lambda e: e.tensor_copy(mrow_bf[0:1, :], mrow[0:1, :]),
                       reads=["mrow"], writes=["mrow_bf"], name=f"mbf{v}")
                em.add("dve", lambda e: e.tensor_copy(rrow_bf[0:1, :], rrow[0:1, :]),
                       reads=["rrow"], writes=["rrow_bf"], name=f"rbf{v}")
                em.add("pe", lambda e: e.matmul(ps_c3[:, :], ones_row_bf[:, :], mrow_bf[0:1, :], start=True, stop=True),
                       reads=["ones_row", "mrow_bf"], writes=["c3"], name=f"mb{v}")
                em.add("pe", lambda e: e.matmul(ps_c2[:, :], ones_row_bf[:, :], rrow_bf[0:1, :], start=True, stop=True),
                       reads=["ones_row", "rrow_bf"], writes=["c2"], name=f"rb{v}")
                if DEBUG and v == 0:
                    em.dma("pool", "dbg", lambda e: e.dma_start(d_usb0.ap(), usb[:, :]),
                           reads=["usb"], name="dbgu")
                for c in range(NCH):
                    em.add("dve", (lambda c: lambda e: e.tensor_sub(
                        tmpb[:, :], usb[:, c * 512:(c + 1) * 512], ps_c3[:, :]))(c),
                        reads=["usb", "c3"], writes=["tmpb"], name=f"ap1{v}_{c}")
                    em.add("dve", (lambda c, v: lambda e: e.tensor_tensor(
                        X1[:, c * TOKC + v * 512: c * TOKC + (v + 1) * 512], tmpb[:, :], ps_c2[:, :], ALU.mult))(c, v),
                        reads=["tmpb", "c2"], writes=[f"X1:{v}"], name=f"ap2{v}_{c}")

            if DEBUG:
                em.dma("pool", "dbg", lambda e: e.dma_start(d_x1.ap(), X1[:, :]),
                       reads=["X1:0", "X1:1", "X1:2", "X1:3"], name="dbgx1")
            em.drain({"xq": "pool", "wq": "pool", "wo": "pool", "kv": "sp", "kv2": "act", "kvd": "pool", "dbg": "pool"})
            with nc.Block() as blk:
                em.emit(blk)


    if PHASES >= 4:
        with ExitStack() as ph:
            Ep = ph.enter_context
            Y = Ep(nc.sbuf_tensor("Y", [128, NCH * TAU], F32))
            Hb = Ep(nc.sbuf_tensor("Hb", [128, ICH * TAU], BF16))
            w1c = [Ep(nc.sbuf_tensor(f"w1c{i}", [128, NCH * 128], BF16)) for i in range(2)]
            w3c = [Ep(nc.sbuf_tensor(f"w3c{i}", [128, NCH * 128], BF16)) for i in range(2)]
            w2c = [Ep(nc.sbuf_tensor(f"w2c{i}", [128, ICH * 128], BF16)) for i in range(2)]
            gsc = Ep(nc.sbuf_tensor("gsc", [128, NE], F32))
            gbs = Ep(nc.sbuf_tensor("gbs", [128, NE], F32))
            grp = Ep(nc.sbuf_tensor("grp", [128, 4], F32))
            grp2 = Ep(nc.sbuf_tensor("grp2", [128, 4], F32))
            msc = Ep(nc.sbuf_tensor("msc", [128, NE], F32))
            cur = Ep(nc.sbuf_tensor("cur", [128, NE], F32))
            indt = Ep(nc.sbuf_tensor("indt", [128, NE], F32))
            sc1 = Ep(nc.sbuf_tensor("sc1", [128, 1], F32))
            sc2 = Ep(nc.sbuf_tensor("sc2", [128, 1], F32))
            cw = Ep(nc.sbuf_tensor("cw", [128, 8 * NE], F32))
            cwb16 = Ep(nc.sbuf_tensor("cwb16", [128, NE], BF16))
            cwT = Ep(nc.sbuf_tensor("cwT", [16, 1024], BF16))
            cwB = [Ep(nc.sbuf_tensor(f"cwB{i}", [128, 512], F32)) for i in range(2)]
            silu_t = Ep(nc.sbuf_tensor("silu_t", [128, 512], F32))
            btmp = Ep(nc.sbuf_tensor("btmp", [128, 512], F32))
            outm = Ep(nc.sbuf_tensor("outm", [128, D], F32))
            sqm = Ep(nc.sbuf_tensor("sqm", [128, D], F32))
            srow = Ep(nc.sbuf_tensor("srow", [128, 1], F32))
            qrow = Ep(nc.sbuf_tensor("qrow", [128, 1], F32))
            vr2 = Ep(nc.sbuf_tensor("vr2", [128, 1], F32))
            em = Em(nc, top, "bb")

            for tau in range(2):
                t0 = tau * TAU
                for sub in range(8):
                    tok0 = t0 + sub * 128
                    for c in range(NCH):
                        em.add("pe", (lambda c, tok0: lambda e: e.matmul(
                            ps_c2[:, 0:NE], X1[:, c * TOKC + tok0: c * TOKC + tok0 + 128],
                            gw_fm[:, c * NE:(c + 1) * NE],
                            start=(c == 0), stop=(c == NCH - 1)))(c, tok0),
                            reads=[f"X1:{2 * tau}", f"X1:{2 * tau + 1}"], writes=["c2"], name=f"gmm{tau}_{sub}")
                    em.add("act", lambda e: e.activation(gsc[:, :], ps_c2[:, 0:NE], AF.Sigmoid),
                           reads=["c2"], writes=["gsc"], name=f"sig{tau}_{sub}")
                    em.add("dve", lambda e: e.tensor_add(gbs[:, :], gsc[:, :], ebias_b[:, :]),
                           reads=["gsc", "ebias_b"], writes=["gbs"], name=f"gb{tau}_{sub}")
                    for gg in range(4):
                        em.add("dve", (lambda gg: lambda e: e.reduce_max(
                            grp[:, gg:gg + 1], gbs[:, 4 * gg:4 * (gg + 1)], axis=AX.X))(gg),
                            reads=["gbs"], writes=["grp"], name=f"gmax{tau}_{sub}_{gg}")
                    em.add("dve", lambda e: e.reduce_max(sc1[:, :], grp[:, :], axis=AX.X),
                           reads=["grp"], writes=["sc1"], name=f"m1_{tau}_{sub}")
                    em.add("dve", lambda e: e.tensor_scalar(grp2[:, :], grp[:, :], sc1[:, 0:1], None, ALU.is_equal),
                           reads=["grp", "sc1"], writes=["grp2"], name=f"e1_{tau}_{sub}")
                    em.add("dve", lambda e: e.scalar_tensor_tensor(grp2[:, :], grp2[:, :], -BIG, grp[:, :], ALU.mult, ALU.add),
                           reads=["grp2", "grp"], writes=["grp2"], name=f"g2_{tau}_{sub}")
                    em.add("dve", lambda e: e.reduce_max(sc2[:, :], grp2[:, :], axis=AX.X),
                           reads=["grp2"], writes=["sc2"], name=f"m2_{tau}_{sub}")
                    for gg in range(4):
                        em.add("dve", (lambda gg: lambda e: e.tensor_scalar(
                            sc1[:, :], grp[:, gg:gg + 1], sc2[:, 0:1], None, ALU.is_ge))(gg),
                            reads=["grp", "sc2"], writes=["sc1"], name=f"gi{tau}_{sub}_{gg}")
                        em.add("dve", (lambda gg: lambda e: e.tensor_scalar(
                            sc1[:, :], sc1[:, :], BIG, None, ALU.mult))(gg),
                            reads=["sc1"], writes=["sc1"], name=f"gp{tau}_{sub}_{gg}")
                        em.add("dve", (lambda gg: lambda e: e.tensor_scalar(
                            sc1[:, :], sc1[:, :], -BIG, None, ALU.add))(gg),
                            reads=["sc1"], writes=["sc1"], name=f"gp2{tau}_{sub}_{gg}")
                        em.add("dve", (lambda gg: lambda e: e.tensor_scalar(
                            msc[:, 4 * gg:4 * (gg + 1)], gbs[:, 4 * gg:4 * (gg + 1)], sc1[:, 0:1], None, ALU.add))(gg),
                            reads=["gbs", "sc1"], writes=["msc"], name=f"gm{tau}_{sub}_{gg}")
                    em.add("dve", lambda e: e.tensor_copy(cur[:, :], msc[:, :]),
                           reads=["msc"], writes=["cur"], name=f"cp{tau}_{sub}")
                    for it in range(3):
                        em.add("dve", lambda e: e.reduce_max(sc1[:, :], cur[:, :], axis=AX.X),
                               reads=["cur"], writes=["sc1"], name=f"tm{tau}_{sub}_{it}")
                        em.add("dve", lambda e: e.tensor_scalar(indt[:, :], cur[:, :], sc1[:, 0:1], None, ALU.is_ge),
                               reads=["cur", "sc1"], writes=["indt"], name=f"ti{tau}_{sub}_{it}")
                        em.add("dve", lambda e: e.scalar_tensor_tensor(cur[:, :], indt[:, :], -BIG, cur[:, :], ALU.mult, ALU.add),
                               reads=["indt", "cur"], writes=["cur"], name=f"tc{tau}_{sub}_{it}")
                    em.add("dve", lambda e: e.reduce_max(sc1[:, :], cur[:, :], axis=AX.X),
                           reads=["cur"], writes=["sc1"], name=f"m4_{tau}_{sub}")
                    em.add("dve", lambda e: e.tensor_scalar(indt[:, :], msc[:, :], sc1[:, 0:1], None, ALU.is_ge),
                           reads=["msc", "sc1"], writes=["indt"], name=f"ind{tau}_{sub}")
                    em.add("dve", lambda e: e.tensor_mul(indt[:, :], indt[:, :], gsc[:, :]),
                           reads=["indt", "gsc"], writes=["indt"], name=f"w_{tau}_{sub}")
                    em.add("dve", lambda e: e.reduce_sum(sc1[:, :], indt[:, :], axis=AX.X),
                           reads=["indt"], writes=["sc1"], name=f"ws{tau}_{sub}")
                    em.add("dve", lambda e: e.reciprocal(sc1[:, :], sc1[:, :]),
                           reads=["sc1"], writes=["sc1"], name=f"wr{tau}_{sub}")
                    em.add("dve", (lambda sub: lambda e: e.tensor_scalar(
                        cw[:, sub * NE:(sub + 1) * NE], indt[:, :], sc1[:, 0:1], None, ALU.mult))(sub),
                        reads=["indt", "sc1"], writes=["cw"], name=f"cw{tau}_{sub}")
                    em.add("dve", (lambda sub: lambda e: e.tensor_copy(cwb16[:, :], cw[:, sub * NE:(sub + 1) * NE]))(sub),
                           reads=["cw"], writes=["cwb16"], name=f"cwb{tau}_{sub}")
                    em.add("pe", (lambda sub: lambda e: e.transpose(
                        ps_tb[0:16, sub * 128:(sub + 1) * 128], cwb16[:, 0:NE], id_bf[:, :]))(sub),
                        reads=["cwb16", "id_bf"], writes=["tb"], name=f"cwt{tau}_{sub}")
                em.add("dve", lambda e: e.tensor_copy(cwT[0:16, :], ps_tb[0:16, :]),
                       reads=["tb"], writes=["cwT"], name=f"cwc{tau}")

                for ex in range(NE):
                    for s in range(2):
                        em.add("pe", (lambda ex, s: lambda e: e.matmul(
                            ps_c3[:, :], oneh[:, ex * 128:(ex + 1) * 128], cwT[0:16, s * 512:(s + 1) * 512],
                            start=True, stop=True))(ex, s),
                            reads=["oneh", "cwT"], writes=["c3"], name=f"cb{tau}e{ex}s{s}")
                        em.add("dve", (lambda s: lambda e: e.tensor_copy(cwB[s][:, :], ps_c3[:, :]))(s),
                               reads=["c3"], writes=[f"cwB{s}"], name=f"cbe{tau}e{ex}s{s}")
                    for mch in range(ICH):
                        wb = mch % 2
                        em.dma("pool", "wex", (lambda ex, mch, wb: lambda e: e.dma_start(
                            w1c[wb][:, :].rearrange("p (c m) -> p c m", c=NCH),
                            bass.AP(dw1, ex * D * INTER + mch * 128, [[INTER, 128], [128 * INTER, NCH], [1, 128]])))(ex, mch, wb),
                            writes=[f"w1c{wb}"], name=f"w1{tau}e{ex}m{mch}")
                        em.dma("pool", "wex", (lambda ex, mch, wb: lambda e: e.dma_start(
                            w3c[wb][:, :].rearrange("p (c m) -> p c m", c=NCH),
                            bass.AP(dw3, ex * D * INTER + mch * 128, [[INTER, 128], [128 * INTER, NCH], [1, 128]])))(ex, mch, wb),
                            writes=[f"w3c{wb}"], name=f"w3{tau}e{ex}m{mch}")
                        for s in range(2):
                            for c in range(NCH):
                                em.add("pe", (lambda c, s, wb, t0: lambda e: e.matmul(
                                    ps_s1[:, s * 512:(s + 1) * 512], w1c[wb][:, c * 128:(c + 1) * 128],
                                    X1[:, c * TOKC + t0 + s * 512: c * TOKC + t0 + (s + 1) * 512],
                                    start=(c == 0), stop=(c == NCH - 1)))(c, s, wb, t0),
                                    reads=[f"w1c{wb}", f"X1:{2 * tau}", f"X1:{2 * tau + 1}"], writes=["s1"], name=f"a{tau}e{ex}m{mch}s{s}")
                                em.add("pe", (lambda c, s, wb, t0: lambda e: e.matmul(
                                    ps_s2[:, s * 512:(s + 1) * 512], w3c[wb][:, c * 128:(c + 1) * 128],
                                    X1[:, c * TOKC + t0 + s * 512: c * TOKC + t0 + (s + 1) * 512],
                                    start=(c == 0), stop=(c == NCH - 1)))(c, s, wb, t0),
                                    reads=[f"w3c{wb}", f"X1:{2 * tau}", f"X1:{2 * tau + 1}"], writes=["s2"], name=f"b{tau}e{ex}m{mch}s{s}")
                            em.add("act", (lambda s: lambda e: e.activation(
                                silu_t[:, :], ps_s1[:, s * 512:(s + 1) * 512], AF.Silu))(s),
                                reads=["s1"], writes=["silu_t"], name=f"si{tau}e{ex}m{mch}s{s}")
                            em.add("dve", (lambda s: lambda e: e.tensor_tensor(
                                btmp[:, :], cwB[s][:, :], ps_s2[:, s * 512:(s + 1) * 512], ALU.mult))(s),
                                reads=[f"cwB{s}", "s2"], writes=["btmp"], name=f"bt{tau}e{ex}m{mch}s{s}")
                            em.add("dve", (lambda s, mch: lambda e: e.tensor_tensor(
                                Hb[:, mch * TAU + s * 512: mch * TAU + (s + 1) * 512], silu_t[:, :], btmp[:, :], ALU.mult))(s, mch),
                                reads=["silu_t", "btmp"], writes=[f"Hb:{mch}"], name=f"h{tau}e{ex}m{mch}s{s}")
                    for dch in range(NCH):
                        wb = dch % 2
                        em.dma("pool", "wex", (lambda ex, dch, wb: lambda e: e.dma_start(
                            w2c[wb][:, :].rearrange("p (c m) -> p c m", c=ICH),
                            bass.AP(dw2, ex * INTER * D + dch * 128, [[D, 128], [128 * D, ICH], [1, 128]])))(ex, dch, wb),
                            writes=[f"w2c{wb}"], name=f"w2{tau}e{ex}d{dch}")
                        for s in range(2):
                            for mch in range(ICH):
                                em.add("pe", (lambda mch, s, wb: lambda e: e.matmul(
                                    ps_c1[:, :], w2c[wb][:, mch * 128:(mch + 1) * 128],
                                    Hb[:, mch * TAU + s * 512: mch * TAU + (s + 1) * 512],
                                    start=(mch == 0), stop=(mch == ICH - 1)))(mch, s, wb),
                                    reads=[f"w2c{wb}", f"Hb:{mch}"], writes=["c1"], name=f"y{tau}e{ex}d{dch}s{s}")
                            if ex == 0:
                                em.add("dve", (lambda dch, s: lambda e: e.tensor_copy(
                                    Y[:, dch * TAU + s * 512: dch * TAU + (s + 1) * 512], ps_c1[:, :]))(dch, s),
                                    reads=["c1"], writes=[f"Y:{dch}"], name=f"ye{tau}e{ex}d{dch}s{s}")
                            else:
                                em.add("dve", (lambda dch, s: lambda e: e.tensor_tensor(
                                    Y[:, dch * TAU + s * 512: dch * TAU + (s + 1) * 512],
                                    Y[:, dch * TAU + s * 512: dch * TAU + (s + 1) * 512], ps_c1[:, :], ALU.add))(dch, s),
                                    reads=["c1", f"Y:{dch}"], writes=[f"Y:{dch}"], name=f"ya{tau}e{ex}d{dch}s{s}")

                for half in range(2):
                    mlist = list(range(half * 11, min(SCH, half * 11 + 11)))
                    for mi, mch in enumerate(mlist):
                        wb = mi % 2
                        em.dma("pool", "wex", (lambda mch, wb: lambda e: e.dma_start(
                            w1c[wb][:, :].rearrange("p (c m) -> p c m", c=NCH),
                            bass.AP(dsw1, mch * 128, [[SINTER, 128], [128 * SINTER, NCH], [1, 128]])))(mch, wb),
                            writes=[f"w1c{wb}"], name=f"sw1{tau}m{mch}")
                        em.dma("pool", "wex", (lambda mch, wb: lambda e: e.dma_start(
                            w3c[wb][:, :].rearrange("p (c m) -> p c m", c=NCH),
                            bass.AP(dsw3, mch * 128, [[SINTER, 128], [128 * SINTER, NCH], [1, 128]])))(mch, wb),
                            writes=[f"w3c{wb}"], name=f"sw3{tau}m{mch}")
                        for s in range(2):
                            for c in range(NCH):
                                em.add("pe", (lambda c, s, wb, t0: lambda e: e.matmul(
                                    ps_s1[:, s * 512:(s + 1) * 512], w1c[wb][:, c * 128:(c + 1) * 128],
                                    X1[:, c * TOKC + t0 + s * 512: c * TOKC + t0 + (s + 1) * 512],
                                    start=(c == 0), stop=(c == NCH - 1)))(c, s, wb, t0),
                                    reads=[f"w1c{wb}", f"X1:{2 * tau}", f"X1:{2 * tau + 1}"], writes=["s1"], name=f"sa{tau}m{mch}s{s}")
                                em.add("pe", (lambda c, s, wb, t0: lambda e: e.matmul(
                                    ps_s2[:, s * 512:(s + 1) * 512], w3c[wb][:, c * 128:(c + 1) * 128],
                                    X1[:, c * TOKC + t0 + s * 512: c * TOKC + t0 + (s + 1) * 512],
                                    start=(c == 0), stop=(c == NCH - 1)))(c, s, wb, t0),
                                    reads=[f"w3c{wb}", f"X1:{2 * tau}", f"X1:{2 * tau + 1}"], writes=["s2"], name=f"sb{tau}m{mch}s{s}")
                            em.add("act", (lambda s: lambda e: e.activation(
                                silu_t[:, :], ps_s1[:, s * 512:(s + 1) * 512], AF.Silu))(s),
                                reads=["s1"], writes=["silu_t"], name=f"ssi{tau}m{mch}s{s}")
                            em.add("dve", (lambda s, mi: lambda e: e.tensor_tensor(
                                Hb[:, mi * TAU + s * 512: mi * TAU + (s + 1) * 512], silu_t[:, :],
                                ps_s2[:, s * 512:(s + 1) * 512], ALU.mult))(s, mi),
                                reads=["silu_t", "s2"], writes=[f"Hb:{mi}"], name=f"sh{tau}m{mch}s{s}")
                    for dch in range(NCH):
                        wb = dch % 2
                        em.dma("pool", "wex", (lambda dch, wb, half: lambda e: e.dma_start(
                            w2c[wb][:, 0:len(mlist) * 128].rearrange("p (c m) -> p c m", c=len(mlist)),
                            bass.AP(dsw2, (half * 11 * 128) * D + dch * 128, [[D, 128], [128 * D, len(mlist)], [1, 128]])))(dch, wb, half),
                            writes=[f"w2c{wb}"], name=f"sw2{tau}h{half}d{dch}")
                        for s in range(2):
                            for mi in range(len(mlist)):
                                em.add("pe", (lambda mi, s, wb, nm=len(mlist): lambda e: e.matmul(
                                    ps_c1[:, :], w2c[wb][:, mi * 128:(mi + 1) * 128],
                                    Hb[:, mi * TAU + s * 512: mi * TAU + (s + 1) * 512],
                                    start=(mi == 0), stop=(mi == nm - 1)))(mi, s, wb),
                                    reads=[f"w2c{wb}", f"Hb:{mi}"], writes=["c1"], name=f"sy{tau}h{half}d{dch}s{s}")
                            em.add("dve", (lambda dch, s: lambda e: e.tensor_tensor(
                                Y[:, dch * TAU + s * 512: dch * TAU + (s + 1) * 512],
                                Y[:, dch * TAU + s * 512: dch * TAU + (s + 1) * 512], ps_c1[:, :], ALU.add))(dch, s),
                                reads=["c1", f"Y:{dch}"], writes=[f"Y:{dch}"], name=f"sya{tau}h{half}d{dch}s{s}")

                for c in range(NCH):
                    em.add("dve", (lambda c, t0: lambda e: e.tensor_tensor(
                        Y[:, c * TAU:(c + 1) * TAU], Y[:, c * TAU:(c + 1) * TAU],
                        X1[:, c * TOKC + t0: c * TOKC + t0 + TAU], ALU.add))(c, t0),
                        reads=[f"Y:{c}", f"X1:{2 * tau}", f"X1:{2 * tau + 1}"], writes=[f"Y:{c}"], name=f"yx{tau}_{c}")
                for ti in range(8):
                    for q in range(4):
                        for cc in range(4):
                            c = q * 4 + cc
                            em.add("pe", (lambda c, cc, ti: lambda e: e.transpose(
                                ps_c3[:, cc * 128:(cc + 1) * 128], Y[:, c * TAU + ti * 128: c * TAU + (ti + 1) * 128],
                                id_f[:, :]))(c, cc, ti),
                                reads=[f"Y:{c}", "id_f"], writes=["c3"], name=f"ot{tau}_{ti}_{c}")
                        em.add("dve", (lambda q, ti: lambda e: e.tensor_copy(
                            outm[:, q * 512:(q + 1) * 512], ps_c3[:, :]))(q, ti),
                            reads=["c3"], writes=["outm"], name=f"oc{tau}_{ti}_{q}")
                    em.add("dve", (lambda: lambda e: e.reduce_sum(srow[:, :], outm[:, :], axis=AX.X))(),
                           reads=["outm"], writes=["srow"], name=f"os{tau}_{ti}")
                    em.add("dve", lambda e: e.tensor_mul(sqm[:, :], outm[:, :], outm[:, :]),
                           reads=["outm"], writes=["sqm"], name=f"oq{tau}_{ti}")
                    em.add("dve", lambda e: e.reduce_sum(qrow[:, :], sqm[:, :], axis=AX.X),
                           reads=["sqm"], writes=["qrow"], name=f"oqs{tau}_{ti}")
                    em.add("dve", lambda e: e.tensor_scalar(srow[:, :], srow[:, :], 1.0 / D, None, ALU.mult),
                           reads=["srow"], writes=["srow"], name=f"om{tau}_{ti}")
                    em.add("dve", lambda e: e.tensor_scalar(qrow[:, :], qrow[:, :], 1.0 / D, None, ALU.mult),
                           reads=["qrow"], writes=["qrow"], name=f"oq2{tau}_{ti}")
                    em.add("dve", lambda e: e.tensor_mul(vr2[:, :], srow[:, :], srow[:, :]),
                           reads=["srow"], writes=["vr2"], name=f"ov{tau}_{ti}")
                    em.add("dve", lambda e: e.tensor_sub(vr2[:, :], qrow[:, :], vr2[:, :]),
                           reads=["qrow", "vr2"], writes=["vr2"], name=f"ov2{tau}_{ti}")
                    em.add("act", lambda e: e.activation(vr2[:, :], vr2[:, :], AF.Ln, bias=c_eps[:, :]),
                           reads=["vr2", "c_eps"], writes=["vr2"], name=f"oln{tau}_{ti}")
                    em.add("act", lambda e: e.activation(vr2[:, :], vr2[:, :], AF.Exp, scale=c_nhalf[:, :]),
                           reads=["vr2", "c_nhalf"], writes=["vr2"], name=f"orst{tau}_{ti}")
                    em.add("dve", lambda e: e.tensor_scalar(outm[:, :], outm[:, :], srow[:, 0:1], vr2[:, 0:1], ALU.subtract, ALU.mult),
                           reads=["outm", "srow", "vr2"], writes=["outm"], name=f"oap{tau}_{ti}")
                    em.dma("sp", "out", (lambda tau, ti: lambda e: e.dma_start(
                        bass.AP(dout, (tau * TAU + ti * 128) * D, [[D, 128], [1, D]]), outm[:, :]))(tau, ti),
                        reads=["outm"], name=f"ow{tau}_{ti}")

            em.dma("sp", "out", lambda e: e.dma_start(doid.ap(), sched_i[0:1, 0:1]), reads=["sched"], name="oid")
            em.drain({"wex": "pool", "out": "sp"})
            with nc.Block() as blk:
                em.emit(blk)


    _CACHED["nc"] = nc
    return nc


def kernel(L0_attn_x, L0_attn_Wq, L0_attn_Wk, L0_attn_Wv, L0_attn_Wo, L0_attn_bo,
           L0_ffn_gate_w, L0_ffn_expert_bias, L0_ffn_w1, L0_ffn_w2, L0_ffn_w3,
           L0_ffn_sw1, L0_ffn_sw2, L0_ffn_sw3):
    nc = _build()
    inp = {
        "x": np.ascontiguousarray(L0_attn_x, dtype=np.float32),
        "wq": np.ascontiguousarray(L0_attn_Wq, dtype=np.float32),
        "wk": np.ascontiguousarray(L0_attn_Wk, dtype=np.float32),
        "wv": np.ascontiguousarray(L0_attn_Wv, dtype=np.float32),
        "wo": np.ascontiguousarray(L0_attn_Wo, dtype=np.float32),
        "bo": np.ascontiguousarray(L0_attn_bo, dtype=np.float32),
        "gw": np.ascontiguousarray(L0_ffn_gate_w, dtype=np.float32),
        "eb": np.ascontiguousarray(L0_ffn_expert_bias, dtype=np.float32),
        "w1": np.ascontiguousarray(L0_ffn_w1, dtype=np.float32),
        "w2": np.ascontiguousarray(L0_ffn_w2, dtype=np.float32),
        "w3": np.ascontiguousarray(L0_ffn_w3, dtype=np.float32),
        "sw1": np.ascontiguousarray(L0_ffn_sw1, dtype=np.float32),
        "sw2": np.ascontiguousarray(L0_ffn_sw2, dtype=np.float32),
        "sw3": np.ascontiguousarray(L0_ffn_sw3, dtype=np.float32),
    }
    res = run_bass_kernel_spmd(nc, [dict(inp) for _ in range(8)], core_ids=list(range(8)))
    out = np.empty((B, T, D), dtype=np.float32)
    seen = set()
    for c in range(8):
        r = res.results[c]
        w = int(np.asarray(r["oid"]).reshape(-1)[0])
        assert 0 <= w < 8 and w not in seen, f"bad workload ids: {w}, {seen}"
        seen.add(w)
        j, b = w % 4, w // 4
        om = np.asarray(r["out_my"]).reshape(TOKC, D)
        for v in range(NV):
            q0 = q0tok(w, v)
            out[b, q0:q0 + QB, :] = om[v * QB:(v + 1) * QB, :]
    return out


# version 04
# speedup vs baseline: 1.4190x; 1.4190x over previous
import math
import os
import numpy as np
from contextlib import ExitStack

import concourse.bass as bass
import concourse.mybir as mybir
from concourse.bass_utils import run_bass_kernel_spmd
from concourse.bass import IndirectOffsetOnAxis

F32 = mybir.dt.float32
BF16 = mybir.dt.bfloat16
I32 = mybir.dt.int32
AF = mybir.ActivationFunctionType
ALU = mybir.AluOpType
AX = mybir.AxisListType

B, T, D = 2, 8192, 2048
NCH = 16
H, G, DH = 16, 4, 128
KVD = 512
NE, INTER, ICH = 16, 1408, 11
SINTER, SCH = 2816, 22
QB = 512
NV = 4
TOKC = 2048
TAU = 1024
KSW = T + 128
EXP_SHIFT = -12.0
SM_SCALE = 1.0 / math.sqrt(DH)
EPS = 1e-5
BIG = 1.0e9

PPREF = [2048 * (v + 1) for v in range(NV)]
NPAIR = [p // 128 for p in PPREF]
PAIR_OFF = [0, 16, 48, 96]
NPAIR_TOT = 160

def q0tok(w, v):
    j = w % 4
    return (j + 4 * v) * QB

ENG2BLK = {"pe": "tensor", "dve": "vector", "act": "scalar", "pool": "gpsimd", "sp": "sync"}

class _Op:
    __slots__ = ("eng", "fn", "deps", "inc", "event", "clock", "stream", "waits", "name")
    def __init__(self):
        self.deps = []
        self.inc = False
        self.event = None
        self.clock = None
        self.stream = None
        self.waits = None
        self.name = ""

class Em:
    def __init__(self, nc, es, tag):
        self.nc = nc
        self.es = es
        self.tag = tag
        self.ops = []
        self.lw = {}
        self.rd = {}
        self.sems = {}
        self.scnt = {}

    def _sem(self, key):
        s = self.sems.get(key)
        if s is None:
            s = self.es.enter_context(self.nc.semaphore(f"{self.tag}_{key}".replace(":", "_")))
            self.sems[key] = s
        return s

    def add(self, eng, fn, reads=(), writes=(), name=""):
        op = _Op()
        op.eng = eng
        op.fn = fn
        op.name = name
        deps = {}
        for r in reads:
            w = self.lw.get(r)
            if w is not None:
                deps[id(w)] = w
        for r in writes:
            w = self.lw.get(r)
            if w is not None:
                deps[id(w)] = w
            for rr in self.rd.get(r, ()):
                deps[id(rr)] = rr
        if deps.pop(id(op), None) is not None:
            pass
        op.deps = list(deps.values())
        for r in reads:
            self.rd.setdefault(r, []).append(op)
        for r in writes:
            self.lw[r] = op
            self.rd[r] = [op]
        self.ops.append(op)
        return op

    def dma(self, eng, stream, fn, reads=(), writes=(), name=""):
        op = self.add(eng, fn, reads, writes, name=name)
        op.stream = stream
        k = self.scnt.get(stream, 0) + 1
        self.scnt[stream] = k
        op.event = ("s:" + stream, 16 * k)
        return op

    def stream_count(self, stream):
        return self.scnt.get(stream, 0)

    def emit(self, block):
        for op in self.ops:
            for d in op.deps:
                if d.stream is not None:
                    continue
                if d.eng == "pe" and op.eng == "pe":
                    continue
                d.inc = True
        ctr = {}
        for op in self.ops:
            if op.stream is None and op.inc:
                c = ctr.get(op.eng, 0) + 1
                ctr[op.eng] = c
                op.event = ("e:" + op.eng, c)
        known = {}
        for op in self.ops:
            st = known.setdefault(op.eng, {})
            clk = {}
            need = {}
            for d in op.deps:
                if d.clock:
                    for s2, v2 in d.clock.items():
                        if clk.get(s2, 0) < v2:
                            clk[s2] = v2
                ev = d.event
                if ev is not None:
                    if clk.get(ev[0], 0) < ev[1]:
                        clk[ev[0]] = ev[1]
                    if need.get(ev[0], 0) < ev[1]:
                        need[ev[0]] = ev[1]
            waits = []
            for semk, val in need.items():
                if st.get(semk, 0) < val:
                    waits.append((semk, val))
            if waits:
                for d in op.deps:
                    ev = d.event
                    if ev is not None and st.get(ev[0], 0) < ev[1]:
                        st[ev[0]] = ev[1]
                        if d.clock:
                            for s2, v2 in d.clock.items():
                                if st.get(s2, 0) < v2:
                                    st[s2] = v2
            for s2, v2 in st.items():
                if clk.get(s2, 0) < v2:
                    clk[s2] = v2
            if op.event is not None:
                clk[op.event[0]] = op.event[1]
            op.clock = clk
            op.waits = waits
        per = {}
        for op in self.ops:
            per.setdefault(op.eng, []).append(op)

        def mk_runner(oplist):
            def run(eh):
                for op in oplist:
                    for (semk, val) in op.waits:
                        eh.wait_ge(self._sem(semk), val)
                    try:
                        ins = op.fn(eh)
                    except Exception as ex:
                        raise RuntimeError(f"op failed: {self.tag}/{op.eng}/{op.name} waits={op.waits}") from ex
                    if op.stream is not None:
                        ins.then_inc(self._sem("s:" + op.stream), 16)
                    elif op.inc:
                        ins.then_inc(self._sem("e:" + op.eng), 1)
            return run

        for eng, oplist in per.items():
            getattr(block, ENG2BLK[eng])(mk_runner(oplist))

    def drain(self, eng_for_stream):
        for stream, cnt in list(self.scnt.items()):
            eng = eng_for_stream[stream]
            sem = ("s:" + stream, 16 * cnt)
            op = _Op()
            op.eng = eng
            op.fn = lambda eh: eh.nop()
            op.waits = None
            fake = _Op()
            fake.eng = "?"
            fake.event = sem
            fake.clock = {}
            op.deps = [fake]
            self.ops.append(op)


_CACHED = {}

def _build():
    if "nc" in _CACHED:
        return _CACHED["nc"]
    PHASES = int(os.environ.get("KN_PHASES", "4"))
    nc = bass.Bass("TRN2", target_bir_lowering=False)

    dx = nc.dram_tensor("x", [B, T, D], F32, kind="ExternalInput")
    dwq = nc.dram_tensor("wq", [D, D], F32, kind="ExternalInput")
    dwk = nc.dram_tensor("wk", [D, KVD], F32, kind="ExternalInput")
    dwv = nc.dram_tensor("wv", [D, KVD], F32, kind="ExternalInput")
    dwo = nc.dram_tensor("wo", [D, D], F32, kind="ExternalInput")
    dbo = nc.dram_tensor("bo", [D], F32, kind="ExternalInput")
    dgw = nc.dram_tensor("gw", [NE, D], F32, kind="ExternalInput")
    deb = nc.dram_tensor("eb", [NE], F32, kind="ExternalInput")
    dw1 = nc.dram_tensor("w1", [NE, D, INTER], F32, kind="ExternalInput")
    dw2 = nc.dram_tensor("w2", [NE, INTER, D], F32, kind="ExternalInput")
    dw3 = nc.dram_tensor("w3", [NE, D, INTER], F32, kind="ExternalInput")
    dsw1 = nc.dram_tensor("sw1", [D, SINTER], F32, kind="ExternalInput")
    dsw2 = nc.dram_tensor("sw2", [SINTER, D], F32, kind="ExternalInput")
    dsw3 = nc.dram_tensor("sw3", [D, SINTER], F32, kind="ExternalInput")

    dout = nc.dram_tensor("out_my", [TOKC, D], F32, kind="ExternalOutput")
    doid = nc.dram_tensor("oid", [1, 1], I32, kind="ExternalOutput")
    DEBUG = int(os.environ.get("KN_DEBUG", "0"))
    if DEBUG:
        d_ctx0 = nc.dram_tensor("d_ctx0", [128, H * 512], F32, kind="ExternalOutput")
        d_l0 = nc.dram_tensor("d_l0", [1, 512], F32, kind="ExternalOutput")
        d_ksb0 = nc.dram_tensor("d_ksb0", [128, 16 * 128], F32, kind="ExternalOutput")
        d_vsb0 = nc.dram_tensor("d_vsb0", [128, 16 * 128], F32, kind="ExternalOutput")
        d_q0 = nc.dram_tensor("d_q0", [128, 512], F32, kind="ExternalOutput")
        d_p0 = nc.dram_tensor("d_p0", [128, 1024], F32, kind="ExternalOutput")
        d_x1 = nc.dram_tensor("d_x1", [128, NCH * TOKC], F32, kind="ExternalOutput")
        d_usb0 = nc.dram_tensor("d_usb0", [128, NCH * 512], F32, kind="ExternalOutput")
        d_vdix = nc.dram_tensor("d_vdix", [128, 64], I32, kind="ExternalOutput")
        d_kdix = nc.dram_tensor("d_kdix", [128, 64], I32, kind="ExternalOutput")

    kscr = nc.dram_tensor("kscr", [KVD, KSW], BF16)
    vscr = nc.dram_tensor("vscr", [KSW, KVD], BF16)

    top = ExitStack()
    _CACHED["top"] = top
    E_ = top.enter_context

    sched_i = E_(nc.sbuf_tensor("sched_i", [1, 8], I32))
    sched_f = E_(nc.sbuf_tensor("sched_f", [1, 16], F32))
    id_bf = E_(nc.sbuf_tensor("id_bf", [128, 128], BF16))
    id_f = E_(nc.sbuf_tensor("id_f", [128, 128], F32))
    ones_col_bf = E_(nc.sbuf_tensor("ones_col_bf", [128, 1], BF16))
    ones_row_bf = E_(nc.sbuf_tensor("ones_row_bf", [1, 128], BF16))
    ones_row_f = E_(nc.sbuf_tensor("ones_row_f", [1, 128], F32))
    c_scale = E_(nc.sbuf_tensor("c_scale", [128, 1], F32))
    c_shift = E_(nc.sbuf_tensor("c_shift", [128, 1], F32))
    c_eps = E_(nc.sbuf_tensor("c_eps", [128, 1], F32))
    c_nhalf = E_(nc.sbuf_tensor("c_nhalf", [128, 1], F32))
    bo_sb = E_(nc.sbuf_tensor("bo_sb", [128, NCH], F32))
    ebias_b = E_(nc.sbuf_tensor("ebias_b", [128, NE], F32))
    gw_fm = E_(nc.sbuf_tensor("gw_fm", [128, NCH * NE], BF16))
    oneh = E_(nc.sbuf_tensor("oneh", [16, NE * 128], BF16))
    kdix = E_(nc.sbuf_tensor("kdix", [128, 64], I32))
    vdix = E_(nc.sbuf_tensor("vdix", [128, 64], I32))
    mcol = E_(nc.sbuf_tensor("mcol", [128, 48], F32))
    X1 = E_(nc.sbuf_tensor("X1", [128, NCH * TOKC], BF16))

    ps_s1 = E_(nc.psum_tensor("ps_s1", [128, 1024], F32))
    ps_s2 = E_(nc.psum_tensor("ps_s2", [128, 1024], F32))
    ps_c1 = E_(nc.psum_tensor("ps_c1", [128, 512], F32))
    ps_c2 = E_(nc.psum_tensor("ps_c2", [128, 512], F32))
    ps_c3 = E_(nc.psum_tensor("ps_c3", [128, 512], F32))
    ps_tb = E_(nc.psum_tensor("ps_tb", [128, 1024], BF16))

    r_g1 = nc.gpsimd.alloc_register("r_g1")
    r_g2 = nc.gpsimd.alloc_register("r_g2")

    for arm in nc.switch_core_id([nc.vector], n=128):
        w = int(arm) % 8
        j, b = w % 4, w // 4
        nc.vector.memset(sched_i[0:1, 0:1], w)
        nc.vector.memset(sched_i[0:1, 1:2], b * T * D)
        for v in range(NV):
            nc.vector.memset(sched_i[0:1, 2 + v:3 + v], (b * T + q0tok(w, v)) * D)
        for v in range(NV):
            nc.vector.memset(sched_f[0:1, v:v + 1], float(q0tok(w, v)))


    with ExitStack() as ph:
        Ep = ph.enter_context
        bo_t = Ep(nc.sbuf_tensor("bo_t", [16, 128], F32))
        eb_t = Ep(nc.sbuf_tensor("eb_t", [1, NE], F32))
        gw_t = Ep(nc.sbuf_tensor("gw_t", [16, D], BF16))
        itf = Ep(nc.sbuf_tensor("itf", [1, 16], F32))
        m1f = Ep(nc.sbuf_tensor("m1f", [1, 16], F32))
        m2f = Ep(nc.sbuf_tensor("m2f", [1, 16], F32))
        iop = Ep(nc.sbuf_tensor("iop", [128, 1], F32))
        io65 = Ep(nc.sbuf_tensor("io65", [128, 1], F32))
        gb = Ep(nc.sbuf_tensor("gb", [128, 4], F32))
        q0b = Ep(nc.sbuf_tensor("q0b", [128, 4], F32))
        q0tb = Ep(nc.sbuf_tensor("q0tb", [128, 4], F32))
        q128r = Ep(nc.sbuf_tensor("q128r", [1, 4], F32))
        kdf = Ep(nc.sbuf_tensor("kdf", [128, 64], F32))
        vdf = Ep(nc.sbuf_tensor("vdf", [128, 64], F32))
        vtmp = Ep(nc.sbuf_tensor("vtmp", [128, 1], F32))
        em = Em(nc, top, "p0")

        em.dma("pool", "ld", lambda e: e.dma_start(bo_t[:, :], dbo.ap().rearrange("(c p) -> c p", p=128)), writes=["bo_t"])
        em.dma("pool", "ld", lambda e: e.dma_start(eb_t[:, :], deb.ap().unsqueeze(0)), writes=["eb_t"])
        em.dma("pool", "ld", lambda e: e.dma_start(gw_t[:, :], dgw.ap()), writes=["gw_t"])

        em.add("pool", lambda e: e.memset(id_bf[:, :], 0.0), writes=["id_bf"])
        em.add("pool", lambda e: e.affine_select(out=id_bf[:, :], in_=id_bf[:, :], compare_op=ALU.not_equal,
                                                 fill=1.0, base=0, pattern=[[-1, 128]], channel_multiplier=1),
               reads=["id_bf"], writes=["id_bf"])
        em.add("pool", lambda e: e.memset(id_f[:, :], 0.0), writes=["id_f"])
        em.add("pool", lambda e: e.affine_select(out=id_f[:, :], in_=id_f[:, :], compare_op=ALU.not_equal,
                                                 fill=1.0, base=0, pattern=[[-1, 128]], channel_multiplier=1),
               reads=["id_f"], writes=["id_f"])
        em.add("pool", lambda e: e.memset(ones_col_bf[:, :], 1.0), writes=["ones_col"])
        em.add("pool", lambda e: e.memset(ones_row_bf[:, :], 1.0), writes=["ones_row"])
        em.add("pool", lambda e: e.memset(ones_row_f[:, :], 1.0), writes=["ones_row_f"])
        em.add("pool", lambda e: e.memset(c_scale[:, :], SM_SCALE), writes=["c_scale"])
        em.add("pool", lambda e: e.memset(c_shift[:, :], EXP_SHIFT), writes=["c_shift"])
        em.add("pool", lambda e: e.memset(c_eps[:, :], EPS), writes=["c_eps"])
        em.add("pool", lambda e: e.memset(c_nhalf[:, :], -0.5), writes=["c_nhalf"])
        em.add("pool", lambda e: e.memset(oneh[:, :], 0.0), writes=["oneh"])
        em.add("pool", lambda e: e.affine_select(out=oneh[:, :], in_=oneh[:, :], compare_op=ALU.not_equal,
                                                 fill=1.0, base=0, pattern=[[-1, NE], [0, 128]], channel_multiplier=1),
               reads=["oneh"], writes=["oneh"])

        em.add("pe", lambda e: e.transpose(ps_c1[:, 0:16], bo_t[:, :], id_f[0:16, 0:16]),
               reads=["bo_t", "id_f"], writes=["c1"])
        em.add("dve", lambda e: e.tensor_copy(bo_sb[:, :], ps_c1[:, 0:16]), reads=["c1"], writes=["bo_sb"])
        em.add("pe", lambda e: e.matmul(ps_c2[:, 0:NE], ones_row_f[:, :], eb_t[:, :],
                                        start=True, stop=True),
               reads=["ones_row_f", "eb_t"], writes=["c2"])
        em.add("dve", lambda e: e.tensor_copy(ebias_b[:, :], ps_c2[:, 0:NE]), reads=["c2"], writes=["ebias_b"])
        for c in range(NCH):
            em.add("pe", (lambda c: lambda e: e.transpose(ps_tb[:, c * 16:(c + 1) * 16],
                                                          gw_t[:, c * 128:(c + 1) * 128],
                                                          id_bf[0:16, 0:16]))(c),
                   reads=["gw_t", "id_bf"], writes=["tb"])
        em.add("dve", lambda e: e.tensor_copy(gw_fm[:, 0:NCH * NE], ps_tb[:, 0:NCH * NE]),
               reads=["tb"], writes=["gw_fm"])

        em.add("pool", lambda e: e.iota(itf[0:1, 0:12], [[1, 12]], base=0, channel_multiplier=0,
                                        allow_small_or_imprecise_dtypes=True), writes=["i12"])
        em.add("pool", lambda e: e.iota(iop[:, :], [[0, 1]], base=0, channel_multiplier=1,
                                        allow_small_or_imprecise_dtypes=True), writes=["iop"])
        em.add("pool", lambda e: e.iota(io65[:, :], [[0, 1]], base=0, channel_multiplier=65,
                                        allow_small_or_imprecise_dtypes=True), writes=["io65"])
        em.add("dve", lambda e: e.tensor_scalar(q128r[0:1, 0:4], sched_f[0:1, 0:4], 1.0 / 128.0, None, ALU.mult),
               reads=["sched"], writes=["q128r"])
        em.add("pe", lambda e: e.matmul(ps_c3[:, 0:4], ones_row_f[:, :], sched_f[0:1, 0:4], start=True, stop=True),
               reads=["ones_row_f", "sched"], writes=["c3"])
        em.add("dve", lambda e: e.tensor_copy(q0b[:, 0:4], ps_c3[:, 0:4]), reads=["c3"], writes=["q0b"])
        em.add("pe", lambda e: e.matmul(ps_c3[:, 4:8], ones_row_f[:, :], q128r[0:1, 0:4], start=True, stop=True),
               reads=["ones_row_f", "q128r"], writes=["c3"])
        em.add("dve", lambda e: e.tensor_copy(q0tb[:, 0:4], ps_c3[:, 4:8]), reads=["c3"], writes=["q0tb"])
        for g in range(G):
            em.add("dve", (lambda g: lambda e: e.tensor_scalar(
                gb[:, g:g + 1], io65[:, :], float(g * 128 * 65), None, ALU.add))(g),
                reads=["io65"], writes=["gb"])
        for v in range(NV):
            for g in range(G):
                for p in range(4):
                    col = (v * 4 + g) * 4 + p
                    em.add("dve", (lambda col, g, v, p: lambda e: e.tensor_scalar(
                        kdf[:, col:col + 1], gb[:, g:g + 1], q0tb[:, v:v + 1], float(p), ALU.add, ALU.add))(col, g, v, p),
                        reads=["gb", "q0tb"], writes=["kdf"])
        em.add("dve", lambda e: e.tensor_copy(kdix[:, :], kdf[:, :]), reads=["kdf"], writes=["kdix"])
        for v in range(NV):
            for g in range(G):
                for p in range(4):
                    col = (v * 4 + g) * 4 + p
                    em.add("dve", (lambda col, v, p: lambda e: e.tensor_scalar(
                        vdf[:, col:col + 1], iop[:, :], q0b[:, v:v + 1], float(128 * p), ALU.add, ALU.add))(col, v, p),
                        reads=["iop", "q0b"], writes=["vdf"])
                    em.add("dve", (lambda col: lambda e: e.tensor_scalar(
                        vdf[:, col:col + 1], vdf[:, col:col + 1], 4.0, None, ALU.mult))(col),
                        reads=["vdf"], writes=["vdf"])
                    em.add("dve", (lambda col, g: lambda e: e.tensor_scalar(
                        vdf[:, col:col + 1], vdf[:, col:col + 1], float(g), None, ALU.add))(col, g),
                        reads=["vdf"], writes=["vdf"])
        em.add("dve", lambda e: e.tensor_copy(vdix[:, :], vdf[:, :]), reads=["vdf"], writes=["vdix"])
        for v in range(NV):
            n = NPAIR[v]
            em.add("dve", (lambda v, n: lambda e: e.tensor_scalar(
                m1f[0:1, 0:12], itf[0:1, 0:12], float(n - 16), None, ALU.add))(v, n),
                reads=["i12"], writes=["m1f"])
            em.add("dve", (lambda v: lambda e: e.tensor_scalar(
                m1f[0:1, 0:12], m1f[0:1, 0:12], q128r[0:1, v:v + 1], None, ALU.is_lt))(v),
                reads=["m1f", "q128r"], writes=["m1f"])
            em.add("dve", lambda e: e.tensor_copy(m2f[0:1, 0:12], m1f[0:1, 0:12]),
                   reads=["m1f"], writes=["m2f"])
            em.add("pe", (lambda v: lambda e: e.matmul(
                ps_c3[:, 0:12], ones_row_f[:, :], m2f[0:1, 0:12], start=True, stop=True))(v),
                reads=["ones_row_f", "m2f"], writes=["c3"])
            em.add("dve", (lambda v: lambda e: e.tensor_copy(mcol[:, v * 12:(v + 1) * 12], ps_c3[:, 0:12]))(v),
                   reads=["c3"], writes=["mcol"])

        if DEBUG:
            em.dma("pool", "dbg0", lambda e: e.dma_start(d_vdix.ap(), vdix[:, :]), reads=["vdix"], name="dbgvd")
            em.dma("pool", "dbg0", lambda e: e.dma_start(d_kdix.ap(), kdix[:, :]), reads=["kdix"], name="dbgkd")
        em.drain({"ld": "pool", "dbg0": "pool"})
        with nc.Block() as blk:
            em.emit(blk)

    if PHASES >= 2:
        with ExitStack() as ph:
            Ep = ph.enter_context
            wk_sb = Ep(nc.sbuf_tensor("wk_sb", [128, NCH * KVD], BF16))
            wv_sb = Ep(nc.sbuf_tensor("wv_sb", [128, NCH * KVD], BF16))
            x_tm = [Ep(nc.sbuf_tensor(f"x_tm{i}", [128, 8 * D], BF16)) for i in range(2)]
            xfm = [Ep(nc.sbuf_tensor(f"xfm{i}", [128, NCH * 512], BF16)) for i in range(2)]
            kst = [Ep(nc.sbuf_tensor(f"kst{i}", [128, 512], BF16)) for i in range(2)]
            vst = [Ep(nc.sbuf_tensor(f"vst{i}", [128, 512], BF16)) for i in range(2)]
            em = Em(nc, top, "a1")

            em.dma("pool", "wkv", lambda e: e.dma_start(
                wk_sb[:, :].rearrange("p (c n) -> p c n", c=NCH),
                dwk.ap().rearrange("(c p) n -> p c n", p=128)), writes=["wk"])
            em.dma("pool", "wkv", lambda e: e.dma_start(
                wv_sb[:, :].rearrange("p (c n) -> p c n", c=NCH),
                dwv.ap().rearrange("(c p) n -> p c n", p=128)), writes=["wv"])

            em.add("pool", lambda e: e.reg_load(r_g1, sched_i[0:1, 1:2]), reads=["sched"], name="ldxb")

            for st0 in range(16):
                pb = st0 % 2
                if st0 % 2 == 0:
                    ld = st0 // 2
                    lb = ld % 2
                    def xload(e, ld=ld, lb=lb):
                        e.reg_mov(r_g2, ld * 1024 * D)
                        e.reg_add(r_g2, r_g2, r_g1)
                        return e.dma_start(x_tm[lb][:, :].rearrange("p (t c) -> p t c", t=8),
                                           bass.AP(dx, r_g2, [[D, 128], [128 * D, 8], [1, D]]))
                    em.dma("pool", "xin", xload, reads=[], writes=[f"x8_{lb}"], name=f"xl{ld}")
                lb = (st0 // 2) % 2
                for ti in range(4):
                    toff = (st0 % 2) * 4 + ti
                    for half in range(2):
                        for cc in range(8):
                            c = half * 8 + cc
                            em.add("pe", (lambda lb, c, cc, toff: lambda e: e.transpose(
                                ps_tb[:, cc * 128:(cc + 1) * 128],
                                x_tm[lb][:, toff * D + c * 128: toff * D + (c + 1) * 128], id_bf[:, :]))(lb, c, cc, toff),
                                reads=[f"x8_{lb}", "id_bf"], writes=["tb"], name=f"xt{st0}_{ti}")
                        em.add("dve", (lambda pb, ti, half: lambda e: e.tensor_copy(
                            xfm[pb][:, :].rearrange("p (c t) -> p c t", c=NCH)[:, half * 8:(half + 1) * 8, ti * 128:(ti + 1) * 128],
                            ps_tb[:, :].rearrange("p (c t) -> p c t", c=8)))(pb, ti, half),
                            reads=["tb"], writes=[f"xfm{pb}:{ti}"], name=f"xc{st0}_{ti}")
                for g in range(G):
                    kb = g % 2
                    for c in range(NCH):
                        em.add("pe", (lambda pb, g, c, kb: lambda e: e.matmul(
                            (ps_c1 if kb == 0 else ps_c2)[:, :],
                            wk_sb[:, c * KVD + g * 128: c * KVD + (g + 1) * 128],
                            xfm[pb][:, c * 512:(c + 1) * 512],
                            start=(c == 0), stop=(c == NCH - 1)))(pb, g, c, kb),
                            reads=["wk", f"xfm{pb}:0", f"xfm{pb}:1", f"xfm{pb}:2", f"xfm{pb}:3"],
                            writes=["c1" if kb == 0 else "c2"], name=f"kmm{st0}g{g}")
                    em.add("dve", (lambda kb, g: lambda e: e.tensor_copy(kst[kb][:, :], (ps_c1 if kb == 0 else ps_c2)[:, :]))(kb, g),
                           reads=["c1" if kb == 0 else "c2"], writes=[f"kst{kb}"], name=f"kev{st0}g{g}")
                    em.dma("sp", "kout", (lambda kb, g, st0: lambda e: e.dma_start(
                        bass.AP(kscr, g * 128 * KSW + st0 * 512, [[KSW, 128], [1, 512]]), kst[kb][:, :]))(kb, g, st0),
                        reads=[f"kst{kb}"], name=f"kw{st0}g{g}")
                for ti in range(4):
                    vb = ti % 2
                    for c in range(NCH):
                        em.add("pe", (lambda pb, ti, c, vb: lambda e: e.matmul(
                            (ps_s1 if vb == 0 else ps_s2)[:, 0:512],
                            xfm[pb][:, c * 512 + ti * 128: c * 512 + (ti + 1) * 128],
                            wv_sb[:, c * KVD:(c + 1) * KVD],
                            start=(c == 0), stop=(c == NCH - 1)))(pb, ti, c, vb),
                            reads=["wv", f"xfm{pb}:{ti}"], writes=["s1" if vb == 0 else "s2"], name=f"vmm{st0}t{ti}")
                    em.add("dve", (lambda vb: lambda e: e.tensor_copy(vst[vb][:, :], (ps_s1 if vb == 0 else ps_s2)[:, 0:512]))(vb),
                           reads=["s1" if vb == 0 else "s2"], writes=[f"vst{vb}"], name=f"vev{st0}t{ti}")
                    em.dma("act", "vout", (lambda vb, st0, ti: lambda e: e.dma_start(
                        bass.AP(vscr, (st0 * 512 + ti * 128) * KVD, [[KVD, 128], [1, KVD]]), vst[vb][:, :]))(vb, st0, ti),
                        reads=[f"vst{vb}"], name=f"vw{st0}t{ti}")

            em.drain({"wkv": "pool", "xin": "pool", "kout": "sp", "vout": "act"})
            with nc.Block() as blk:
                em.emit(blk)


    if PHASES >= 3:
        with ExitStack() as ph:
            Ep = ph.enter_context
            x_tm2 = Ep(nc.sbuf_tensor("x_tm2", [128, 4 * D], BF16))
            xfq = Ep(nc.sbuf_tensor("xfq", [128, NCH * 512], BF16))
            wcol = [Ep(nc.sbuf_tensor(f"wcol{i}", [128, NCH * 128], BF16)) for i in range(2)]
            qh = [Ep(nc.sbuf_tensor(f"qh{i}", [128, 512], BF16)) for i in range(2)]
            ksb = Ep(nc.sbuf_tensor("ksb", [128, 64 * 128], BF16))
            vsb = Ep(nc.sbuf_tensor("vsb", [128, 64 * 128], BF16))
            pbuf = [Ep(nc.sbuf_tensor(f"pbuf{i}", [128, 1024], BF16)) for i in range(2)]
            ctx_sb = Ep(nc.sbuf_tensor("ctx_sb", [128, H * 512], BF16))
            usb = Ep(nc.sbuf_tensor("usb", [128, NCH * 512], BF16))
            tmpf = Ep(nc.sbuf_tensor("tmpf", [128, 512], F32))
            tmpb = Ep(nc.sbuf_tensor("tmpb", [128, 512], BF16))
            lrow = Ep(nc.sbuf_tensor("lrow", [1, 512], F32))
            lrow_bf = Ep(nc.sbuf_tensor("lrow_bf", [1, 512], BF16))
            mrow = Ep(nc.sbuf_tensor("mrow", [1, 512], F32))
            mrow_bf = Ep(nc.sbuf_tensor("mrow_bf", [1, 512], BF16))
            rrow = Ep(nc.sbuf_tensor("rrow", [1, 512], F32))
            rrow_bf = Ep(nc.sbuf_tensor("rrow_bf", [1, 512], BF16))
            vrow = Ep(nc.sbuf_tensor("vrow", [1, 512], F32))
            em = Em(nc, top, "a2")

            for v in range(NV):
                n = NPAIR[v]
                em.add("pool", (lambda v: lambda e: e.reg_load(r_g1, sched_i[0:1, 2 + v:3 + v]))(v),
                       reads=["sched"], name=f"ldxmy{v}")
                def xload2(e):
                    return e.dma_start(x_tm2[:, :].rearrange("p (t c) -> p t c", t=4),
                                       bass.AP(dx, r_g1, [[D, 128], [128 * D, 4], [1, D]]))
                em.dma("pool", "xq", xload2, writes=["x2"], name=f"xq{v}")
                for i in range(4):
                    for half in range(2):
                        for cc in range(8):
                            c = half * 8 + cc
                            em.add("pe", (lambda c, cc, i: lambda e: e.transpose(
                                ps_tb[:, cc * 128:(cc + 1) * 128],
                                x_tm2[:, i * D + c * 128: i * D + (c + 1) * 128], id_bf[:, :]))(c, cc, i),
                                reads=["x2", "id_bf"], writes=["tb"], name=f"qt{v}_{i}")
                        em.add("dve", (lambda i, half: lambda e: e.tensor_copy(
                            xfq[:, :].rearrange("p (c t) -> p c t", c=NCH)[:, half * 8:(half + 1) * 8, i * 128:(i + 1) * 128],
                            ps_tb[:, :].rearrange("p (c t) -> p c t", c=8)))(i, half),
                            reads=["tb"], writes=["xfq"], name=f"qc{v}_{i}")

                ctx_regions = []
                for g in range(G):
                    for p in range(4):
                        kcol = (v * 4 + g) * 4 + p
                        vcol = (v * 4 + g) * 4 + p
                        em.dma("pool", "kvd", (lambda p, kcol: lambda e: e.indirect_dma_start(
                            ksb[:, p * 128:(p + 1) * 128], None,
                            bass.AP(kscr, 0, [[128, KVD * (KSW // 128)], [1, 128]]),
                            IndirectOffsetOnAxis(kdix[:, kcol:kcol + 1], 0)))(p, kcol),
                            reads=["kdix"], writes=[f"ks:{p}"], name=f"kd{v}g{g}p{p}")
                        em.dma("pool", "kvd", (lambda p, vcol: lambda e: e.indirect_dma_start(
                            vsb[:, p * 128:(p + 1) * 128], None,
                            bass.AP(vscr, 0, [[128, KSW * 4], [1, 128]]),
                            IndirectOffsetOnAxis(vdix[:, vcol:vcol + 1], 0)))(p, vcol),
                            reads=["vdix"], writes=[f"vs:{p}"], name=f"vd{v}g{g}p{p}")
                    if DEBUG and v == 0 and g == 0:
                        em.dma("pool", "dbg", lambda e: e.dma_start(d_ksb0.ap(), ksb[:, 0:16 * 128]),
                               reads=[f"ks:{sl}" for sl in range(16)], name="dbgk")
                        em.dma("pool", "dbg", lambda e: e.dma_start(d_vsb0.ap(), vsb[:, 0:16 * 128]),
                               reads=[f"vs:{sl}" for sl in range(16)], name="dbgv")
                    nq4 = (n - 4) // 4
                    for qq in range(4):
                        s0 = 4 + qq * nq4
                        em.dma("sp", "kv", (lambda g, qq, s0, nq4: lambda e: e.dma_start(
                            ksb[:, s0 * 128:(s0 + nq4) * 128],
                            bass.AP(kscr, g * 128 * KSW + qq * nq4 * 128, [[KSW, 128], [1, nq4 * 128]])))(g, qq, s0, nq4),
                            writes=[f"ks:{sl}" for sl in range(s0, s0 + nq4)], name=f"kb{v}g{g}q{qq}")
                        em.dma("act", "kv2", (lambda g, qq, s0, nq4: lambda e: e.dma_start(
                            vsb[:, s0 * 128:(s0 + nq4) * 128].rearrange("p (s c) -> p s c", s=nq4),
                            bass.AP(vscr, (qq * nq4 * 128) * KVD + g * 128, [[KVD, 128], [128 * KVD, nq4], [1, 128]])))(g, qq, s0, nq4),
                            writes=[f"vs:{sl}" for sl in range(s0, s0 + nq4)], name=f"vb{v}g{g}q{qq}")

                    for hh in range(4):
                        h = g * 4 + hh
                        qb_ = h % 2
                        em.dma("pool", "wq", (lambda h, qb_: lambda e: e.dma_start(
                            wcol[qb_][:, :].rearrange("p (c m) -> p c m", c=NCH),
                            bass.AP(dwq, h * 128, [[D, 128], [128 * D, NCH], [1, 128]])))(h, qb_),
                            writes=[f"wcol{qb_}"], name=f"wq{h}")
                        for c in range(NCH):
                            em.add("pe", (lambda c, qb_: lambda e: e.matmul(
                                ps_c3[:, :], wcol[qb_][:, c * 128:(c + 1) * 128], xfq[:, c * 512:(c + 1) * 512],
                                start=(c == 0), stop=(c == NCH - 1)))(c, qb_),
                                reads=[f"wcol{qb_}", "xfq"], writes=["c3"], name=f"qmm{h}")
                        em.add("dve", (lambda qb_: lambda e: e.tensor_copy(qh[qb_][:, :], ps_c3[:, :]))(qb_),
                               reads=["c3"], writes=[f"qh{qb_}"], name=f"qev{h}")

                        M = n // 2

                        def emit_s_exp(m):
                            sb_ps = ps_s1 if (m % 2 == 0) else ps_s2
                            sregs = ["s1a", "s1b"] if (m % 2 == 0) else ["s2a", "s2b"]
                            pb_ = m % 2
                            for u in range(2):
                                p = 2 * m + u
                                em.add("pe", (lambda p, u, sb_ps, qb_: lambda e: e.matmul(
                                    sb_ps[:, u * 512:(u + 1) * 512],
                                    ksb[:, p * 128:(p + 1) * 128], qh[qb_][:, :],
                                    start=True, stop=True))(p, u, sb_ps, qb_),
                                    reads=[f"ks:{p}", f"qh{qb_}"], writes=[sregs[u]], name=f"smm{h}p{p}")
                            em.add("act", (lambda sb_ps, pb_: lambda e: e.activation(
                                pbuf[pb_][:, :], sb_ps[:, :], AF.Exp, bias=c_shift[:, :], scale=c_scale[:, :]))(sb_ps, pb_),
                                reads=sregs + ["c_shift", "c_scale"], writes=[f"pb{pb_}"], name=f"exp{h}m{m}")
                            for u in range(2):
                                p = 2 * m + u
                                if p < 4:
                                    em.add("pool", (lambda p, u, pb_: lambda e: e.affine_select(
                                        out=pbuf[pb_][:, u * 512:(u + 1) * 512], in_=pbuf[pb_][:, u * 512:(u + 1) * 512],
                                        compare_op=ALU.is_ge, fill=0.0, base=-128 * p,
                                        pattern=[[1, 512]], channel_multiplier=-1))(p, u, pb_),
                                        reads=[f"pb{pb_}"], writes=[f"pb{pb_}"], name=f"diag{h}p{p}")
                                if p >= n - 12:
                                    mc = v * 12 + (p - (n - 12))
                                    em.add("dve", (lambda p, u, pb_, mc: lambda e: e.tensor_scalar(
                                        pbuf[pb_][:, u * 512:(u + 1) * 512], pbuf[pb_][:, u * 512:(u + 1) * 512],
                                        mcol[:, mc:mc + 1], None, ALU.mult))(p, u, pb_, mc),
                                        reads=[f"pb{pb_}", "mcol"], writes=[f"pb{pb_}"], name=f"gmask{h}p{p}")

                        def emit_pv(m):
                            pb_ = m % 2
                            for u in range(2):
                                p = 2 * m + u
                                em.add("pe", (lambda p, u, pb_: lambda e: e.matmul(
                                    ps_c1[:, :], vsb[:, p * 128:(p + 1) * 128], pbuf[pb_][:, u * 512:(u + 1) * 512],
                                    start=(p == 0), stop=(p == n - 1)))(p, u, pb_),
                                    reads=[f"vs:{p}", f"pb{pb_}"], writes=["c1"], name=f"pv{h}p{p}")
                                em.add("pe", (lambda p, u, pb_: lambda e: e.matmul(
                                    ps_c2[0:1, :], ones_col_bf[:, :], pbuf[pb_][:, u * 512:(u + 1) * 512],
                                    start=(p == 0), stop=(p == n - 1)))(p, u, pb_),
                                    reads=["ones_col", f"pb{pb_}"], writes=["c2"], name=f"lmm{h}p{p}")

                        for m in range(M + 1):
                            if m < M:
                                emit_s_exp(m)
                            if m >= 1:
                                emit_pv(m - 1)

                        if DEBUG and v == 0 and h == 0:
                            em.dma("pool", "dbg", (lambda qb_: lambda e: e.dma_start(d_q0.ap(), qh[qb_][:, :]))(qb_),
                                   reads=[f"qh{qb_}"], name="dbgq")
                            em.dma("pool", "dbg", lambda e: e.dma_start(d_p0.ap(), pbuf[0][:, :]),
                                   reads=["pb0"], name="dbgp")
                        em.add("dve", lambda e: e.reciprocal(lrow[0:1, :], ps_c2[0:1, :]),
                               reads=["c2"], writes=["lrow"], name=f"li{h}")
                        em.add("dve", lambda e: e.tensor_copy(lrow_bf[0:1, :], lrow[0:1, :]),
                               reads=["lrow"], writes=["lrow_bf"], name=f"lb{h}")
                        em.add("pe", lambda e: e.matmul(ps_c3[:, :], ones_row_bf[:, :], lrow_bf[0:1, :], start=True, stop=True),
                               reads=["ones_row", "lrow_bf"], writes=["c3"], name=f"lbc{h}")
                        em.add("dve", lambda e: e.tensor_copy(tmpf[:, :], ps_c1[:, :]),
                               reads=["c1"], writes=["tmpf"], name=f"cev{h}")
                        em.add("dve", (lambda h: lambda e: e.tensor_tensor(
                            ctx_sb[:, h * 512:(h + 1) * 512], tmpf[:, :], ps_c3[:, :], ALU.mult))(h),
                            reads=["tmpf", "c3"], writes=["ctx"], name=f"cscl{h}")

                if DEBUG and v == 0:
                    em.dma("pool", "dbg", lambda e: e.dma_start(d_l0.ap(), lrow[0:1, :]),
                           reads=["lrow"], name="dbgl")
                    em.dma("pool", "dbg", lambda e: e.dma_start(d_ctx0.ap(), ctx_sb[:, :]),
                           reads=["ctx"], name="dbgctx")
                for dch in range(NCH):
                    wb = dch % 2
                    em.dma("pool", "wo", (lambda dch, wb: lambda e: e.dma_start(
                        wcol[wb][:, :].rearrange("p (c m) -> p c m", c=NCH),
                        bass.AP(dwo, dch * 128, [[D, 128], [128 * D, NCH], [1, 128]])))(dch, wb),
                        writes=[f"wcol{wb}"], name=f"wo{v}_{dch}")
                    for hc in range(NCH):
                        em.add("pe", (lambda dch, hc, wb: lambda e: e.matmul(
                            ps_s1[:, 0:512], wcol[wb][:, hc * 128:(hc + 1) * 128], ctx_sb[:, hc * 512:(hc + 1) * 512],
                            start=(hc == 0), stop=(hc == NCH - 1)))(dch, hc, wb),
                            reads=[f"wcol{wb}", "ctx"], writes=["s1a"], name=f"womm{v}_{dch}")
                    em.add("dve", (lambda dch: lambda e: e.scalar_tensor_tensor(
                        usb[:, dch * 512:(dch + 1) * 512], ps_s1[:, 0:512], bo_sb[:, dch:dch + 1],
                        xfq[:, dch * 512:(dch + 1) * 512], ALU.add, ALU.add))(dch),
                        reads=["s1a", "bo_sb", "xfq"], writes=["usb"], name=f"ures{v}_{dch}")

                for c in range(NCH):
                    em.add("pe", (lambda c: lambda e: e.matmul(
                        ps_s1[0:1, 0:512], ones_col_bf[:, :], usb[:, c * 512:(c + 1) * 512],
                        start=(c == 0), stop=(c == NCH - 1)))(c),
                        reads=["ones_col", "usb"], writes=["s1a"], name=f"lns{v}")
                for c in range(NCH):
                    em.add("dve", (lambda c: lambda e: e.tensor_mul(
                        tmpb[:, :], usb[:, c * 512:(c + 1) * 512], usb[:, c * 512:(c + 1) * 512]))(c),
                        reads=["usb"], writes=["tmpb"], name=f"sq{v}_{c}")
                    em.add("pe", (lambda c: lambda e: e.matmul(
                        ps_s2[0:1, 0:512], ones_col_bf[:, :], tmpb[:, :],
                        start=(c == 0), stop=(c == NCH - 1)))(c),
                        reads=["ones_col", "tmpb"], writes=["s2a"], name=f"sqs{v}")
                em.add("dve", lambda e: e.tensor_scalar(mrow[0:1, :], ps_s1[0:1, 0:512], 1.0 / D, None, ALU.mult),
                       reads=["s1a"], writes=["mrow"], name=f"mean{v}")
                em.add("dve", lambda e: e.tensor_scalar(vrow[0:1, :], ps_s2[0:1, 0:512], 1.0 / D, None, ALU.mult),
                       reads=["s2a"], writes=["vrow"], name=f"msq{v}")
                em.add("dve", lambda e: e.tensor_mul(lrow[0:1, :], mrow[0:1, :], mrow[0:1, :]),
                       reads=["mrow"], writes=["lrow"], name=f"m2{v}")
                em.add("dve", lambda e: e.tensor_sub(vrow[0:1, :], vrow[0:1, :], lrow[0:1, :]),
                       reads=["vrow", "lrow"], writes=["vrow"], name=f"var{v}")
                em.add("act", lambda e: e.activation(vrow[0:1, :], vrow[0:1, :], AF.Ln, bias=c_eps[0:1, :]),
                       reads=["vrow", "c_eps"], writes=["vrow"], name=f"lnv{v}")
                em.add("act", lambda e: e.activation(rrow[0:1, :], vrow[0:1, :], AF.Exp, scale=c_nhalf[0:1, :]),
                       reads=["vrow", "c_nhalf"], writes=["rrow"], name=f"rstd{v}")
                em.add("dve", lambda e: e.tensor_copy(mrow_bf[0:1, :], mrow[0:1, :]),
                       reads=["mrow"], writes=["mrow_bf"], name=f"mbf{v}")
                em.add("dve", lambda e: e.tensor_copy(rrow_bf[0:1, :], rrow[0:1, :]),
                       reads=["rrow"], writes=["rrow_bf"], name=f"rbf{v}")
                em.add("pe", lambda e: e.matmul(ps_c3[:, :], ones_row_bf[:, :], mrow_bf[0:1, :], start=True, stop=True),
                       reads=["ones_row", "mrow_bf"], writes=["c3"], name=f"mb{v}")
                em.add("pe", lambda e: e.matmul(ps_c2[:, :], ones_row_bf[:, :], rrow_bf[0:1, :], start=True, stop=True),
                       reads=["ones_row", "rrow_bf"], writes=["c2"], name=f"rb{v}")
                if DEBUG and v == 0:
                    em.dma("pool", "dbg", lambda e: e.dma_start(d_usb0.ap(), usb[:, :]),
                           reads=["usb"], name="dbgu")
                for c in range(NCH):
                    em.add("dve", (lambda c: lambda e: e.tensor_sub(
                        tmpb[:, :], usb[:, c * 512:(c + 1) * 512], ps_c3[:, :]))(c),
                        reads=["usb", "c3"], writes=["tmpb"], name=f"ap1{v}_{c}")
                    em.add("dve", (lambda c, v: lambda e: e.tensor_tensor(
                        X1[:, c * TOKC + v * 512: c * TOKC + (v + 1) * 512], tmpb[:, :], ps_c2[:, :], ALU.mult))(c, v),
                        reads=["tmpb", "c2"], writes=[f"X1:{v}"], name=f"ap2{v}_{c}")

            if DEBUG:
                em.dma("pool", "dbg", lambda e: e.dma_start(d_x1.ap(), X1[:, :]),
                       reads=["X1:0", "X1:1", "X1:2", "X1:3"], name="dbgx1")
            em.drain({"xq": "pool", "wq": "pool", "wo": "pool", "kv": "sp", "kv2": "act", "kvd": "pool", "dbg": "pool"})
            with nc.Block() as blk:
                em.emit(blk)


    if PHASES >= 4:
        with ExitStack() as ph:
            Ep = ph.enter_context
            Y = Ep(nc.sbuf_tensor("Y", [128, NCH * TAU], F32))
            Hb = Ep(nc.sbuf_tensor("Hb", [128, ICH * TAU], BF16))
            w1c = [Ep(nc.sbuf_tensor(f"w1c{i}", [128, NCH * 128], BF16)) for i in range(2)]
            w3c = [Ep(nc.sbuf_tensor(f"w3c{i}", [128, NCH * 128], BF16)) for i in range(2)]
            w2c = [Ep(nc.sbuf_tensor(f"w2c{i}", [128, ICH * 128], BF16)) for i in range(2)]
            gsc = Ep(nc.sbuf_tensor("gsc", [128, NE], F32))
            gbs = Ep(nc.sbuf_tensor("gbs", [128, NE], F32))
            grp = Ep(nc.sbuf_tensor("grp", [128, 4], F32))
            grp2 = Ep(nc.sbuf_tensor("grp2", [128, 4], F32))
            msc = Ep(nc.sbuf_tensor("msc", [128, NE], F32))
            cur = Ep(nc.sbuf_tensor("cur", [128, NE], F32))
            indt = Ep(nc.sbuf_tensor("indt", [128, NE], F32))
            sc1 = Ep(nc.sbuf_tensor("sc1", [128, 1], F32))
            sc2 = Ep(nc.sbuf_tensor("sc2", [128, 1], F32))
            cw = Ep(nc.sbuf_tensor("cw", [128, 8 * NE], F32))
            cwb16 = Ep(nc.sbuf_tensor("cwb16", [128, NE], BF16))
            cwT = Ep(nc.sbuf_tensor("cwT", [16, 1024], BF16))
            cwB = [Ep(nc.sbuf_tensor(f"cwB{i}", [128, 512], F32)) for i in range(2)]
            silu_t = Ep(nc.sbuf_tensor("silu_t", [128, 512], F32))
            btmp = Ep(nc.sbuf_tensor("btmp", [128, 512], F32))
            outm = Ep(nc.sbuf_tensor("outm", [128, D], F32))
            sqm = Ep(nc.sbuf_tensor("sqm", [128, D], F32))
            srow = Ep(nc.sbuf_tensor("srow", [128, 1], F32))
            qrow = Ep(nc.sbuf_tensor("qrow", [128, 1], F32))
            vr2 = Ep(nc.sbuf_tensor("vr2", [128, 1], F32))
            em = Em(nc, top, "bb")

            for tau in range(2):
                t0 = tau * TAU
                for sub in range(8):
                    tok0 = t0 + sub * 128
                    for c in range(NCH):
                        em.add("pe", (lambda c, tok0: lambda e: e.matmul(
                            ps_c2[:, 0:NE], X1[:, c * TOKC + tok0: c * TOKC + tok0 + 128],
                            gw_fm[:, c * NE:(c + 1) * NE],
                            start=(c == 0), stop=(c == NCH - 1)))(c, tok0),
                            reads=[f"X1:{2 * tau}", f"X1:{2 * tau + 1}"], writes=["c2"], name=f"gmm{tau}_{sub}")
                    em.add("act", lambda e: e.activation(gsc[:, :], ps_c2[:, 0:NE], AF.Sigmoid),
                           reads=["c2"], writes=["gsc"], name=f"sig{tau}_{sub}")
                    em.add("dve", lambda e: e.tensor_add(gbs[:, :], gsc[:, :], ebias_b[:, :]),
                           reads=["gsc", "ebias_b"], writes=["gbs"], name=f"gb{tau}_{sub}")
                    for gg in range(4):
                        em.add("dve", (lambda gg: lambda e: e.reduce_max(
                            grp[:, gg:gg + 1], gbs[:, 4 * gg:4 * (gg + 1)], axis=AX.X))(gg),
                            reads=["gbs"], writes=["grp"], name=f"gmax{tau}_{sub}_{gg}")
                    em.add("dve", lambda e: e.reduce_max(sc1[:, :], grp[:, :], axis=AX.X),
                           reads=["grp"], writes=["sc1"], name=f"m1_{tau}_{sub}")
                    em.add("dve", lambda e: e.tensor_scalar(grp2[:, :], grp[:, :], sc1[:, 0:1], None, ALU.is_equal),
                           reads=["grp", "sc1"], writes=["grp2"], name=f"e1_{tau}_{sub}")
                    em.add("dve", lambda e: e.scalar_tensor_tensor(grp2[:, :], grp2[:, :], -BIG, grp[:, :], ALU.mult, ALU.add),
                           reads=["grp2", "grp"], writes=["grp2"], name=f"g2_{tau}_{sub}")
                    em.add("dve", lambda e: e.reduce_max(sc2[:, :], grp2[:, :], axis=AX.X),
                           reads=["grp2"], writes=["sc2"], name=f"m2_{tau}_{sub}")
                    for gg in range(4):
                        em.add("dve", (lambda gg: lambda e: e.tensor_scalar(
                            sc1[:, :], grp[:, gg:gg + 1], sc2[:, 0:1], None, ALU.is_ge))(gg),
                            reads=["grp", "sc2"], writes=["sc1"], name=f"gi{tau}_{sub}_{gg}")
                        em.add("dve", (lambda gg: lambda e: e.tensor_scalar(
                            sc1[:, :], sc1[:, :], BIG, None, ALU.mult))(gg),
                            reads=["sc1"], writes=["sc1"], name=f"gp{tau}_{sub}_{gg}")
                        em.add("dve", (lambda gg: lambda e: e.tensor_scalar(
                            sc1[:, :], sc1[:, :], -BIG, None, ALU.add))(gg),
                            reads=["sc1"], writes=["sc1"], name=f"gp2{tau}_{sub}_{gg}")
                        em.add("dve", (lambda gg: lambda e: e.tensor_scalar(
                            msc[:, 4 * gg:4 * (gg + 1)], gbs[:, 4 * gg:4 * (gg + 1)], sc1[:, 0:1], None, ALU.add))(gg),
                            reads=["gbs", "sc1"], writes=["msc"], name=f"gm{tau}_{sub}_{gg}")
                    em.add("dve", lambda e: e.tensor_copy(cur[:, :], msc[:, :]),
                           reads=["msc"], writes=["cur"], name=f"cp{tau}_{sub}")
                    for it in range(3):
                        em.add("dve", lambda e: e.reduce_max(sc1[:, :], cur[:, :], axis=AX.X),
                               reads=["cur"], writes=["sc1"], name=f"tm{tau}_{sub}_{it}")
                        em.add("dve", lambda e: e.tensor_scalar(indt[:, :], cur[:, :], sc1[:, 0:1], None, ALU.is_ge),
                               reads=["cur", "sc1"], writes=["indt"], name=f"ti{tau}_{sub}_{it}")
                        em.add("dve", lambda e: e.scalar_tensor_tensor(cur[:, :], indt[:, :], -BIG, cur[:, :], ALU.mult, ALU.add),
                               reads=["indt", "cur"], writes=["cur"], name=f"tc{tau}_{sub}_{it}")
                    em.add("dve", lambda e: e.reduce_max(sc1[:, :], cur[:, :], axis=AX.X),
                           reads=["cur"], writes=["sc1"], name=f"m4_{tau}_{sub}")
                    em.add("dve", lambda e: e.tensor_scalar(indt[:, :], msc[:, :], sc1[:, 0:1], None, ALU.is_ge),
                           reads=["msc", "sc1"], writes=["indt"], name=f"ind{tau}_{sub}")
                    em.add("dve", lambda e: e.tensor_mul(indt[:, :], indt[:, :], gsc[:, :]),
                           reads=["indt", "gsc"], writes=["indt"], name=f"w_{tau}_{sub}")
                    em.add("dve", lambda e: e.reduce_sum(sc1[:, :], indt[:, :], axis=AX.X),
                           reads=["indt"], writes=["sc1"], name=f"ws{tau}_{sub}")
                    em.add("dve", lambda e: e.reciprocal(sc1[:, :], sc1[:, :]),
                           reads=["sc1"], writes=["sc1"], name=f"wr{tau}_{sub}")
                    em.add("dve", (lambda sub: lambda e: e.tensor_scalar(
                        cw[:, sub * NE:(sub + 1) * NE], indt[:, :], sc1[:, 0:1], None, ALU.mult))(sub),
                        reads=["indt", "sc1"], writes=["cw"], name=f"cw{tau}_{sub}")
                    em.add("dve", (lambda sub: lambda e: e.tensor_copy(cwb16[:, :], cw[:, sub * NE:(sub + 1) * NE]))(sub),
                           reads=["cw"], writes=["cwb16"], name=f"cwb{tau}_{sub}")
                    em.add("pe", (lambda sub: lambda e: e.transpose(
                        ps_tb[0:16, sub * 128:(sub + 1) * 128], cwb16[:, 0:NE], id_bf[:, :]))(sub),
                        reads=["cwb16", "id_bf"], writes=["tb"], name=f"cwt{tau}_{sub}")
                em.add("dve", lambda e: e.tensor_copy(cwT[0:16, :], ps_tb[0:16, :]),
                       reads=["tb"], writes=["cwT"], name=f"cwc{tau}")

                for ex in range(NE):
                    for s in range(2):
                        em.add("pe", (lambda ex, s: lambda e: e.matmul(
                            ps_c3[:, :], oneh[:, ex * 128:(ex + 1) * 128], cwT[0:16, s * 512:(s + 1) * 512],
                            start=True, stop=True))(ex, s),
                            reads=["oneh", "cwT"], writes=["c3"], name=f"cb{tau}e{ex}s{s}")
                        em.add("dve", (lambda s: lambda e: e.tensor_copy(cwB[s][:, :], ps_c3[:, :]))(s),
                               reads=["c3"], writes=[f"cwB{s}"], name=f"cbe{tau}e{ex}s{s}")
                    for mch in range(ICH):
                        wb = mch % 2
                        em.dma("pool", "wex", (lambda ex, mch, wb: lambda e: e.dma_start(
                            w1c[wb][:, :].rearrange("p (c m) -> p c m", c=NCH),
                            bass.AP(dw1, ex * D * INTER + mch * 128, [[INTER, 128], [128 * INTER, NCH], [1, 128]])))(ex, mch, wb),
                            writes=[f"w1c{wb}"], name=f"w1{tau}e{ex}m{mch}")
                        em.dma("pool", "wex", (lambda ex, mch, wb: lambda e: e.dma_start(
                            w3c[wb][:, :].rearrange("p (c m) -> p c m", c=NCH),
                            bass.AP(dw3, ex * D * INTER + mch * 128, [[INTER, 128], [128 * INTER, NCH], [1, 128]])))(ex, mch, wb),
                            writes=[f"w3c{wb}"], name=f"w3{tau}e{ex}m{mch}")
                        a_ps = [ps_s1, ps_s2][mch % 2]
                        a_reg = ["s1a", "s1b"] if mch % 2 == 0 else ["s2a", "s2b"]
                        for s in range(2):
                            for c in range(NCH):
                                em.add("pe", (lambda c, s, wb, t0, a_ps: lambda e: e.matmul(
                                    a_ps[:, s * 512:(s + 1) * 512], w1c[wb][:, c * 128:(c + 1) * 128],
                                    X1[:, c * TOKC + t0 + s * 512: c * TOKC + t0 + (s + 1) * 512],
                                    start=(c == 0), stop=(c == NCH - 1)))(c, s, wb, t0, a_ps),
                                    reads=[f"w1c{wb}", f"X1:{2 * tau}", f"X1:{2 * tau + 1}"], writes=[a_reg[s]], name=f"a{tau}e{ex}m{mch}s{s}")
                                em.add("pe", (lambda c, s, wb, t0: lambda e: e.matmul(
                                    [ps_c2, ps_c3][s][:, :], w3c[wb][:, c * 128:(c + 1) * 128],
                                    X1[:, c * TOKC + t0 + s * 512: c * TOKC + t0 + (s + 1) * 512],
                                    start=(c == 0), stop=(c == NCH - 1)))(c, s, wb, t0),
                                    reads=[f"w3c{wb}", f"X1:{2 * tau}", f"X1:{2 * tau + 1}"], writes=[["c2", "c3"][s]], name=f"b{tau}e{ex}m{mch}s{s}")
                            em.add("act", (lambda s, a_ps: lambda e: e.activation(
                                silu_t[:, :], a_ps[:, s * 512:(s + 1) * 512], AF.Silu))(s, a_ps),
                                reads=[a_reg[s]], writes=["silu_t"], name=f"si{tau}e{ex}m{mch}s{s}")
                            em.add("dve", (lambda s: lambda e: e.tensor_tensor(
                                btmp[:, :], cwB[s][:, :], [ps_c2, ps_c3][s][:, :], ALU.mult))(s),
                                reads=[f"cwB{s}", ["c2", "c3"][s]], writes=["btmp"], name=f"bt{tau}e{ex}m{mch}s{s}")
                            em.add("dve", (lambda s, mch: lambda e: e.tensor_tensor(
                                Hb[:, mch * TAU + s * 512: mch * TAU + (s + 1) * 512], silu_t[:, :], btmp[:, :], ALU.mult))(s, mch),
                                reads=["silu_t", "btmp"], writes=[f"Hb:{mch}"], name=f"h{tau}e{ex}m{mch}s{s}")
                    for dch in range(NCH):
                        wb = dch % 2
                        em.dma("pool", "wex", (lambda ex, dch, wb: lambda e: e.dma_start(
                            w2c[wb][:, :].rearrange("p (c m) -> p c m", c=ICH),
                            bass.AP(dw2, ex * INTER * D + dch * 128, [[D, 128], [128 * D, ICH], [1, 128]])))(ex, dch, wb),
                            writes=[f"w2c{wb}"], name=f"w2{tau}e{ex}d{dch}")
                        for s in range(2):
                            yb = (2 * dch + s) % 2
                            y_ps = [ps_c1, ps_s1][yb]
                            y_reg = ["c1", "s1a"][yb]
                            for mch in range(ICH):
                                em.add("pe", (lambda mch, s, wb, y_ps: lambda e: e.matmul(
                                    y_ps[:, 0:512], w2c[wb][:, mch * 128:(mch + 1) * 128],
                                    Hb[:, mch * TAU + s * 512: mch * TAU + (s + 1) * 512],
                                    start=(mch == 0), stop=(mch == ICH - 1)))(mch, s, wb, y_ps),
                                    reads=[f"w2c{wb}", f"Hb:{mch}"], writes=[y_reg], name=f"y{tau}e{ex}d{dch}s{s}")
                            if ex == 0:
                                em.add("dve", (lambda dch, s, y_ps: lambda e: e.tensor_copy(
                                    Y[:, dch * TAU + s * 512: dch * TAU + (s + 1) * 512], y_ps[:, 0:512]))(dch, s, y_ps),
                                    reads=[y_reg], writes=[f"Y:{dch}"], name=f"ye{tau}e{ex}d{dch}s{s}")
                            else:
                                em.add("dve", (lambda dch, s, y_ps: lambda e: e.tensor_tensor(
                                    Y[:, dch * TAU + s * 512: dch * TAU + (s + 1) * 512],
                                    Y[:, dch * TAU + s * 512: dch * TAU + (s + 1) * 512], y_ps[:, 0:512], ALU.add))(dch, s, y_ps),
                                    reads=[y_reg, f"Y:{dch}"], writes=[f"Y:{dch}"], name=f"ya{tau}e{ex}d{dch}s{s}")

                for half in range(2):
                    mlist = list(range(half * 11, min(SCH, half * 11 + 11)))
                    for mi, mch in enumerate(mlist):
                        wb = mi % 2
                        em.dma("pool", "wex", (lambda mch, wb: lambda e: e.dma_start(
                            w1c[wb][:, :].rearrange("p (c m) -> p c m", c=NCH),
                            bass.AP(dsw1, mch * 128, [[SINTER, 128], [128 * SINTER, NCH], [1, 128]])))(mch, wb),
                            writes=[f"w1c{wb}"], name=f"sw1{tau}m{mch}")
                        em.dma("pool", "wex", (lambda mch, wb: lambda e: e.dma_start(
                            w3c[wb][:, :].rearrange("p (c m) -> p c m", c=NCH),
                            bass.AP(dsw3, mch * 128, [[SINTER, 128], [128 * SINTER, NCH], [1, 128]])))(mch, wb),
                            writes=[f"w3c{wb}"], name=f"sw3{tau}m{mch}")
                        a_ps = [ps_s1, ps_s2][mi % 2]
                        a_reg = ["s1a", "s1b"] if mi % 2 == 0 else ["s2a", "s2b"]
                        for s in range(2):
                            for c in range(NCH):
                                em.add("pe", (lambda c, s, wb, t0, a_ps: lambda e: e.matmul(
                                    a_ps[:, s * 512:(s + 1) * 512], w1c[wb][:, c * 128:(c + 1) * 128],
                                    X1[:, c * TOKC + t0 + s * 512: c * TOKC + t0 + (s + 1) * 512],
                                    start=(c == 0), stop=(c == NCH - 1)))(c, s, wb, t0, a_ps),
                                    reads=[f"w1c{wb}", f"X1:{2 * tau}", f"X1:{2 * tau + 1}"], writes=[a_reg[s]], name=f"sa{tau}m{mch}s{s}")
                                em.add("pe", (lambda c, s, wb, t0: lambda e: e.matmul(
                                    [ps_c2, ps_c3][s][:, :], w3c[wb][:, c * 128:(c + 1) * 128],
                                    X1[:, c * TOKC + t0 + s * 512: c * TOKC + t0 + (s + 1) * 512],
                                    start=(c == 0), stop=(c == NCH - 1)))(c, s, wb, t0),
                                    reads=[f"w3c{wb}", f"X1:{2 * tau}", f"X1:{2 * tau + 1}"], writes=[["c2", "c3"][s]], name=f"sb{tau}m{mch}s{s}")
                            em.add("act", (lambda s, a_ps: lambda e: e.activation(
                                silu_t[:, :], a_ps[:, s * 512:(s + 1) * 512], AF.Silu))(s, a_ps),
                                reads=[a_reg[s]], writes=["silu_t"], name=f"ssi{tau}m{mch}s{s}")
                            em.add("dve", (lambda s, mi: lambda e: e.tensor_tensor(
                                Hb[:, mi * TAU + s * 512: mi * TAU + (s + 1) * 512], silu_t[:, :],
                                [ps_c2, ps_c3][s][:, :], ALU.mult))(s, mi),
                                reads=["silu_t", ["c2", "c3"][s]], writes=[f"Hb:{mi}"], name=f"sh{tau}m{mch}s{s}")
                    for dch in range(NCH):
                        wb = dch % 2
                        em.dma("pool", "wex", (lambda dch, wb, half: lambda e: e.dma_start(
                            w2c[wb][:, 0:len(mlist) * 128].rearrange("p (c m) -> p c m", c=len(mlist)),
                            bass.AP(dsw2, (half * 11 * 128) * D + dch * 128, [[D, 128], [128 * D, len(mlist)], [1, 128]])))(dch, wb, half),
                            writes=[f"w2c{wb}"], name=f"sw2{tau}h{half}d{dch}")
                        for s in range(2):
                            yb = (2 * dch + s) % 2
                            y_ps = [ps_c1, ps_s1][yb]
                            y_reg = ["c1", "s1a"][yb]
                            for mi in range(len(mlist)):
                                em.add("pe", (lambda mi, s, wb, y_ps, nm=len(mlist): lambda e: e.matmul(
                                    y_ps[:, 0:512], w2c[wb][:, mi * 128:(mi + 1) * 128],
                                    Hb[:, mi * TAU + s * 512: mi * TAU + (s + 1) * 512],
                                    start=(mi == 0), stop=(mi == nm - 1)))(mi, s, wb, y_ps),
                                    reads=[f"w2c{wb}", f"Hb:{mi}"], writes=[y_reg], name=f"sy{tau}h{half}d{dch}s{s}")
                            em.add("dve", (lambda dch, s, y_ps: lambda e: e.tensor_tensor(
                                Y[:, dch * TAU + s * 512: dch * TAU + (s + 1) * 512],
                                Y[:, dch * TAU + s * 512: dch * TAU + (s + 1) * 512], y_ps[:, 0:512], ALU.add))(dch, s, y_ps),
                                reads=[y_reg, f"Y:{dch}"], writes=[f"Y:{dch}"], name=f"sya{tau}h{half}d{dch}s{s}")

                for c in range(NCH):
                    em.add("dve", (lambda c, t0: lambda e: e.tensor_tensor(
                        Y[:, c * TAU:(c + 1) * TAU], Y[:, c * TAU:(c + 1) * TAU],
                        X1[:, c * TOKC + t0: c * TOKC + t0 + TAU], ALU.add))(c, t0),
                        reads=[f"Y:{c}", f"X1:{2 * tau}", f"X1:{2 * tau + 1}"], writes=[f"Y:{c}"], name=f"yx{tau}_{c}")
                for ti in range(8):
                    for q in range(4):
                        for cc in range(4):
                            c = q * 4 + cc
                            em.add("pe", (lambda c, cc, ti: lambda e: e.transpose(
                                ps_c3[:, cc * 128:(cc + 1) * 128], Y[:, c * TAU + ti * 128: c * TAU + (ti + 1) * 128],
                                id_f[:, :]))(c, cc, ti),
                                reads=[f"Y:{c}", "id_f"], writes=["c3"], name=f"ot{tau}_{ti}_{c}")
                        em.add("dve", (lambda q, ti: lambda e: e.tensor_copy(
                            outm[:, q * 512:(q + 1) * 512], ps_c3[:, :]))(q, ti),
                            reads=["c3"], writes=["outm"], name=f"oc{tau}_{ti}_{q}")
                    em.add("dve", (lambda: lambda e: e.reduce_sum(srow[:, :], outm[:, :], axis=AX.X))(),
                           reads=["outm"], writes=["srow"], name=f"os{tau}_{ti}")
                    em.add("dve", lambda e: e.tensor_mul(sqm[:, :], outm[:, :], outm[:, :]),
                           reads=["outm"], writes=["sqm"], name=f"oq{tau}_{ti}")
                    em.add("dve", lambda e: e.reduce_sum(qrow[:, :], sqm[:, :], axis=AX.X),
                           reads=["sqm"], writes=["qrow"], name=f"oqs{tau}_{ti}")
                    em.add("dve", lambda e: e.tensor_scalar(srow[:, :], srow[:, :], 1.0 / D, None, ALU.mult),
                           reads=["srow"], writes=["srow"], name=f"om{tau}_{ti}")
                    em.add("dve", lambda e: e.tensor_scalar(qrow[:, :], qrow[:, :], 1.0 / D, None, ALU.mult),
                           reads=["qrow"], writes=["qrow"], name=f"oq2{tau}_{ti}")
                    em.add("dve", lambda e: e.tensor_mul(vr2[:, :], srow[:, :], srow[:, :]),
                           reads=["srow"], writes=["vr2"], name=f"ov{tau}_{ti}")
                    em.add("dve", lambda e: e.tensor_sub(vr2[:, :], qrow[:, :], vr2[:, :]),
                           reads=["qrow", "vr2"], writes=["vr2"], name=f"ov2{tau}_{ti}")
                    em.add("act", lambda e: e.activation(vr2[:, :], vr2[:, :], AF.Ln, bias=c_eps[:, :]),
                           reads=["vr2", "c_eps"], writes=["vr2"], name=f"oln{tau}_{ti}")
                    em.add("act", lambda e: e.activation(vr2[:, :], vr2[:, :], AF.Exp, scale=c_nhalf[:, :]),
                           reads=["vr2", "c_nhalf"], writes=["vr2"], name=f"orst{tau}_{ti}")
                    em.add("dve", lambda e: e.tensor_scalar(outm[:, :], outm[:, :], srow[:, 0:1], vr2[:, 0:1], ALU.subtract, ALU.mult),
                           reads=["outm", "srow", "vr2"], writes=["outm"], name=f"oap{tau}_{ti}")
                    em.dma("sp", "out", (lambda tau, ti: lambda e: e.dma_start(
                        bass.AP(dout, (tau * TAU + ti * 128) * D, [[D, 128], [1, D]]), outm[:, :]))(tau, ti),
                        reads=["outm"], name=f"ow{tau}_{ti}")

            em.dma("sp", "out", lambda e: e.dma_start(doid.ap(), sched_i[0:1, 0:1]), reads=["sched"], name="oid")
            em.drain({"wex": "pool", "out": "sp"})
            with nc.Block() as blk:
                em.emit(blk)


    _CACHED["nc"] = nc
    return nc


def kernel(L0_attn_x, L0_attn_Wq, L0_attn_Wk, L0_attn_Wv, L0_attn_Wo, L0_attn_bo,
           L0_ffn_gate_w, L0_ffn_expert_bias, L0_ffn_w1, L0_ffn_w2, L0_ffn_w3,
           L0_ffn_sw1, L0_ffn_sw2, L0_ffn_sw3):
    nc = _build()
    inp = {
        "x": np.ascontiguousarray(L0_attn_x, dtype=np.float32),
        "wq": np.ascontiguousarray(L0_attn_Wq, dtype=np.float32),
        "wk": np.ascontiguousarray(L0_attn_Wk, dtype=np.float32),
        "wv": np.ascontiguousarray(L0_attn_Wv, dtype=np.float32),
        "wo": np.ascontiguousarray(L0_attn_Wo, dtype=np.float32),
        "bo": np.ascontiguousarray(L0_attn_bo, dtype=np.float32),
        "gw": np.ascontiguousarray(L0_ffn_gate_w, dtype=np.float32),
        "eb": np.ascontiguousarray(L0_ffn_expert_bias, dtype=np.float32),
        "w1": np.ascontiguousarray(L0_ffn_w1, dtype=np.float32),
        "w2": np.ascontiguousarray(L0_ffn_w2, dtype=np.float32),
        "w3": np.ascontiguousarray(L0_ffn_w3, dtype=np.float32),
        "sw1": np.ascontiguousarray(L0_ffn_sw1, dtype=np.float32),
        "sw2": np.ascontiguousarray(L0_ffn_sw2, dtype=np.float32),
        "sw3": np.ascontiguousarray(L0_ffn_sw3, dtype=np.float32),
    }
    res = run_bass_kernel_spmd(nc, [dict(inp) for _ in range(8)], core_ids=list(range(8)))
    out = np.empty((B, T, D), dtype=np.float32)
    seen = set()
    for c in range(8):
        r = res.results[c]
        w = int(np.asarray(r["oid"]).reshape(-1)[0])
        assert 0 <= w < 8 and w not in seen, f"bad workload ids: {w}, {seen}"
        seen.add(w)
        j, b = w % 4, w // 4
        om = np.asarray(r["out_my"]).reshape(TOKC, D)
        for v in range(NV):
            q0 = q0tok(w, v)
            out[b, q0:q0 + QB, :] = om[v * QB:(v + 1) * QB, :]
    return out


# version 05
# speedup vs baseline: 1.8455x; 1.3006x over previous
import math
import os
import numpy as np
from contextlib import ExitStack

import concourse.bass as bass
import concourse.mybir as mybir
from concourse.bass_utils import run_bass_kernel_spmd
from concourse.bass import IndirectOffsetOnAxis

F32 = mybir.dt.float32
BF16 = mybir.dt.bfloat16
I32 = mybir.dt.int32
AF = mybir.ActivationFunctionType
ALU = mybir.AluOpType
AX = mybir.AxisListType

B, T, D = 2, 8192, 2048
NCH = 16
H, G, DH = 16, 4, 128
KVD = 512
NE, INTER, ICH = 16, 1408, 11
SINTER, SCH = 2816, 22
QB = 512
NV = 4
TOKC = 2048
TAU = 1024
KSW = T + 128
EXP_SHIFT = -12.0
SM_SCALE = 1.0 / math.sqrt(DH)
EPS = 1e-5
BIG = 1.0e9

PPREF = [2048 * (v + 1) for v in range(NV)]
NPAIR = [p // 128 for p in PPREF]
PAIR_OFF = [0, 16, 48, 96]
NPAIR_TOT = 160

def q0tok(w, v):
    j = w % 4
    return (j + 4 * v) * QB

ENG2BLK = {"pe": "tensor", "dve": "vector", "act": "scalar", "pool": "gpsimd", "sp": "sync"}

class _Op:
    __slots__ = ("eng", "fn", "deps", "inc", "event", "clock", "stream", "waits", "name")
    def __init__(self):
        self.deps = []
        self.inc = False
        self.event = None
        self.clock = None
        self.stream = None
        self.waits = None
        self.name = ""

class Em:
    def __init__(self, nc, es, tag):
        self.nc = nc
        self.es = es
        self.tag = tag
        self.ops = []
        self.lw = {}
        self.rd = {}
        self.sems = {}
        self.scnt = {}

    def _sem(self, key):
        s = self.sems.get(key)
        if s is None:
            s = self.es.enter_context(self.nc.semaphore(f"{self.tag}_{key}".replace(":", "_")))
            self.sems[key] = s
        return s

    def add(self, eng, fn, reads=(), writes=(), name=""):
        op = _Op()
        op.eng = eng
        op.fn = fn
        op.name = name
        deps = {}
        for r in reads:
            w = self.lw.get(r)
            if w is not None:
                deps[id(w)] = w
        for r in writes:
            w = self.lw.get(r)
            if w is not None:
                deps[id(w)] = w
            for rr in self.rd.get(r, {}).values():
                deps[id(rr)] = rr
        if deps.pop(id(op), None) is not None:
            pass
        op.deps = list(deps.values())
        for r in reads:
            self.rd.setdefault(r, {})[eng] = op
        for r in writes:
            self.lw[r] = op
            self.rd[r] = {eng: op}
        self.ops.append(op)
        return op

    def dma(self, eng, stream, fn, reads=(), writes=(), name=""):
        op = self.add(eng, fn, reads, writes, name=name)
        op.stream = stream
        k = self.scnt.get(stream, 0) + 1
        self.scnt[stream] = k
        op.event = ("s:" + stream, 16 * k)
        return op

    def stream_count(self, stream):
        return self.scnt.get(stream, 0)

    def emit(self, block):
        for op in self.ops:
            for d in op.deps:
                if d.stream is not None:
                    continue
                if d.eng == "pe" and op.eng == "pe":
                    continue
                d.inc = True
        ctr = {}
        for op in self.ops:
            if op.stream is None and op.inc:
                c = ctr.get(op.eng, 0) + 1
                ctr[op.eng] = c
                op.event = ("e:" + op.eng, c)
        known = {}
        for op in self.ops:
            st = known.setdefault(op.eng, {})
            clk = {}
            need = {}
            for d in op.deps:
                if d.clock:
                    for s2, v2 in d.clock.items():
                        if clk.get(s2, 0) < v2:
                            clk[s2] = v2
                ev = d.event
                if ev is not None:
                    if clk.get(ev[0], 0) < ev[1]:
                        clk[ev[0]] = ev[1]
                    if d.eng == "pe" and op.eng == "pe":
                        continue
                    if need.get(ev[0], 0) < ev[1]:
                        need[ev[0]] = ev[1]
            waits = []
            for semk, val in need.items():
                if st.get(semk, 0) < val:
                    waits.append((semk, val))
            if waits:
                for d in op.deps:
                    ev = d.event
                    if ev is not None and st.get(ev[0], 0) < ev[1]:
                        st[ev[0]] = ev[1]
                        if d.clock:
                            for s2, v2 in d.clock.items():
                                if st.get(s2, 0) < v2:
                                    st[s2] = v2
            for s2, v2 in st.items():
                if clk.get(s2, 0) < v2:
                    clk[s2] = v2
            if op.event is not None:
                clk[op.event[0]] = op.event[1]
            op.clock = clk
            op.waits = waits
        if os.environ.get("KN_WAITSTATS"):
            import collections
            wc = collections.Counter()
            for op in self.ops:
                wc[(op.eng, "wait")] += len(op.waits)
                if op.stream is None and op.inc:
                    wc[(op.eng, "inc")] += 1
            print(f"[{self.tag}] waits/incs:", dict(wc))
        per = {}
        for op in self.ops:
            per.setdefault(op.eng, []).append(op)

        def mk_runner(oplist):
            def run(eh):
                for op in oplist:
                    for (semk, val) in op.waits:
                        eh.wait_ge(self._sem(semk), val)
                    try:
                        ins = op.fn(eh)
                    except Exception as ex:
                        raise RuntimeError(f"op failed: {self.tag}/{op.eng}/{op.name} waits={op.waits}") from ex
                    if op.stream is not None:
                        ins.then_inc(self._sem("s:" + op.stream), 16)
                    elif op.inc:
                        ins.then_inc(self._sem("e:" + op.eng), 1)
            return run

        for eng, oplist in per.items():
            getattr(block, ENG2BLK[eng])(mk_runner(oplist))

    def drain(self, eng_for_stream):
        for stream, cnt in list(self.scnt.items()):
            eng = eng_for_stream[stream]
            sem = ("s:" + stream, 16 * cnt)
            op = _Op()
            op.eng = eng
            op.fn = lambda eh: eh.nop()
            op.waits = None
            fake = _Op()
            fake.eng = "?"
            fake.event = sem
            fake.clock = {}
            op.deps = [fake]
            self.ops.append(op)


_CACHED = {}

def _build():
    if "nc" in _CACHED:
        return _CACHED["nc"]
    PHASES = int(os.environ.get("KN_PHASES", "4"))
    nc = bass.Bass("TRN2", target_bir_lowering=False)

    dx = nc.dram_tensor("x", [B, T, D], F32, kind="ExternalInput")
    dwq = nc.dram_tensor("wq", [D, D], F32, kind="ExternalInput")
    dwk = nc.dram_tensor("wk", [D, KVD], F32, kind="ExternalInput")
    dwv = nc.dram_tensor("wv", [D, KVD], F32, kind="ExternalInput")
    dwo = nc.dram_tensor("wo", [D, D], F32, kind="ExternalInput")
    dbo = nc.dram_tensor("bo", [D], F32, kind="ExternalInput")
    dgw = nc.dram_tensor("gw", [NE, D], F32, kind="ExternalInput")
    deb = nc.dram_tensor("eb", [NE], F32, kind="ExternalInput")
    dw1 = nc.dram_tensor("w1", [NE, D, INTER], F32, kind="ExternalInput")
    dw2 = nc.dram_tensor("w2", [NE, INTER, D], F32, kind="ExternalInput")
    dw3 = nc.dram_tensor("w3", [NE, D, INTER], F32, kind="ExternalInput")
    dsw1 = nc.dram_tensor("sw1", [D, SINTER], F32, kind="ExternalInput")
    dsw2 = nc.dram_tensor("sw2", [SINTER, D], F32, kind="ExternalInput")
    dsw3 = nc.dram_tensor("sw3", [D, SINTER], F32, kind="ExternalInput")

    dout = nc.dram_tensor("out_my", [TOKC, D], F32, kind="ExternalOutput")
    doid = nc.dram_tensor("oid", [1, 1], I32, kind="ExternalOutput")
    DEBUG = int(os.environ.get("KN_DEBUG", "0"))
    if DEBUG:
        d_ctx0 = nc.dram_tensor("d_ctx0", [128, H * 512], F32, kind="ExternalOutput")
        d_l0 = nc.dram_tensor("d_l0", [1, 512], F32, kind="ExternalOutput")
        d_ksb0 = nc.dram_tensor("d_ksb0", [128, 16 * 128], F32, kind="ExternalOutput")
        d_vsb0 = nc.dram_tensor("d_vsb0", [128, 16 * 128], F32, kind="ExternalOutput")
        d_q0 = nc.dram_tensor("d_q0", [128, 512], F32, kind="ExternalOutput")
        d_p0 = nc.dram_tensor("d_p0", [128, 1024], F32, kind="ExternalOutput")
        d_x1 = nc.dram_tensor("d_x1", [128, NCH * TOKC], F32, kind="ExternalOutput")
        d_usb0 = nc.dram_tensor("d_usb0", [128, NCH * 512], F32, kind="ExternalOutput")
        d_vdix = nc.dram_tensor("d_vdix", [128, 64], I32, kind="ExternalOutput")
        d_kdix = nc.dram_tensor("d_kdix", [128, 64], I32, kind="ExternalOutput")

    kscr = nc.dram_tensor("kscr", [KVD, KSW], BF16)
    vscr = nc.dram_tensor("vscr", [KSW, KVD], BF16)

    top = ExitStack()
    _CACHED["top"] = top
    E_ = top.enter_context

    sched_i = E_(nc.sbuf_tensor("sched_i", [1, 8], I32))
    sched_f = E_(nc.sbuf_tensor("sched_f", [1, 16], F32))
    id_bf = E_(nc.sbuf_tensor("id_bf", [128, 128], BF16))
    id_f = E_(nc.sbuf_tensor("id_f", [128, 128], F32))
    ones_col_bf = E_(nc.sbuf_tensor("ones_col_bf", [128, 1], BF16))
    ones_row_bf = E_(nc.sbuf_tensor("ones_row_bf", [1, 128], BF16))
    ones_row_f = E_(nc.sbuf_tensor("ones_row_f", [1, 128], F32))
    c_scale = E_(nc.sbuf_tensor("c_scale", [128, 1], F32))
    c_shift = E_(nc.sbuf_tensor("c_shift", [128, 1], F32))
    c_eps = E_(nc.sbuf_tensor("c_eps", [128, 1], F32))
    c_nhalf = E_(nc.sbuf_tensor("c_nhalf", [128, 1], F32))
    bo_sb = E_(nc.sbuf_tensor("bo_sb", [128, NCH], F32))
    ebias_b = E_(nc.sbuf_tensor("ebias_b", [128, NE], F32))
    gw_fm = E_(nc.sbuf_tensor("gw_fm", [128, NCH * NE], BF16))
    oneh = E_(nc.sbuf_tensor("oneh", [16, NE * 128], BF16))
    kdix = E_(nc.sbuf_tensor("kdix", [128, 64], I32))
    vdix = E_(nc.sbuf_tensor("vdix", [128, 64], I32))
    mcol = E_(nc.sbuf_tensor("mcol", [128, 48], F32))
    X1 = E_(nc.sbuf_tensor("X1", [128, NCH * TOKC], BF16))

    ps_s1 = E_(nc.psum_tensor("ps_s1", [128, 1024], F32))
    ps_s2 = E_(nc.psum_tensor("ps_s2", [128, 1024], F32))
    ps_c1 = E_(nc.psum_tensor("ps_c1", [128, 512], F32))
    ps_c2 = E_(nc.psum_tensor("ps_c2", [128, 512], F32))
    ps_c3 = E_(nc.psum_tensor("ps_c3", [128, 512], F32))
    ps_tb = E_(nc.psum_tensor("ps_tb", [128, 1024], BF16))

    r_g1 = nc.gpsimd.alloc_register("r_g1")
    r_g2 = nc.gpsimd.alloc_register("r_g2")

    for arm in nc.switch_core_id([nc.vector], n=128):
        w = int(arm) % 8
        j, b = w % 4, w // 4
        nc.vector.memset(sched_i[0:1, 0:1], w)
        nc.vector.memset(sched_i[0:1, 1:2], b * T * D)
        for v in range(NV):
            nc.vector.memset(sched_i[0:1, 2 + v:3 + v], (b * T + q0tok(w, v)) * D)
        for v in range(NV):
            nc.vector.memset(sched_f[0:1, v:v + 1], float(q0tok(w, v)))


    with ExitStack() as ph:
        Ep = ph.enter_context
        bo_t = Ep(nc.sbuf_tensor("bo_t", [16, 128], F32))
        eb_t = Ep(nc.sbuf_tensor("eb_t", [1, NE], F32))
        gw_t = Ep(nc.sbuf_tensor("gw_t", [16, D], BF16))
        itf = Ep(nc.sbuf_tensor("itf", [1, 16], F32))
        m1f = Ep(nc.sbuf_tensor("m1f", [1, 16], F32))
        m2f = Ep(nc.sbuf_tensor("m2f", [1, 16], F32))
        iop = Ep(nc.sbuf_tensor("iop", [128, 1], F32))
        io65 = Ep(nc.sbuf_tensor("io65", [128, 1], F32))
        gb = Ep(nc.sbuf_tensor("gb", [128, 4], F32))
        q0b = Ep(nc.sbuf_tensor("q0b", [128, 4], F32))
        q0tb = Ep(nc.sbuf_tensor("q0tb", [128, 4], F32))
        q128r = Ep(nc.sbuf_tensor("q128r", [1, 4], F32))
        kdf = Ep(nc.sbuf_tensor("kdf", [128, 64], F32))
        vdf = Ep(nc.sbuf_tensor("vdf", [128, 64], F32))
        vtmp = Ep(nc.sbuf_tensor("vtmp", [128, 1], F32))
        em = Em(nc, top, "p0")

        em.dma("pool", "ld", lambda e: e.dma_start(bo_t[:, :], dbo.ap().rearrange("(c p) -> c p", p=128)), writes=["bo_t"])
        em.dma("pool", "ld", lambda e: e.dma_start(eb_t[:, :], deb.ap().unsqueeze(0)), writes=["eb_t"])
        em.dma("pool", "ld", lambda e: e.dma_start(gw_t[:, :], dgw.ap()), writes=["gw_t"])

        em.add("pool", lambda e: e.memset(id_bf[:, :], 0.0), writes=["id_bf"])
        em.add("pool", lambda e: e.affine_select(out=id_bf[:, :], in_=id_bf[:, :], compare_op=ALU.not_equal,
                                                 fill=1.0, base=0, pattern=[[-1, 128]], channel_multiplier=1),
               reads=["id_bf"], writes=["id_bf"])
        em.add("pool", lambda e: e.memset(id_f[:, :], 0.0), writes=["id_f"])
        em.add("pool", lambda e: e.affine_select(out=id_f[:, :], in_=id_f[:, :], compare_op=ALU.not_equal,
                                                 fill=1.0, base=0, pattern=[[-1, 128]], channel_multiplier=1),
               reads=["id_f"], writes=["id_f"])
        em.add("pool", lambda e: e.memset(ones_col_bf[:, :], 1.0), writes=["ones_col"])
        em.add("pool", lambda e: e.memset(ones_row_bf[:, :], 1.0), writes=["ones_row"])
        em.add("pool", lambda e: e.memset(ones_row_f[:, :], 1.0), writes=["ones_row_f"])
        em.add("pool", lambda e: e.memset(c_scale[:, :], SM_SCALE), writes=["c_scale"])
        em.add("pool", lambda e: e.memset(c_shift[:, :], EXP_SHIFT), writes=["c_shift"])
        em.add("pool", lambda e: e.memset(c_eps[:, :], EPS), writes=["c_eps"])
        em.add("pool", lambda e: e.memset(c_nhalf[:, :], -0.5), writes=["c_nhalf"])
        em.add("pool", lambda e: e.memset(oneh[:, :], 0.0), writes=["oneh"])
        em.add("pool", lambda e: e.affine_select(out=oneh[:, :], in_=oneh[:, :], compare_op=ALU.not_equal,
                                                 fill=1.0, base=0, pattern=[[-1, NE], [0, 128]], channel_multiplier=1),
               reads=["oneh"], writes=["oneh"])

        em.add("pe", lambda e: e.transpose(ps_c1[:, 0:16], bo_t[:, :], id_f[0:16, 0:16]),
               reads=["bo_t", "id_f"], writes=["c1"])
        em.add("dve", lambda e: e.tensor_copy(bo_sb[:, :], ps_c1[:, 0:16]), reads=["c1"], writes=["bo_sb"])
        em.add("pe", lambda e: e.matmul(ps_c2[:, 0:NE], ones_row_f[:, :], eb_t[:, :],
                                        start=True, stop=True),
               reads=["ones_row_f", "eb_t"], writes=["c2"])
        em.add("dve", lambda e: e.tensor_copy(ebias_b[:, :], ps_c2[:, 0:NE]), reads=["c2"], writes=["ebias_b"])
        for c in range(NCH):
            em.add("pe", (lambda c: lambda e: e.transpose(ps_tb[:, c * 16:(c + 1) * 16],
                                                          gw_t[:, c * 128:(c + 1) * 128],
                                                          id_bf[0:16, 0:16]))(c),
                   reads=["gw_t", "id_bf"], writes=["tb"])
        em.add("dve", lambda e: e.tensor_copy(gw_fm[:, 0:NCH * NE], ps_tb[:, 0:NCH * NE]),
               reads=["tb"], writes=["gw_fm"])

        em.add("pool", lambda e: e.iota(itf[0:1, 0:12], [[1, 12]], base=0, channel_multiplier=0,
                                        allow_small_or_imprecise_dtypes=True), writes=["i12"])
        em.add("pool", lambda e: e.iota(iop[:, :], [[0, 1]], base=0, channel_multiplier=1,
                                        allow_small_or_imprecise_dtypes=True), writes=["iop"])
        em.add("pool", lambda e: e.iota(io65[:, :], [[0, 1]], base=0, channel_multiplier=65,
                                        allow_small_or_imprecise_dtypes=True), writes=["io65"])
        em.add("dve", lambda e: e.tensor_scalar(q128r[0:1, 0:4], sched_f[0:1, 0:4], 1.0 / 128.0, None, ALU.mult),
               reads=["sched"], writes=["q128r"])
        em.add("pe", lambda e: e.matmul(ps_c3[:, 0:4], ones_row_f[:, :], sched_f[0:1, 0:4], start=True, stop=True),
               reads=["ones_row_f", "sched"], writes=["c3"])
        em.add("dve", lambda e: e.tensor_copy(q0b[:, 0:4], ps_c3[:, 0:4]), reads=["c3"], writes=["q0b"])
        em.add("pe", lambda e: e.matmul(ps_c3[:, 4:8], ones_row_f[:, :], q128r[0:1, 0:4], start=True, stop=True),
               reads=["ones_row_f", "q128r"], writes=["c3"])
        em.add("dve", lambda e: e.tensor_copy(q0tb[:, 0:4], ps_c3[:, 4:8]), reads=["c3"], writes=["q0tb"])
        for g in range(G):
            em.add("dve", (lambda g: lambda e: e.tensor_scalar(
                gb[:, g:g + 1], io65[:, :], float(g * 128 * 65), None, ALU.add))(g),
                reads=["io65"], writes=["gb"])
        for v in range(NV):
            for g in range(G):
                for p in range(4):
                    col = (v * 4 + g) * 4 + p
                    em.add("dve", (lambda col, g, v, p: lambda e: e.tensor_scalar(
                        kdf[:, col:col + 1], gb[:, g:g + 1], q0tb[:, v:v + 1], float(p), ALU.add, ALU.add))(col, g, v, p),
                        reads=["gb", "q0tb"], writes=["kdf"])
        em.add("dve", lambda e: e.tensor_copy(kdix[:, :], kdf[:, :]), reads=["kdf"], writes=["kdix"])
        for v in range(NV):
            for g in range(G):
                for p in range(4):
                    col = (v * 4 + g) * 4 + p
                    em.add("dve", (lambda col, v, p: lambda e: e.tensor_scalar(
                        vdf[:, col:col + 1], iop[:, :], q0b[:, v:v + 1], float(128 * p), ALU.add, ALU.add))(col, v, p),
                        reads=["iop", "q0b"], writes=["vdf"])
                    em.add("dve", (lambda col: lambda e: e.tensor_scalar(
                        vdf[:, col:col + 1], vdf[:, col:col + 1], 4.0, None, ALU.mult))(col),
                        reads=["vdf"], writes=["vdf"])
                    em.add("dve", (lambda col, g: lambda e: e.tensor_scalar(
                        vdf[:, col:col + 1], vdf[:, col:col + 1], float(g), None, ALU.add))(col, g),
                        reads=["vdf"], writes=["vdf"])
        em.add("dve", lambda e: e.tensor_copy(vdix[:, :], vdf[:, :]), reads=["vdf"], writes=["vdix"])
        for v in range(NV):
            n = NPAIR[v]
            em.add("dve", (lambda v, n: lambda e: e.tensor_scalar(
                m1f[0:1, 0:12], itf[0:1, 0:12], float(n - 16), None, ALU.add))(v, n),
                reads=["i12"], writes=["m1f"])
            em.add("dve", (lambda v: lambda e: e.tensor_scalar(
                m1f[0:1, 0:12], m1f[0:1, 0:12], q128r[0:1, v:v + 1], None, ALU.is_lt))(v),
                reads=["m1f", "q128r"], writes=["m1f"])
            em.add("dve", lambda e: e.tensor_copy(m2f[0:1, 0:12], m1f[0:1, 0:12]),
                   reads=["m1f"], writes=["m2f"])
            em.add("pe", (lambda v: lambda e: e.matmul(
                ps_c3[:, 0:12], ones_row_f[:, :], m2f[0:1, 0:12], start=True, stop=True))(v),
                reads=["ones_row_f", "m2f"], writes=["c3"])
            em.add("dve", (lambda v: lambda e: e.tensor_copy(mcol[:, v * 12:(v + 1) * 12], ps_c3[:, 0:12]))(v),
                   reads=["c3"], writes=["mcol"])

        if DEBUG:
            em.dma("pool", "dbg0", lambda e: e.dma_start(d_vdix.ap(), vdix[:, :]), reads=["vdix"], name="dbgvd")
            em.dma("pool", "dbg0", lambda e: e.dma_start(d_kdix.ap(), kdix[:, :]), reads=["kdix"], name="dbgkd")
        em.drain({"ld": "pool", "dbg0": "pool"})
        with nc.Block() as blk:
            em.emit(blk)

    if PHASES >= 2:
        with ExitStack() as ph:
            Ep = ph.enter_context
            wk_sb = Ep(nc.sbuf_tensor("wk_sb", [128, NCH * KVD], BF16))
            wv_sb = Ep(nc.sbuf_tensor("wv_sb", [128, NCH * KVD], BF16))
            x_tm = [Ep(nc.sbuf_tensor(f"x_tm{i}", [128, 8 * D], BF16)) for i in range(2)]
            xfm = [Ep(nc.sbuf_tensor(f"xfm{i}", [128, NCH * 512], BF16)) for i in range(2)]
            kst = [Ep(nc.sbuf_tensor(f"kst{i}", [128, 512], BF16)) for i in range(2)]
            vst = [Ep(nc.sbuf_tensor(f"vst{i}", [128, 512], BF16)) for i in range(2)]
            em = Em(nc, top, "a1")

            em.dma("pool", "wkv", lambda e: e.dma_start(
                wk_sb[:, :].rearrange("p (c n) -> p c n", c=NCH),
                dwk.ap().rearrange("(c p) n -> p c n", p=128)), writes=["wk"])
            em.dma("pool", "wkv", lambda e: e.dma_start(
                wv_sb[:, :].rearrange("p (c n) -> p c n", c=NCH),
                dwv.ap().rearrange("(c p) n -> p c n", p=128)), writes=["wv"])

            em.add("pool", lambda e: e.reg_load(r_g1, sched_i[0:1, 1:2]), reads=["sched"], name="ldxb")

            for st0 in range(16):
                pb = st0 % 2
                if st0 % 2 == 0:
                    ld = st0 // 2
                    lb = ld % 2
                    def xload(e, ld=ld, lb=lb):
                        e.reg_mov(r_g2, ld * 1024 * D)
                        e.reg_add(r_g2, r_g2, r_g1)
                        return e.dma_start(x_tm[lb][:, :].rearrange("p (t c) -> p t c", t=8),
                                           bass.AP(dx, r_g2, [[D, 128], [128 * D, 8], [1, D]]))
                    em.dma("pool", "xin", xload, reads=[], writes=[f"x8_{lb}"], name=f"xl{ld}")
                lb = (st0 // 2) % 2
                for ti in range(4):
                    toff = (st0 % 2) * 4 + ti
                    for half in range(2):
                        for cc in range(8):
                            c = half * 8 + cc
                            em.add("pe", (lambda lb, c, cc, toff: lambda e: e.transpose(
                                ps_tb[:, cc * 128:(cc + 1) * 128],
                                x_tm[lb][:, toff * D + c * 128: toff * D + (c + 1) * 128], id_bf[:, :]))(lb, c, cc, toff),
                                reads=[f"x8_{lb}", "id_bf"], writes=["tb"], name=f"xt{st0}_{ti}")
                        em.add("dve", (lambda pb, ti, half: lambda e: e.tensor_copy(
                            xfm[pb][:, :].rearrange("p (c t) -> p c t", c=NCH)[:, half * 8:(half + 1) * 8, ti * 128:(ti + 1) * 128],
                            ps_tb[:, :].rearrange("p (c t) -> p c t", c=8)))(pb, ti, half),
                            reads=["tb"], writes=[f"xfm{pb}:{ti}"], name=f"xc{st0}_{ti}")
                for g in range(G):
                    kb = g % 2
                    for c in range(NCH):
                        em.add("pe", (lambda pb, g, c, kb: lambda e: e.matmul(
                            (ps_c1 if kb == 0 else ps_c2)[:, :],
                            wk_sb[:, c * KVD + g * 128: c * KVD + (g + 1) * 128],
                            xfm[pb][:, c * 512:(c + 1) * 512],
                            start=(c == 0), stop=(c == NCH - 1)))(pb, g, c, kb),
                            reads=["wk", f"xfm{pb}:0", f"xfm{pb}:1", f"xfm{pb}:2", f"xfm{pb}:3"],
                            writes=["c1" if kb == 0 else "c2"], name=f"kmm{st0}g{g}")
                    em.add("dve", (lambda kb, g: lambda e: e.tensor_copy(kst[kb][:, :], (ps_c1 if kb == 0 else ps_c2)[:, :]))(kb, g),
                           reads=["c1" if kb == 0 else "c2"], writes=[f"kst{kb}"], name=f"kev{st0}g{g}")
                    em.dma("sp", "kout", (lambda kb, g, st0: lambda e: e.dma_start(
                        bass.AP(kscr, g * 128 * KSW + st0 * 512, [[KSW, 128], [1, 512]]), kst[kb][:, :]))(kb, g, st0),
                        reads=[f"kst{kb}"], name=f"kw{st0}g{g}")
                for ti in range(4):
                    vb = ti % 2
                    for c in range(NCH):
                        em.add("pe", (lambda pb, ti, c, vb: lambda e: e.matmul(
                            (ps_s1 if vb == 0 else ps_s2)[:, 0:512],
                            xfm[pb][:, c * 512 + ti * 128: c * 512 + (ti + 1) * 128],
                            wv_sb[:, c * KVD:(c + 1) * KVD],
                            start=(c == 0), stop=(c == NCH - 1)))(pb, ti, c, vb),
                            reads=["wv", f"xfm{pb}:{ti}"], writes=["s1" if vb == 0 else "s2"], name=f"vmm{st0}t{ti}")
                    em.add("dve", (lambda vb: lambda e: e.tensor_copy(vst[vb][:, :], (ps_s1 if vb == 0 else ps_s2)[:, 0:512]))(vb),
                           reads=["s1" if vb == 0 else "s2"], writes=[f"vst{vb}"], name=f"vev{st0}t{ti}")
                    em.dma("act", "vout", (lambda vb, st0, ti: lambda e: e.dma_start(
                        bass.AP(vscr, (st0 * 512 + ti * 128) * KVD, [[KVD, 128], [1, KVD]]), vst[vb][:, :]))(vb, st0, ti),
                        reads=[f"vst{vb}"], name=f"vw{st0}t{ti}")

            em.drain({"wkv": "pool", "xin": "pool", "kout": "sp", "vout": "act"})
            with nc.Block() as blk:
                em.emit(blk)


    if PHASES >= 3:
        with ExitStack() as ph:
            Ep = ph.enter_context
            x_tm2 = Ep(nc.sbuf_tensor("x_tm2", [128, 4 * D], BF16))
            xfq = Ep(nc.sbuf_tensor("xfq", [128, NCH * 512], BF16))
            wcol = [Ep(nc.sbuf_tensor(f"wcol{i}", [128, NCH * 128], BF16)) for i in range(2)]
            qh = [Ep(nc.sbuf_tensor(f"qh{i}", [128, 512], BF16)) for i in range(2)]
            ksb = Ep(nc.sbuf_tensor("ksb", [128, 64 * 128], BF16))
            vsb = Ep(nc.sbuf_tensor("vsb", [128, 64 * 128], BF16))
            pbuf = [Ep(nc.sbuf_tensor(f"pbuf{i}", [128, 1024], BF16)) for i in range(2)]
            ctx_sb = Ep(nc.sbuf_tensor("ctx_sb", [128, H * 512], BF16))
            usb = Ep(nc.sbuf_tensor("usb", [128, NCH * 512], BF16))
            tmpf = Ep(nc.sbuf_tensor("tmpf", [128, 512], F32))
            tmpb = Ep(nc.sbuf_tensor("tmpb", [128, 512], BF16))
            lrow = Ep(nc.sbuf_tensor("lrow", [1, 512], F32))
            lrow_bf = Ep(nc.sbuf_tensor("lrow_bf", [1, 512], BF16))
            mrow = Ep(nc.sbuf_tensor("mrow", [1, 512], F32))
            mrow_bf = Ep(nc.sbuf_tensor("mrow_bf", [1, 512], BF16))
            rrow = Ep(nc.sbuf_tensor("rrow", [1, 512], F32))
            rrow_bf = Ep(nc.sbuf_tensor("rrow_bf", [1, 512], BF16))
            vrow = Ep(nc.sbuf_tensor("vrow", [1, 512], F32))
            em = Em(nc, top, "a2")

            for v in range(NV):
                n = NPAIR[v]
                em.add("pool", (lambda v: lambda e: e.reg_load(r_g1, sched_i[0:1, 2 + v:3 + v]))(v),
                       reads=["sched"], name=f"ldxmy{v}")
                def xload2(e):
                    return e.dma_start(x_tm2[:, :].rearrange("p (t c) -> p t c", t=4),
                                       bass.AP(dx, r_g1, [[D, 128], [128 * D, 4], [1, D]]))
                em.dma("pool", "xq", xload2, writes=["x2"], name=f"xq{v}")
                for i in range(4):
                    for half in range(2):
                        for cc in range(8):
                            c = half * 8 + cc
                            em.add("pe", (lambda c, cc, i: lambda e: e.transpose(
                                ps_tb[:, cc * 128:(cc + 1) * 128],
                                x_tm2[:, i * D + c * 128: i * D + (c + 1) * 128], id_bf[:, :]))(c, cc, i),
                                reads=["x2", "id_bf"], writes=["tb"], name=f"qt{v}_{i}")
                        em.add("dve", (lambda i, half: lambda e: e.tensor_copy(
                            xfq[:, :].rearrange("p (c t) -> p c t", c=NCH)[:, half * 8:(half + 1) * 8, i * 128:(i + 1) * 128],
                            ps_tb[:, :].rearrange("p (c t) -> p c t", c=8)))(i, half),
                            reads=["tb"], writes=["xfq"], name=f"qc{v}_{i}")

                ctx_regions = []
                for g in range(G):
                    for p in range(4):
                        kcol = (v * 4 + g) * 4 + p
                        vcol = (v * 4 + g) * 4 + p
                        em.dma("pool", "kvd", (lambda p, kcol: lambda e: e.indirect_dma_start(
                            ksb[:, p * 128:(p + 1) * 128], None,
                            bass.AP(kscr, 0, [[128, KVD * (KSW // 128)], [1, 128]]),
                            IndirectOffsetOnAxis(kdix[:, kcol:kcol + 1], 0)))(p, kcol),
                            reads=["kdix"], writes=[f"ks:{p // 16}"], name=f"kd{v}g{g}p{p}")
                        em.dma("pool", "kvd", (lambda p, vcol: lambda e: e.indirect_dma_start(
                            vsb[:, p * 128:(p + 1) * 128], None,
                            bass.AP(vscr, 0, [[128, KSW * 4], [1, 128]]),
                            IndirectOffsetOnAxis(vdix[:, vcol:vcol + 1], 0)))(p, vcol),
                            reads=["vdix"], writes=[f"vs:{p // 16}"], name=f"vd{v}g{g}p{p}")
                    if DEBUG and v == 0 and g == 0:
                        em.dma("pool", "dbg", lambda e: e.dma_start(d_ksb0.ap(), ksb[:, 0:16 * 128]),
                               reads=["ks:0"], name="dbgk")
                        em.dma("pool", "dbg", lambda e: e.dma_start(d_vsb0.ap(), vsb[:, 0:16 * 128]),
                               reads=["vs:0"], name="dbgv")
                    nq4 = (n - 4) // 4
                    for qq in range(4):
                        s0 = 4 + qq * nq4
                        em.dma("sp", "kv", (lambda g, qq, s0, nq4: lambda e: e.dma_start(
                            ksb[:, s0 * 128:(s0 + nq4) * 128],
                            bass.AP(kscr, g * 128 * KSW + qq * nq4 * 128, [[KSW, 128], [1, nq4 * 128]])))(g, qq, s0, nq4),
                            writes=sorted({f"ks:{sl // 16}" for sl in range(s0, s0 + nq4)}), name=f"kb{v}g{g}q{qq}")
                        em.dma("act", "kv2", (lambda g, qq, s0, nq4: lambda e: e.dma_start(
                            vsb[:, s0 * 128:(s0 + nq4) * 128].rearrange("p (s c) -> p s c", s=nq4),
                            bass.AP(vscr, (qq * nq4 * 128) * KVD + g * 128, [[KVD, 128], [128 * KVD, nq4], [1, 128]])))(g, qq, s0, nq4),
                            writes=sorted({f"vs:{sl // 16}" for sl in range(s0, s0 + nq4)}), name=f"vb{v}g{g}q{qq}")

                    for hh in range(4):
                        h = g * 4 + hh
                        qb_ = h % 2
                        em.dma("pool", "wq", (lambda h, qb_: lambda e: e.dma_start(
                            wcol[qb_][:, :].rearrange("p (c m) -> p c m", c=NCH),
                            bass.AP(dwq, h * 128, [[D, 128], [128 * D, NCH], [1, 128]])))(h, qb_),
                            writes=[f"wcol{qb_}"], name=f"wq{h}")
                        for c in range(NCH):
                            em.add("pe", (lambda c, qb_: lambda e: e.matmul(
                                ps_c3[:, :], wcol[qb_][:, c * 128:(c + 1) * 128], xfq[:, c * 512:(c + 1) * 512],
                                start=(c == 0), stop=(c == NCH - 1)))(c, qb_),
                                reads=[f"wcol{qb_}", "xfq"], writes=["c3"], name=f"qmm{h}")
                        em.add("dve", (lambda qb_: lambda e: e.tensor_copy(qh[qb_][:, :], ps_c3[:, :]))(qb_),
                               reads=["c3"], writes=[f"qh{qb_}"], name=f"qev{h}")

                        M = n // 2

                        def emit_s_exp(m):
                            sb_ps = ps_s1 if (m % 2 == 0) else ps_s2
                            sregs = ["s1a", "s1b"] if (m % 2 == 0) else ["s2a", "s2b"]
                            pb_ = m % 2
                            for u in range(2):
                                p = 2 * m + u
                                em.add("pe", (lambda p, u, sb_ps, qb_: lambda e: e.matmul(
                                    sb_ps[:, u * 512:(u + 1) * 512],
                                    ksb[:, p * 128:(p + 1) * 128], qh[qb_][:, :],
                                    start=True, stop=True))(p, u, sb_ps, qb_),
                                    reads=[f"ks:{p // 16}", f"qh{qb_}"], writes=[sregs[u]], name=f"smm{h}p{p}")
                            em.add("act", (lambda sb_ps, pb_: lambda e: e.activation(
                                pbuf[pb_][:, :], sb_ps[:, :], AF.Exp, bias=c_shift[:, :], scale=c_scale[:, :]))(sb_ps, pb_),
                                reads=sregs + ["c_shift", "c_scale"], writes=[f"pb{pb_}"], name=f"exp{h}m{m}")
                            for u in range(2):
                                p = 2 * m + u
                                if p < 4:
                                    em.add("pool", (lambda p, u, pb_: lambda e: e.affine_select(
                                        out=pbuf[pb_][:, u * 512:(u + 1) * 512], in_=pbuf[pb_][:, u * 512:(u + 1) * 512],
                                        compare_op=ALU.is_ge, fill=0.0, base=-128 * p,
                                        pattern=[[1, 512]], channel_multiplier=-1))(p, u, pb_),
                                        reads=[f"pb{pb_}"], writes=[f"pb{pb_}"], name=f"diag{h}p{p}")
                                if p >= n - 12:
                                    mc = v * 12 + (p - (n - 12))
                                    em.add("dve", (lambda p, u, pb_, mc: lambda e: e.tensor_scalar(
                                        pbuf[pb_][:, u * 512:(u + 1) * 512], pbuf[pb_][:, u * 512:(u + 1) * 512],
                                        mcol[:, mc:mc + 1], None, ALU.mult))(p, u, pb_, mc),
                                        reads=[f"pb{pb_}", "mcol"], writes=[f"pb{pb_}"], name=f"gmask{h}p{p}")

                        def emit_pv(m):
                            pb_ = m % 2
                            for u in range(2):
                                p = 2 * m + u
                                em.add("pe", (lambda p, u, pb_: lambda e: e.matmul(
                                    ps_c1[:, :], vsb[:, p * 128:(p + 1) * 128], pbuf[pb_][:, u * 512:(u + 1) * 512],
                                    start=(p == 0), stop=(p == n - 1)))(p, u, pb_),
                                    reads=[f"vs:{p // 16}", f"pb{pb_}"], writes=["c1"], name=f"pv{h}p{p}")
                                em.add("pe", (lambda p, u, pb_: lambda e: e.matmul(
                                    ps_c2[0:1, :], ones_col_bf[:, :], pbuf[pb_][:, u * 512:(u + 1) * 512],
                                    start=(p == 0), stop=(p == n - 1)))(p, u, pb_),
                                    reads=["ones_col", f"pb{pb_}"], writes=["c2"], name=f"lmm{h}p{p}")

                        for m in range(M + 1):
                            if m < M:
                                emit_s_exp(m)
                            if m >= 1:
                                emit_pv(m - 1)

                        if DEBUG and v == 0 and h == 0:
                            em.dma("pool", "dbg", (lambda qb_: lambda e: e.dma_start(d_q0.ap(), qh[qb_][:, :]))(qb_),
                                   reads=[f"qh{qb_}"], name="dbgq")
                            em.dma("pool", "dbg", lambda e: e.dma_start(d_p0.ap(), pbuf[0][:, :]),
                                   reads=["pb0"], name="dbgp")
                        em.add("dve", lambda e: e.reciprocal(lrow[0:1, :], ps_c2[0:1, :]),
                               reads=["c2"], writes=["lrow"], name=f"li{h}")
                        em.add("dve", lambda e: e.tensor_copy(lrow_bf[0:1, :], lrow[0:1, :]),
                               reads=["lrow"], writes=["lrow_bf"], name=f"lb{h}")
                        em.add("pe", lambda e: e.matmul(ps_c3[:, :], ones_row_bf[:, :], lrow_bf[0:1, :], start=True, stop=True),
                               reads=["ones_row", "lrow_bf"], writes=["c3"], name=f"lbc{h}")
                        em.add("dve", lambda e: e.tensor_copy(tmpf[:, :], ps_c1[:, :]),
                               reads=["c1"], writes=["tmpf"], name=f"cev{h}")
                        em.add("dve", (lambda h: lambda e: e.tensor_tensor(
                            ctx_sb[:, h * 512:(h + 1) * 512], tmpf[:, :], ps_c3[:, :], ALU.mult))(h),
                            reads=["tmpf", "c3"], writes=["ctx"], name=f"cscl{h}")

                if DEBUG and v == 0:
                    em.dma("pool", "dbg", lambda e: e.dma_start(d_l0.ap(), lrow[0:1, :]),
                           reads=["lrow"], name="dbgl")
                    em.dma("pool", "dbg", lambda e: e.dma_start(d_ctx0.ap(), ctx_sb[:, :]),
                           reads=["ctx"], name="dbgctx")
                for dch in range(NCH):
                    wb = dch % 2
                    em.dma("pool", "wo", (lambda dch, wb: lambda e: e.dma_start(
                        wcol[wb][:, :].rearrange("p (c m) -> p c m", c=NCH),
                        bass.AP(dwo, dch * 128, [[D, 128], [128 * D, NCH], [1, 128]])))(dch, wb),
                        writes=[f"wcol{wb}"], name=f"wo{v}_{dch}")
                    for hc in range(NCH):
                        em.add("pe", (lambda dch, hc, wb: lambda e: e.matmul(
                            ps_s1[:, 0:512], wcol[wb][:, hc * 128:(hc + 1) * 128], ctx_sb[:, hc * 512:(hc + 1) * 512],
                            start=(hc == 0), stop=(hc == NCH - 1)))(dch, hc, wb),
                            reads=[f"wcol{wb}", "ctx"], writes=["s1a"], name=f"womm{v}_{dch}")
                    em.add("dve", (lambda dch: lambda e: e.scalar_tensor_tensor(
                        usb[:, dch * 512:(dch + 1) * 512], ps_s1[:, 0:512], bo_sb[:, dch:dch + 1],
                        xfq[:, dch * 512:(dch + 1) * 512], ALU.add, ALU.add))(dch),
                        reads=["s1a", "bo_sb", "xfq"], writes=["usb"], name=f"ures{v}_{dch}")

                for c in range(NCH):
                    em.add("pe", (lambda c: lambda e: e.matmul(
                        ps_s1[0:1, 0:512], ones_col_bf[:, :], usb[:, c * 512:(c + 1) * 512],
                        start=(c == 0), stop=(c == NCH - 1)))(c),
                        reads=["ones_col", "usb"], writes=["s1a"], name=f"lns{v}")
                for c in range(NCH):
                    em.add("dve", (lambda c: lambda e: e.tensor_mul(
                        tmpb[:, :], usb[:, c * 512:(c + 1) * 512], usb[:, c * 512:(c + 1) * 512]))(c),
                        reads=["usb"], writes=["tmpb"], name=f"sq{v}_{c}")
                    em.add("pe", (lambda c: lambda e: e.matmul(
                        ps_s2[0:1, 0:512], ones_col_bf[:, :], tmpb[:, :],
                        start=(c == 0), stop=(c == NCH - 1)))(c),
                        reads=["ones_col", "tmpb"], writes=["s2a"], name=f"sqs{v}")
                em.add("dve", lambda e: e.tensor_scalar(mrow[0:1, :], ps_s1[0:1, 0:512], 1.0 / D, None, ALU.mult),
                       reads=["s1a"], writes=["mrow"], name=f"mean{v}")
                em.add("dve", lambda e: e.tensor_scalar(vrow[0:1, :], ps_s2[0:1, 0:512], 1.0 / D, None, ALU.mult),
                       reads=["s2a"], writes=["vrow"], name=f"msq{v}")
                em.add("dve", lambda e: e.tensor_mul(lrow[0:1, :], mrow[0:1, :], mrow[0:1, :]),
                       reads=["mrow"], writes=["lrow"], name=f"m2{v}")
                em.add("dve", lambda e: e.tensor_sub(vrow[0:1, :], vrow[0:1, :], lrow[0:1, :]),
                       reads=["vrow", "lrow"], writes=["vrow"], name=f"var{v}")
                em.add("act", lambda e: e.activation(vrow[0:1, :], vrow[0:1, :], AF.Ln, bias=c_eps[0:1, :]),
                       reads=["vrow", "c_eps"], writes=["vrow"], name=f"lnv{v}")
                em.add("act", lambda e: e.activation(rrow[0:1, :], vrow[0:1, :], AF.Exp, scale=c_nhalf[0:1, :]),
                       reads=["vrow", "c_nhalf"], writes=["rrow"], name=f"rstd{v}")
                em.add("dve", lambda e: e.tensor_copy(mrow_bf[0:1, :], mrow[0:1, :]),
                       reads=["mrow"], writes=["mrow_bf"], name=f"mbf{v}")
                em.add("dve", lambda e: e.tensor_copy(rrow_bf[0:1, :], rrow[0:1, :]),
                       reads=["rrow"], writes=["rrow_bf"], name=f"rbf{v}")
                em.add("pe", lambda e: e.matmul(ps_c3[:, :], ones_row_bf[:, :], mrow_bf[0:1, :], start=True, stop=True),
                       reads=["ones_row", "mrow_bf"], writes=["c3"], name=f"mb{v}")
                em.add("pe", lambda e: e.matmul(ps_c2[:, :], ones_row_bf[:, :], rrow_bf[0:1, :], start=True, stop=True),
                       reads=["ones_row", "rrow_bf"], writes=["c2"], name=f"rb{v}")
                if DEBUG and v == 0:
                    em.dma("pool", "dbg", lambda e: e.dma_start(d_usb0.ap(), usb[:, :]),
                           reads=["usb"], name="dbgu")
                for c in range(NCH):
                    em.add("dve", (lambda c: lambda e: e.tensor_sub(
                        tmpb[:, :], usb[:, c * 512:(c + 1) * 512], ps_c3[:, :]))(c),
                        reads=["usb", "c3"], writes=["tmpb"], name=f"ap1{v}_{c}")
                    em.add("dve", (lambda c, v: lambda e: e.tensor_tensor(
                        X1[:, c * TOKC + v * 512: c * TOKC + (v + 1) * 512], tmpb[:, :], ps_c2[:, :], ALU.mult))(c, v),
                        reads=["tmpb", "c2"], writes=[f"X1:{v}"], name=f"ap2{v}_{c}")

            if DEBUG:
                em.dma("pool", "dbg", lambda e: e.dma_start(d_x1.ap(), X1[:, :]),
                       reads=["X1:0", "X1:1", "X1:2", "X1:3"], name="dbgx1")
            em.drain({"xq": "pool", "wq": "pool", "wo": "pool", "kv": "sp", "kv2": "act", "kvd": "pool", "dbg": "pool"})
            with nc.Block() as blk:
                em.emit(blk)


    if PHASES >= 4:
        with ExitStack() as ph:
            Ep = ph.enter_context
            Y = Ep(nc.sbuf_tensor("Y", [128, NCH * TAU], F32))
            Hb = Ep(nc.sbuf_tensor("Hb", [128, ICH * TAU], BF16))
            w1c = [Ep(nc.sbuf_tensor(f"w1c{i}", [128, NCH * 128], BF16)) for i in range(2)]
            w3c = [Ep(nc.sbuf_tensor(f"w3c{i}", [128, NCH * 128], BF16)) for i in range(2)]
            w2c = [Ep(nc.sbuf_tensor(f"w2c{i}", [128, ICH * 128], BF16)) for i in range(2)]
            gsc = Ep(nc.sbuf_tensor("gsc", [128, NE], F32))
            gbs = Ep(nc.sbuf_tensor("gbs", [128, NE], F32))
            grp = Ep(nc.sbuf_tensor("grp", [128, 4], F32))
            grp2 = Ep(nc.sbuf_tensor("grp2", [128, 4], F32))
            msc = Ep(nc.sbuf_tensor("msc", [128, NE], F32))
            cur = Ep(nc.sbuf_tensor("cur", [128, NE], F32))
            indt = Ep(nc.sbuf_tensor("indt", [128, NE], F32))
            sc1 = Ep(nc.sbuf_tensor("sc1", [128, 1], F32))
            sc2 = Ep(nc.sbuf_tensor("sc2", [128, 1], F32))
            cw = Ep(nc.sbuf_tensor("cw", [128, 8 * NE], F32))
            cwb16 = Ep(nc.sbuf_tensor("cwb16", [128, NE], BF16))
            cwT = Ep(nc.sbuf_tensor("cwT", [16, 1024], BF16))
            cwB = [Ep(nc.sbuf_tensor(f"cwB{i}", [128, 512], F32)) for i in range(2)]
            silu_t = Ep(nc.sbuf_tensor("silu_t", [128, 512], F32))
            btmp = Ep(nc.sbuf_tensor("btmp", [128, 512], F32))
            outm = Ep(nc.sbuf_tensor("outm", [128, D], F32))
            sqm = Ep(nc.sbuf_tensor("sqm", [128, D], F32))
            srow = Ep(nc.sbuf_tensor("srow", [128, 1], F32))
            qrow = Ep(nc.sbuf_tensor("qrow", [128, 1], F32))
            vr2 = Ep(nc.sbuf_tensor("vr2", [128, 1], F32))
            em = Em(nc, top, "bb")

            for tau in range(2):
                t0 = tau * TAU
                for sub in range(8):
                    tok0 = t0 + sub * 128
                    for c in range(NCH):
                        em.add("pe", (lambda c, tok0: lambda e: e.matmul(
                            ps_c2[:, 0:NE], X1[:, c * TOKC + tok0: c * TOKC + tok0 + 128],
                            gw_fm[:, c * NE:(c + 1) * NE],
                            start=(c == 0), stop=(c == NCH - 1)))(c, tok0),
                            reads=[f"X1:{2 * tau}", f"X1:{2 * tau + 1}"], writes=["c2"], name=f"gmm{tau}_{sub}")
                    em.add("act", lambda e: e.activation(gsc[:, :], ps_c2[:, 0:NE], AF.Sigmoid),
                           reads=["c2"], writes=["gsc"], name=f"sig{tau}_{sub}")
                    em.add("dve", lambda e: e.tensor_add(gbs[:, :], gsc[:, :], ebias_b[:, :]),
                           reads=["gsc", "ebias_b"], writes=["gbs"], name=f"gb{tau}_{sub}")
                    for gg in range(4):
                        em.add("dve", (lambda gg: lambda e: e.reduce_max(
                            grp[:, gg:gg + 1], gbs[:, 4 * gg:4 * (gg + 1)], axis=AX.X))(gg),
                            reads=["gbs"], writes=["grp"], name=f"gmax{tau}_{sub}_{gg}")
                    em.add("dve", lambda e: e.reduce_max(sc1[:, :], grp[:, :], axis=AX.X),
                           reads=["grp"], writes=["sc1"], name=f"m1_{tau}_{sub}")
                    em.add("dve", lambda e: e.tensor_scalar(grp2[:, :], grp[:, :], sc1[:, 0:1], None, ALU.is_equal),
                           reads=["grp", "sc1"], writes=["grp2"], name=f"e1_{tau}_{sub}")
                    em.add("dve", lambda e: e.scalar_tensor_tensor(grp2[:, :], grp2[:, :], -BIG, grp[:, :], ALU.mult, ALU.add),
                           reads=["grp2", "grp"], writes=["grp2"], name=f"g2_{tau}_{sub}")
                    em.add("dve", lambda e: e.reduce_max(sc2[:, :], grp2[:, :], axis=AX.X),
                           reads=["grp2"], writes=["sc2"], name=f"m2_{tau}_{sub}")
                    for gg in range(4):
                        em.add("dve", (lambda gg: lambda e: e.tensor_scalar(
                            sc1[:, :], grp[:, gg:gg + 1], sc2[:, 0:1], None, ALU.is_ge))(gg),
                            reads=["grp", "sc2"], writes=["sc1"], name=f"gi{tau}_{sub}_{gg}")
                        em.add("dve", (lambda gg: lambda e: e.tensor_scalar(
                            sc1[:, :], sc1[:, :], BIG, None, ALU.mult))(gg),
                            reads=["sc1"], writes=["sc1"], name=f"gp{tau}_{sub}_{gg}")
                        em.add("dve", (lambda gg: lambda e: e.tensor_scalar(
                            sc1[:, :], sc1[:, :], -BIG, None, ALU.add))(gg),
                            reads=["sc1"], writes=["sc1"], name=f"gp2{tau}_{sub}_{gg}")
                        em.add("dve", (lambda gg: lambda e: e.tensor_scalar(
                            msc[:, 4 * gg:4 * (gg + 1)], gbs[:, 4 * gg:4 * (gg + 1)], sc1[:, 0:1], None, ALU.add))(gg),
                            reads=["gbs", "sc1"], writes=["msc"], name=f"gm{tau}_{sub}_{gg}")
                    em.add("dve", lambda e: e.tensor_copy(cur[:, :], msc[:, :]),
                           reads=["msc"], writes=["cur"], name=f"cp{tau}_{sub}")
                    for it in range(3):
                        em.add("dve", lambda e: e.reduce_max(sc1[:, :], cur[:, :], axis=AX.X),
                               reads=["cur"], writes=["sc1"], name=f"tm{tau}_{sub}_{it}")
                        em.add("dve", lambda e: e.tensor_scalar(indt[:, :], cur[:, :], sc1[:, 0:1], None, ALU.is_ge),
                               reads=["cur", "sc1"], writes=["indt"], name=f"ti{tau}_{sub}_{it}")
                        em.add("dve", lambda e: e.scalar_tensor_tensor(cur[:, :], indt[:, :], -BIG, cur[:, :], ALU.mult, ALU.add),
                               reads=["indt", "cur"], writes=["cur"], name=f"tc{tau}_{sub}_{it}")
                    em.add("dve", lambda e: e.reduce_max(sc1[:, :], cur[:, :], axis=AX.X),
                           reads=["cur"], writes=["sc1"], name=f"m4_{tau}_{sub}")
                    em.add("dve", lambda e: e.tensor_scalar(indt[:, :], msc[:, :], sc1[:, 0:1], None, ALU.is_ge),
                           reads=["msc", "sc1"], writes=["indt"], name=f"ind{tau}_{sub}")
                    em.add("dve", lambda e: e.tensor_mul(indt[:, :], indt[:, :], gsc[:, :]),
                           reads=["indt", "gsc"], writes=["indt"], name=f"w_{tau}_{sub}")
                    em.add("dve", lambda e: e.reduce_sum(sc1[:, :], indt[:, :], axis=AX.X),
                           reads=["indt"], writes=["sc1"], name=f"ws{tau}_{sub}")
                    em.add("dve", lambda e: e.reciprocal(sc1[:, :], sc1[:, :]),
                           reads=["sc1"], writes=["sc1"], name=f"wr{tau}_{sub}")
                    em.add("dve", (lambda sub: lambda e: e.tensor_scalar(
                        cw[:, sub * NE:(sub + 1) * NE], indt[:, :], sc1[:, 0:1], None, ALU.mult))(sub),
                        reads=["indt", "sc1"], writes=["cw"], name=f"cw{tau}_{sub}")
                    em.add("dve", (lambda sub: lambda e: e.tensor_copy(cwb16[:, :], cw[:, sub * NE:(sub + 1) * NE]))(sub),
                           reads=["cw"], writes=["cwb16"], name=f"cwb{tau}_{sub}")
                    em.add("pe", (lambda sub: lambda e: e.transpose(
                        ps_tb[0:16, sub * 128:(sub + 1) * 128], cwb16[:, 0:NE], id_bf[:, :]))(sub),
                        reads=["cwb16", "id_bf"], writes=["tb"], name=f"cwt{tau}_{sub}")
                em.add("dve", lambda e: e.tensor_copy(cwT[0:16, :], ps_tb[0:16, :]),
                       reads=["tb"], writes=["cwT"], name=f"cwc{tau}")

                for ex in range(NE):
                    for s in range(2):
                        em.add("pe", (lambda ex, s: lambda e: e.matmul(
                            ps_c3[:, :], oneh[:, ex * 128:(ex + 1) * 128], cwT[0:16, s * 512:(s + 1) * 512],
                            start=True, stop=True))(ex, s),
                            reads=["oneh", "cwT"], writes=["c3"], name=f"cb{tau}e{ex}s{s}")
                        em.add("dve", (lambda s: lambda e: e.tensor_copy(cwB[s][:, :], ps_c3[:, :]))(s),
                               reads=["c3"], writes=[f"cwB{s}"], name=f"cbe{tau}e{ex}s{s}")
                    for mch in range(ICH):
                        wb = mch % 2
                        em.dma("pool", "wex", (lambda ex, mch, wb: lambda e: e.dma_start(
                            w1c[wb][:, :].rearrange("p (c m) -> p c m", c=NCH),
                            bass.AP(dw1, ex * D * INTER + mch * 128, [[INTER, 128], [128 * INTER, NCH], [1, 128]])))(ex, mch, wb),
                            writes=[f"w1c{wb}"], name=f"w1{tau}e{ex}m{mch}")
                        em.dma("pool", "wex", (lambda ex, mch, wb: lambda e: e.dma_start(
                            w3c[wb][:, :].rearrange("p (c m) -> p c m", c=NCH),
                            bass.AP(dw3, ex * D * INTER + mch * 128, [[INTER, 128], [128 * INTER, NCH], [1, 128]])))(ex, mch, wb),
                            writes=[f"w3c{wb}"], name=f"w3{tau}e{ex}m{mch}")
                        a_ps = [ps_s1, ps_s2][mch % 2]
                        a_reg = ["s1a", "s1b"] if mch % 2 == 0 else ["s2a", "s2b"]
                        for s in range(2):
                            for c in range(NCH):
                                em.add("pe", (lambda c, s, wb, t0, a_ps: lambda e: e.matmul(
                                    a_ps[:, s * 512:(s + 1) * 512], w1c[wb][:, c * 128:(c + 1) * 128],
                                    X1[:, c * TOKC + t0 + s * 512: c * TOKC + t0 + (s + 1) * 512],
                                    start=(c == 0), stop=(c == NCH - 1)))(c, s, wb, t0, a_ps),
                                    reads=[f"w1c{wb}", f"X1:{2 * tau}", f"X1:{2 * tau + 1}"], writes=[a_reg[s]], name=f"a{tau}e{ex}m{mch}s{s}")
                                em.add("pe", (lambda c, s, wb, t0: lambda e: e.matmul(
                                    [ps_c2, ps_c3][s][:, :], w3c[wb][:, c * 128:(c + 1) * 128],
                                    X1[:, c * TOKC + t0 + s * 512: c * TOKC + t0 + (s + 1) * 512],
                                    start=(c == 0), stop=(c == NCH - 1)))(c, s, wb, t0),
                                    reads=[f"w3c{wb}", f"X1:{2 * tau}", f"X1:{2 * tau + 1}"], writes=[["c2", "c3"][s]], name=f"b{tau}e{ex}m{mch}s{s}")
                            em.add("act", (lambda s, a_ps: lambda e: e.activation(
                                silu_t[:, :], a_ps[:, s * 512:(s + 1) * 512], AF.Silu))(s, a_ps),
                                reads=[a_reg[s]], writes=["silu_t"], name=f"si{tau}e{ex}m{mch}s{s}")
                            em.add("dve", (lambda s: lambda e: e.tensor_tensor(
                                btmp[:, :], cwB[s][:, :], [ps_c2, ps_c3][s][:, :], ALU.mult))(s),
                                reads=[f"cwB{s}", ["c2", "c3"][s]], writes=["btmp"], name=f"bt{tau}e{ex}m{mch}s{s}")
                            em.add("dve", (lambda s, mch: lambda e: e.tensor_tensor(
                                Hb[:, mch * TAU + s * 512: mch * TAU + (s + 1) * 512], silu_t[:, :], btmp[:, :], ALU.mult))(s, mch),
                                reads=["silu_t", "btmp"], writes=[f"Hb:{mch}"], name=f"h{tau}e{ex}m{mch}s{s}")
                    for dch in range(NCH):
                        wb = dch % 2
                        em.dma("pool", "wex", (lambda ex, dch, wb: lambda e: e.dma_start(
                            w2c[wb][:, :].rearrange("p (c m) -> p c m", c=ICH),
                            bass.AP(dw2, ex * INTER * D + dch * 128, [[D, 128], [128 * D, ICH], [1, 128]])))(ex, dch, wb),
                            writes=[f"w2c{wb}"], name=f"w2{tau}e{ex}d{dch}")
                        for s in range(2):
                            yb = (2 * dch + s) % 2
                            y_ps = [ps_c1, ps_s1][yb]
                            y_reg = ["c1", "s1a"][yb]
                            for mch in range(ICH):
                                em.add("pe", (lambda mch, s, wb, y_ps: lambda e: e.matmul(
                                    y_ps[:, 0:512], w2c[wb][:, mch * 128:(mch + 1) * 128],
                                    Hb[:, mch * TAU + s * 512: mch * TAU + (s + 1) * 512],
                                    start=(mch == 0), stop=(mch == ICH - 1)))(mch, s, wb, y_ps),
                                    reads=[f"w2c{wb}", f"Hb:{mch}"], writes=[y_reg], name=f"y{tau}e{ex}d{dch}s{s}")
                            if ex == 0:
                                em.add("dve", (lambda dch, s, y_ps: lambda e: e.tensor_copy(
                                    Y[:, dch * TAU + s * 512: dch * TAU + (s + 1) * 512], y_ps[:, 0:512]))(dch, s, y_ps),
                                    reads=[y_reg], writes=[f"Y:{dch}"], name=f"ye{tau}e{ex}d{dch}s{s}")
                            else:
                                em.add("dve", (lambda dch, s, y_ps: lambda e: e.tensor_tensor(
                                    Y[:, dch * TAU + s * 512: dch * TAU + (s + 1) * 512],
                                    Y[:, dch * TAU + s * 512: dch * TAU + (s + 1) * 512], y_ps[:, 0:512], ALU.add))(dch, s, y_ps),
                                    reads=[y_reg, f"Y:{dch}"], writes=[f"Y:{dch}"], name=f"ya{tau}e{ex}d{dch}s{s}")

                for half in range(2):
                    mlist = list(range(half * 11, min(SCH, half * 11 + 11)))
                    for mi, mch in enumerate(mlist):
                        wb = mi % 2
                        em.dma("pool", "wex", (lambda mch, wb: lambda e: e.dma_start(
                            w1c[wb][:, :].rearrange("p (c m) -> p c m", c=NCH),
                            bass.AP(dsw1, mch * 128, [[SINTER, 128], [128 * SINTER, NCH], [1, 128]])))(mch, wb),
                            writes=[f"w1c{wb}"], name=f"sw1{tau}m{mch}")
                        em.dma("pool", "wex", (lambda mch, wb: lambda e: e.dma_start(
                            w3c[wb][:, :].rearrange("p (c m) -> p c m", c=NCH),
                            bass.AP(dsw3, mch * 128, [[SINTER, 128], [128 * SINTER, NCH], [1, 128]])))(mch, wb),
                            writes=[f"w3c{wb}"], name=f"sw3{tau}m{mch}")
                        a_ps = [ps_s1, ps_s2][mi % 2]
                        a_reg = ["s1a", "s1b"] if mi % 2 == 0 else ["s2a", "s2b"]
                        for s in range(2):
                            for c in range(NCH):
                                em.add("pe", (lambda c, s, wb, t0, a_ps: lambda e: e.matmul(
                                    a_ps[:, s * 512:(s + 1) * 512], w1c[wb][:, c * 128:(c + 1) * 128],
                                    X1[:, c * TOKC + t0 + s * 512: c * TOKC + t0 + (s + 1) * 512],
                                    start=(c == 0), stop=(c == NCH - 1)))(c, s, wb, t0, a_ps),
                                    reads=[f"w1c{wb}", f"X1:{2 * tau}", f"X1:{2 * tau + 1}"], writes=[a_reg[s]], name=f"sa{tau}m{mch}s{s}")
                                em.add("pe", (lambda c, s, wb, t0: lambda e: e.matmul(
                                    [ps_c2, ps_c3][s][:, :], w3c[wb][:, c * 128:(c + 1) * 128],
                                    X1[:, c * TOKC + t0 + s * 512: c * TOKC + t0 + (s + 1) * 512],
                                    start=(c == 0), stop=(c == NCH - 1)))(c, s, wb, t0),
                                    reads=[f"w3c{wb}", f"X1:{2 * tau}", f"X1:{2 * tau + 1}"], writes=[["c2", "c3"][s]], name=f"sb{tau}m{mch}s{s}")
                            em.add("act", (lambda s, a_ps: lambda e: e.activation(
                                silu_t[:, :], a_ps[:, s * 512:(s + 1) * 512], AF.Silu))(s, a_ps),
                                reads=[a_reg[s]], writes=["silu_t"], name=f"ssi{tau}m{mch}s{s}")
                            em.add("dve", (lambda s, mi: lambda e: e.tensor_tensor(
                                Hb[:, mi * TAU + s * 512: mi * TAU + (s + 1) * 512], silu_t[:, :],
                                [ps_c2, ps_c3][s][:, :], ALU.mult))(s, mi),
                                reads=["silu_t", ["c2", "c3"][s]], writes=[f"Hb:{mi}"], name=f"sh{tau}m{mch}s{s}")
                    for dch in range(NCH):
                        wb = dch % 2
                        em.dma("pool", "wex", (lambda dch, wb, half: lambda e: e.dma_start(
                            w2c[wb][:, 0:len(mlist) * 128].rearrange("p (c m) -> p c m", c=len(mlist)),
                            bass.AP(dsw2, (half * 11 * 128) * D + dch * 128, [[D, 128], [128 * D, len(mlist)], [1, 128]])))(dch, wb, half),
                            writes=[f"w2c{wb}"], name=f"sw2{tau}h{half}d{dch}")
                        for s in range(2):
                            yb = (2 * dch + s) % 2
                            y_ps = [ps_c1, ps_s1][yb]
                            y_reg = ["c1", "s1a"][yb]
                            for mi in range(len(mlist)):
                                em.add("pe", (lambda mi, s, wb, y_ps, nm=len(mlist): lambda e: e.matmul(
                                    y_ps[:, 0:512], w2c[wb][:, mi * 128:(mi + 1) * 128],
                                    Hb[:, mi * TAU + s * 512: mi * TAU + (s + 1) * 512],
                                    start=(mi == 0), stop=(mi == nm - 1)))(mi, s, wb, y_ps),
                                    reads=[f"w2c{wb}", f"Hb:{mi}"], writes=[y_reg], name=f"sy{tau}h{half}d{dch}s{s}")
                            em.add("dve", (lambda dch, s, y_ps: lambda e: e.tensor_tensor(
                                Y[:, dch * TAU + s * 512: dch * TAU + (s + 1) * 512],
                                Y[:, dch * TAU + s * 512: dch * TAU + (s + 1) * 512], y_ps[:, 0:512], ALU.add))(dch, s, y_ps),
                                reads=[y_reg, f"Y:{dch}"], writes=[f"Y:{dch}"], name=f"sya{tau}h{half}d{dch}s{s}")

                for c in range(NCH):
                    em.add("dve", (lambda c, t0: lambda e: e.tensor_tensor(
                        Y[:, c * TAU:(c + 1) * TAU], Y[:, c * TAU:(c + 1) * TAU],
                        X1[:, c * TOKC + t0: c * TOKC + t0 + TAU], ALU.add))(c, t0),
                        reads=[f"Y:{c}", f"X1:{2 * tau}", f"X1:{2 * tau + 1}"], writes=[f"Y:{c}"], name=f"yx{tau}_{c}")
                for ti in range(8):
                    for q in range(4):
                        for cc in range(4):
                            c = q * 4 + cc
                            em.add("pe", (lambda c, cc, ti: lambda e: e.transpose(
                                ps_c3[:, cc * 128:(cc + 1) * 128], Y[:, c * TAU + ti * 128: c * TAU + (ti + 1) * 128],
                                id_f[:, :]))(c, cc, ti),
                                reads=[f"Y:{c}", "id_f"], writes=["c3"], name=f"ot{tau}_{ti}_{c}")
                        em.add("dve", (lambda q, ti: lambda e: e.tensor_copy(
                            outm[:, q * 512:(q + 1) * 512], ps_c3[:, :]))(q, ti),
                            reads=["c3"], writes=["outm"], name=f"oc{tau}_{ti}_{q}")
                    em.add("dve", (lambda: lambda e: e.reduce_sum(srow[:, :], outm[:, :], axis=AX.X))(),
                           reads=["outm"], writes=["srow"], name=f"os{tau}_{ti}")
                    em.add("dve", lambda e: e.tensor_mul(sqm[:, :], outm[:, :], outm[:, :]),
                           reads=["outm"], writes=["sqm"], name=f"oq{tau}_{ti}")
                    em.add("dve", lambda e: e.reduce_sum(qrow[:, :], sqm[:, :], axis=AX.X),
                           reads=["sqm"], writes=["qrow"], name=f"oqs{tau}_{ti}")
                    em.add("dve", lambda e: e.tensor_scalar(srow[:, :], srow[:, :], 1.0 / D, None, ALU.mult),
                           reads=["srow"], writes=["srow"], name=f"om{tau}_{ti}")
                    em.add("dve", lambda e: e.tensor_scalar(qrow[:, :], qrow[:, :], 1.0 / D, None, ALU.mult),
                           reads=["qrow"], writes=["qrow"], name=f"oq2{tau}_{ti}")
                    em.add("dve", lambda e: e.tensor_mul(vr2[:, :], srow[:, :], srow[:, :]),
                           reads=["srow"], writes=["vr2"], name=f"ov{tau}_{ti}")
                    em.add("dve", lambda e: e.tensor_sub(vr2[:, :], qrow[:, :], vr2[:, :]),
                           reads=["qrow", "vr2"], writes=["vr2"], name=f"ov2{tau}_{ti}")
                    em.add("act", lambda e: e.activation(vr2[:, :], vr2[:, :], AF.Ln, bias=c_eps[:, :]),
                           reads=["vr2", "c_eps"], writes=["vr2"], name=f"oln{tau}_{ti}")
                    em.add("act", lambda e: e.activation(vr2[:, :], vr2[:, :], AF.Exp, scale=c_nhalf[:, :]),
                           reads=["vr2", "c_nhalf"], writes=["vr2"], name=f"orst{tau}_{ti}")
                    em.add("dve", lambda e: e.tensor_scalar(outm[:, :], outm[:, :], srow[:, 0:1], vr2[:, 0:1], ALU.subtract, ALU.mult),
                           reads=["outm", "srow", "vr2"], writes=["outm"], name=f"oap{tau}_{ti}")
                    em.dma("sp", "out", (lambda tau, ti: lambda e: e.dma_start(
                        bass.AP(dout, (tau * TAU + ti * 128) * D, [[D, 128], [1, D]]), outm[:, :]))(tau, ti),
                        reads=["outm"], name=f"ow{tau}_{ti}")

            em.dma("sp", "out", lambda e: e.dma_start(doid.ap(), sched_i[0:1, 0:1]), reads=["sched"], name="oid")
            em.drain({"wex": "pool", "out": "sp"})
            with nc.Block() as blk:
                em.emit(blk)


    _CACHED["nc"] = nc
    return nc


def kernel(L0_attn_x, L0_attn_Wq, L0_attn_Wk, L0_attn_Wv, L0_attn_Wo, L0_attn_bo,
           L0_ffn_gate_w, L0_ffn_expert_bias, L0_ffn_w1, L0_ffn_w2, L0_ffn_w3,
           L0_ffn_sw1, L0_ffn_sw2, L0_ffn_sw3):
    nc = _build()
    inp = {
        "x": np.ascontiguousarray(L0_attn_x, dtype=np.float32),
        "wq": np.ascontiguousarray(L0_attn_Wq, dtype=np.float32),
        "wk": np.ascontiguousarray(L0_attn_Wk, dtype=np.float32),
        "wv": np.ascontiguousarray(L0_attn_Wv, dtype=np.float32),
        "wo": np.ascontiguousarray(L0_attn_Wo, dtype=np.float32),
        "bo": np.ascontiguousarray(L0_attn_bo, dtype=np.float32),
        "gw": np.ascontiguousarray(L0_ffn_gate_w, dtype=np.float32),
        "eb": np.ascontiguousarray(L0_ffn_expert_bias, dtype=np.float32),
        "w1": np.ascontiguousarray(L0_ffn_w1, dtype=np.float32),
        "w2": np.ascontiguousarray(L0_ffn_w2, dtype=np.float32),
        "w3": np.ascontiguousarray(L0_ffn_w3, dtype=np.float32),
        "sw1": np.ascontiguousarray(L0_ffn_sw1, dtype=np.float32),
        "sw2": np.ascontiguousarray(L0_ffn_sw2, dtype=np.float32),
        "sw3": np.ascontiguousarray(L0_ffn_sw3, dtype=np.float32),
    }
    res = run_bass_kernel_spmd(nc, [dict(inp) for _ in range(8)], core_ids=list(range(8)))
    out = np.empty((B, T, D), dtype=np.float32)
    seen = set()
    for c in range(8):
        r = res.results[c]
        w = int(np.asarray(r["oid"]).reshape(-1)[0])
        assert 0 <= w < 8 and w not in seen, f"bad workload ids: {w}, {seen}"
        seen.add(w)
        j, b = w % 4, w // 4
        om = np.asarray(r["out_my"]).reshape(TOKC, D)
        for v in range(NV):
            q0 = q0tok(w, v)
            out[b, q0:q0 + QB, :] = om[v * QB:(v + 1) * QB, :]
    return out


# version 08
# speedup vs baseline: 2.1373x; 1.1581x over previous
import math
import os
import numpy as np
from contextlib import ExitStack

import concourse.bass as bass
import concourse.mybir as mybir
from concourse.bass_utils import run_bass_kernel_spmd
from concourse.bass import IndirectOffsetOnAxis

F32 = mybir.dt.float32
BF16 = mybir.dt.bfloat16
I32 = mybir.dt.int32
AF = mybir.ActivationFunctionType
ALU = mybir.AluOpType
AX = mybir.AxisListType

B, T, D = 2, 8192, 2048
NCH = 16
H, G, DH = 16, 4, 128
KVD = 512
NE, INTER, ICH = 16, 1408, 11
SINTER, SCH = 2816, 22
QB = 512
NV = 4
TOKC = 2048
TAU = 1024
KSW = T + 128
EXP_SHIFT = -12.0
SM_SCALE = 1.0 / math.sqrt(DH)
EPS = 1e-5
BIG = 1.0e9

PPREF = [2048 * (v + 1) for v in range(NV)]
NPAIR = [p // 128 for p in PPREF]
PAIR_OFF = [0, 16, 48, 96]
NPAIR_TOT = 160

def q0tok(w, v):
    j = w % 4
    return (j + 4 * v) * QB

ENG2BLK = {"pe": "tensor", "dve": "vector", "act": "scalar", "pool": "gpsimd", "sp": "sync"}

class _Op:
    __slots__ = ("eng", "fn", "deps", "inc", "event", "clock", "stream", "waits", "name")
    def __init__(self):
        self.deps = []
        self.inc = False
        self.event = None
        self.clock = None
        self.stream = None
        self.waits = None
        self.name = ""

class Em:
    def __init__(self, nc, es, tag):
        self.nc = nc
        self.es = es
        self.tag = tag
        self.ops = []
        self.lw = {}
        self.rd = {}
        self.sems = {}
        self.scnt = {}

    def _sem(self, key):
        s = self.sems.get(key)
        if s is None:
            s = self.es.enter_context(self.nc.semaphore(f"{self.tag}_{key}".replace(":", "_")))
            self.sems[key] = s
        return s

    def add(self, eng, fn, reads=(), writes=(), name=""):
        op = _Op()
        op.eng = eng
        op.fn = fn
        op.name = name
        deps = {}
        for r in reads:
            w = self.lw.get(r)
            if w is not None:
                deps[id(w)] = w
        for r in writes:
            w = self.lw.get(r)
            if w is not None:
                deps[id(w)] = w
            for rr in self.rd.get(r, {}).values():
                deps[id(rr)] = rr
        if deps.pop(id(op), None) is not None:
            pass
        op.deps = list(deps.values())
        for r in reads:
            self.rd.setdefault(r, {})[eng] = op
        for r in writes:
            self.lw[r] = op
            self.rd[r] = {eng: op}
        self.ops.append(op)
        return op

    def dma(self, eng, stream, fn, reads=(), writes=(), name=""):
        op = self.add(eng, fn, reads, writes, name=name)
        op.stream = stream
        k = self.scnt.get(stream, 0) + 1
        self.scnt[stream] = k
        op.event = ("s:" + stream, 16 * k)
        return op

    def stream_count(self, stream):
        return self.scnt.get(stream, 0)

    def emit(self, block):
        for op in self.ops:
            for d in op.deps:
                if d.stream is not None:
                    continue
                if d.eng == "pe" and op.eng == "pe":
                    continue
                d.inc = True
        ctr = {}
        for op in self.ops:
            if op.stream is None and op.inc:
                c = ctr.get(op.eng, 0) + 1
                ctr[op.eng] = c
                op.event = ("e:" + op.eng, c)
        known = {}
        for op in self.ops:
            st = known.setdefault(op.eng, {})
            clk = {}
            need = {}
            for d in op.deps:
                if d.clock:
                    for s2, v2 in d.clock.items():
                        if clk.get(s2, 0) < v2:
                            clk[s2] = v2
                ev = d.event
                if ev is not None:
                    if clk.get(ev[0], 0) < ev[1]:
                        clk[ev[0]] = ev[1]
                    if d.eng == "pe" and op.eng == "pe":
                        continue
                    if need.get(ev[0], 0) < ev[1]:
                        need[ev[0]] = ev[1]
            waits = []
            for semk, val in need.items():
                if st.get(semk, 0) < val:
                    waits.append((semk, val))
            if waits:
                for d in op.deps:
                    ev = d.event
                    if ev is not None and st.get(ev[0], 0) < ev[1]:
                        st[ev[0]] = ev[1]
                        if d.clock:
                            for s2, v2 in d.clock.items():
                                if st.get(s2, 0) < v2:
                                    st[s2] = v2
            for s2, v2 in st.items():
                if clk.get(s2, 0) < v2:
                    clk[s2] = v2
            if op.event is not None:
                clk[op.event[0]] = op.event[1]
            op.clock = clk
            op.waits = waits
        if os.environ.get("KN_WAITSTATS"):
            import collections
            wc = collections.Counter()
            for op in self.ops:
                wc[(op.eng, "wait")] += len(op.waits)
                if op.stream is None and op.inc:
                    wc[(op.eng, "inc")] += 1
            print(f"[{self.tag}] waits/incs:", dict(wc))
        per = {}
        for op in self.ops:
            per.setdefault(op.eng, []).append(op)

        def mk_runner(oplist):
            def run(eh):
                for op in oplist:
                    for (semk, val) in op.waits:
                        eh.wait_ge(self._sem(semk), val)
                    try:
                        ins = op.fn(eh)
                    except Exception as ex:
                        raise RuntimeError(f"op failed: {self.tag}/{op.eng}/{op.name} waits={op.waits}") from ex
                    if op.stream is not None:
                        ins.then_inc(self._sem("s:" + op.stream), 16)
                    elif op.inc:
                        ins.then_inc(self._sem("e:" + op.eng), 1)
            return run

        for eng, oplist in per.items():
            getattr(block, ENG2BLK[eng])(mk_runner(oplist))

    def drain(self, eng_for_stream):
        for stream, cnt in list(self.scnt.items()):
            eng = eng_for_stream[stream]
            sem = ("s:" + stream, 16 * cnt)
            op = _Op()
            op.eng = eng
            op.fn = lambda eh: eh.nop()
            op.waits = None
            fake = _Op()
            fake.eng = "?"
            fake.event = sem
            fake.clock = {}
            op.deps = [fake]
            self.ops.append(op)


_CACHED = {}

def _build():
    if "nc" in _CACHED:
        return _CACHED["nc"]
    PHASES = int(os.environ.get("KN_PHASES", "4"))
    nc = bass.Bass("TRN2", target_bir_lowering=False)

    dx = nc.dram_tensor("x", [B, T, D], F32, kind="ExternalInput")
    dwq = nc.dram_tensor("wq", [D, D], F32, kind="ExternalInput")
    dwk = nc.dram_tensor("wk", [D, KVD], F32, kind="ExternalInput")
    dwv = nc.dram_tensor("wv", [D, KVD], F32, kind="ExternalInput")
    dwo = nc.dram_tensor("wo", [D, D], F32, kind="ExternalInput")
    dbo = nc.dram_tensor("bo", [D], F32, kind="ExternalInput")
    dgw = nc.dram_tensor("gw", [NE, D], F32, kind="ExternalInput")
    deb = nc.dram_tensor("eb", [NE], F32, kind="ExternalInput")
    dw1 = nc.dram_tensor("w1", [NE, D, INTER], F32, kind="ExternalInput")
    dw2 = nc.dram_tensor("w2", [NE, INTER, D], F32, kind="ExternalInput")
    dw3 = nc.dram_tensor("w3", [NE, D, INTER], F32, kind="ExternalInput")
    dsw1 = nc.dram_tensor("sw1", [D, SINTER], F32, kind="ExternalInput")
    dsw2 = nc.dram_tensor("sw2", [SINTER, D], F32, kind="ExternalInput")
    dsw3 = nc.dram_tensor("sw3", [D, SINTER], F32, kind="ExternalInput")

    dout = nc.dram_tensor("out_my", [TOKC, D], F32, kind="ExternalOutput")
    doid = nc.dram_tensor("oid", [1, 1], I32, kind="ExternalOutput")
    DEBUG = int(os.environ.get("KN_DEBUG", "0"))
    if DEBUG:
        d_ctx0 = nc.dram_tensor("d_ctx0", [128, H * 512], F32, kind="ExternalOutput")
        d_l0 = nc.dram_tensor("d_l0", [1, 512], F32, kind="ExternalOutput")
        d_ksb0 = nc.dram_tensor("d_ksb0", [128, 16 * 128], F32, kind="ExternalOutput")
        d_vsb0 = nc.dram_tensor("d_vsb0", [128, 16 * 128], F32, kind="ExternalOutput")
        d_q0 = nc.dram_tensor("d_q0", [128, 512], F32, kind="ExternalOutput")
        d_p0 = nc.dram_tensor("d_p0", [128, 1024], F32, kind="ExternalOutput")
        d_x1 = nc.dram_tensor("d_x1", [128, NCH * TOKC], F32, kind="ExternalOutput")
        d_usb0 = nc.dram_tensor("d_usb0", [128, NCH * 512], F32, kind="ExternalOutput")
        d_vdix = nc.dram_tensor("d_vdix", [128, 64], I32, kind="ExternalOutput")
        d_kdix = nc.dram_tensor("d_kdix", [128, 64], I32, kind="ExternalOutput")

    kscr = nc.dram_tensor("kscr", [KVD, KSW], BF16)
    vscr = nc.dram_tensor("vscr", [KSW, KVD], BF16)

    top = ExitStack()
    _CACHED["top"] = top
    E_ = top.enter_context

    sched_i = E_(nc.sbuf_tensor("sched_i", [1, 8], I32))
    sched_f = E_(nc.sbuf_tensor("sched_f", [1, 16], F32))
    id_bf = E_(nc.sbuf_tensor("id_bf", [128, 128], BF16))
    id_f = E_(nc.sbuf_tensor("id_f", [128, 128], F32))
    ones_col_bf = E_(nc.sbuf_tensor("ones_col_bf", [128, 1], BF16))
    ones_row_bf = E_(nc.sbuf_tensor("ones_row_bf", [1, 128], BF16))
    ones_row_f = E_(nc.sbuf_tensor("ones_row_f", [1, 128], F32))
    c_scale = E_(nc.sbuf_tensor("c_scale", [128, 1], F32))
    c_shift = E_(nc.sbuf_tensor("c_shift", [128, 1], F32))
    c_eps = E_(nc.sbuf_tensor("c_eps", [128, 1], F32))
    c_nhalf = E_(nc.sbuf_tensor("c_nhalf", [128, 1], F32))
    bo_sb = E_(nc.sbuf_tensor("bo_sb", [128, NCH], F32))
    ebias_b = E_(nc.sbuf_tensor("ebias_b", [128, NE], F32))
    gw_fm = E_(nc.sbuf_tensor("gw_fm", [128, NCH * NE], BF16))
    oneh = E_(nc.sbuf_tensor("oneh", [16, NE * 128], BF16))
    kdix = E_(nc.sbuf_tensor("kdix", [128, 64], I32))
    vdix = E_(nc.sbuf_tensor("vdix", [128, 64], I32))
    mcol = E_(nc.sbuf_tensor("mcol", [128, 48], F32))
    X1 = E_(nc.sbuf_tensor("X1", [128, NCH * TOKC], BF16))

    ps_s1 = E_(nc.psum_tensor("ps_s1", [128, 1024], F32))
    ps_s2 = E_(nc.psum_tensor("ps_s2", [128, 1024], F32))
    ps_c1 = E_(nc.psum_tensor("ps_c1", [128, 512], F32))
    ps_c2 = E_(nc.psum_tensor("ps_c2", [128, 512], F32))
    ps_c3 = E_(nc.psum_tensor("ps_c3", [128, 512], F32))
    ps_tb = E_(nc.psum_tensor("ps_tb", [128, 1024], BF16))

    r_g1 = nc.gpsimd.alloc_register("r_g1")
    r_g2 = nc.gpsimd.alloc_register("r_g2")

    for arm in nc.switch_core_id([nc.vector], n=128):
        w = int(arm) % 8
        j, b = w % 4, w // 4
        nc.vector.memset(sched_i[0:1, 0:1], w)
        nc.vector.memset(sched_i[0:1, 1:2], b * T * D)
        for v in range(NV):
            nc.vector.memset(sched_i[0:1, 2 + v:3 + v], (b * T + q0tok(w, v)) * D)
        for v in range(NV):
            nc.vector.memset(sched_f[0:1, v:v + 1], float(q0tok(w, v)))


    with ExitStack() as ph:
        Ep = ph.enter_context
        bo_t = Ep(nc.sbuf_tensor("bo_t", [16, 128], F32))
        eb_t = Ep(nc.sbuf_tensor("eb_t", [1, NE], F32))
        gw_t = Ep(nc.sbuf_tensor("gw_t", [16, D], BF16))
        itf = Ep(nc.sbuf_tensor("itf", [1, 16], F32))
        m1f = Ep(nc.sbuf_tensor("m1f", [1, 16], F32))
        m2f = Ep(nc.sbuf_tensor("m2f", [1, 16], F32))
        iop = Ep(nc.sbuf_tensor("iop", [128, 1], F32))
        io65 = Ep(nc.sbuf_tensor("io65", [128, 1], F32))
        gb = Ep(nc.sbuf_tensor("gb", [128, 4], F32))
        q0b = Ep(nc.sbuf_tensor("q0b", [128, 4], F32))
        q0tb = Ep(nc.sbuf_tensor("q0tb", [128, 4], F32))
        q128r = Ep(nc.sbuf_tensor("q128r", [1, 4], F32))
        kdf = Ep(nc.sbuf_tensor("kdf", [128, 64], F32))
        vdf = Ep(nc.sbuf_tensor("vdf", [128, 64], F32))
        vtmp = Ep(nc.sbuf_tensor("vtmp", [128, 1], F32))
        em = Em(nc, top, "p0")

        em.dma("pool", "ld", lambda e: e.dma_start(bo_t[:, :], dbo.ap().rearrange("(c p) -> c p", p=128)), writes=["bo_t"])
        em.dma("pool", "ld", lambda e: e.dma_start(eb_t[:, :], deb.ap().unsqueeze(0)), writes=["eb_t"])
        em.dma("pool", "ld", lambda e: e.dma_start(gw_t[:, :], dgw.ap()), writes=["gw_t"])

        em.add("pool", lambda e: e.memset(id_bf[:, :], 0.0), writes=["id_bf"])
        em.add("pool", lambda e: e.affine_select(out=id_bf[:, :], in_=id_bf[:, :], compare_op=ALU.not_equal,
                                                 fill=1.0, base=0, pattern=[[-1, 128]], channel_multiplier=1),
               reads=["id_bf"], writes=["id_bf"])
        em.add("pool", lambda e: e.memset(id_f[:, :], 0.0), writes=["id_f"])
        em.add("pool", lambda e: e.affine_select(out=id_f[:, :], in_=id_f[:, :], compare_op=ALU.not_equal,
                                                 fill=1.0, base=0, pattern=[[-1, 128]], channel_multiplier=1),
               reads=["id_f"], writes=["id_f"])
        em.add("pool", lambda e: e.memset(ones_col_bf[:, :], 1.0), writes=["ones_col"])
        em.add("pool", lambda e: e.memset(ones_row_bf[:, :], 1.0), writes=["ones_row"])
        em.add("pool", lambda e: e.memset(ones_row_f[:, :], 1.0), writes=["ones_row_f"])
        em.add("pool", lambda e: e.memset(c_scale[:, :], SM_SCALE), writes=["c_scale"])
        em.add("pool", lambda e: e.memset(c_shift[:, :], EXP_SHIFT), writes=["c_shift"])
        em.add("pool", lambda e: e.memset(c_eps[:, :], EPS), writes=["c_eps"])
        em.add("pool", lambda e: e.memset(c_nhalf[:, :], -0.5), writes=["c_nhalf"])
        em.add("pool", lambda e: e.memset(oneh[:, :], 0.0), writes=["oneh"])
        em.add("pool", lambda e: e.affine_select(out=oneh[:, :], in_=oneh[:, :], compare_op=ALU.not_equal,
                                                 fill=1.0, base=0, pattern=[[-1, NE], [0, 128]], channel_multiplier=1),
               reads=["oneh"], writes=["oneh"])

        em.add("pe", lambda e: e.transpose(ps_c1[:, 0:16], bo_t[:, :], id_f[0:16, 0:16]),
               reads=["bo_t", "id_f"], writes=["c1"])
        em.add("dve", lambda e: e.tensor_copy(bo_sb[:, :], ps_c1[:, 0:16]), reads=["c1"], writes=["bo_sb"])
        em.add("pe", lambda e: e.matmul(ps_c2[:, 0:NE], ones_row_f[:, :], eb_t[:, :],
                                        start=True, stop=True),
               reads=["ones_row_f", "eb_t"], writes=["c2"])
        em.add("dve", lambda e: e.tensor_copy(ebias_b[:, :], ps_c2[:, 0:NE]), reads=["c2"], writes=["ebias_b"])
        for c in range(NCH):
            em.add("pe", (lambda c: lambda e: e.transpose(ps_tb[:, c * 16:(c + 1) * 16],
                                                          gw_t[:, c * 128:(c + 1) * 128],
                                                          id_bf[0:16, 0:16]))(c),
                   reads=["gw_t", "id_bf"], writes=["tb"])
        em.add("dve", lambda e: e.tensor_copy(gw_fm[:, 0:NCH * NE], ps_tb[:, 0:NCH * NE]),
               reads=["tb"], writes=["gw_fm"])

        em.add("pool", lambda e: e.iota(itf[0:1, 0:12], [[1, 12]], base=0, channel_multiplier=0,
                                        allow_small_or_imprecise_dtypes=True), writes=["i12"])
        em.add("pool", lambda e: e.iota(iop[:, :], [[0, 1]], base=0, channel_multiplier=1,
                                        allow_small_or_imprecise_dtypes=True), writes=["iop"])
        em.add("pool", lambda e: e.iota(io65[:, :], [[0, 1]], base=0, channel_multiplier=65,
                                        allow_small_or_imprecise_dtypes=True), writes=["io65"])
        em.add("dve", lambda e: e.tensor_scalar(q128r[0:1, 0:4], sched_f[0:1, 0:4], 1.0 / 128.0, None, ALU.mult),
               reads=["sched"], writes=["q128r"])
        em.add("pe", lambda e: e.matmul(ps_c3[:, 0:4], ones_row_f[:, :], sched_f[0:1, 0:4], start=True, stop=True),
               reads=["ones_row_f", "sched"], writes=["c3"])
        em.add("dve", lambda e: e.tensor_copy(q0b[:, 0:4], ps_c3[:, 0:4]), reads=["c3"], writes=["q0b"])
        em.add("pe", lambda e: e.matmul(ps_c3[:, 4:8], ones_row_f[:, :], q128r[0:1, 0:4], start=True, stop=True),
               reads=["ones_row_f", "q128r"], writes=["c3"])
        em.add("dve", lambda e: e.tensor_copy(q0tb[:, 0:4], ps_c3[:, 4:8]), reads=["c3"], writes=["q0tb"])
        for g in range(G):
            em.add("dve", (lambda g: lambda e: e.tensor_scalar(
                gb[:, g:g + 1], io65[:, :], float(g * 128 * 65), None, ALU.add))(g),
                reads=["io65"], writes=["gb"])
        for v in range(NV):
            for g in range(G):
                for p in range(4):
                    col = (v * 4 + g) * 4 + p
                    em.add("dve", (lambda col, g, v, p: lambda e: e.tensor_scalar(
                        kdf[:, col:col + 1], gb[:, g:g + 1], q0tb[:, v:v + 1], float(p), ALU.add, ALU.add))(col, g, v, p),
                        reads=["gb", "q0tb"], writes=["kdf"])
        em.add("dve", lambda e: e.tensor_copy(kdix[:, :], kdf[:, :]), reads=["kdf"], writes=["kdix"])
        for v in range(NV):
            for g in range(G):
                for p in range(4):
                    col = (v * 4 + g) * 4 + p
                    em.add("dve", (lambda col, v, p: lambda e: e.tensor_scalar(
                        vdf[:, col:col + 1], iop[:, :], q0b[:, v:v + 1], float(128 * p), ALU.add, ALU.add))(col, v, p),
                        reads=["iop", "q0b"], writes=["vdf"])
                    em.add("dve", (lambda col: lambda e: e.tensor_scalar(
                        vdf[:, col:col + 1], vdf[:, col:col + 1], 4.0, None, ALU.mult))(col),
                        reads=["vdf"], writes=["vdf"])
                    em.add("dve", (lambda col, g: lambda e: e.tensor_scalar(
                        vdf[:, col:col + 1], vdf[:, col:col + 1], float(g), None, ALU.add))(col, g),
                        reads=["vdf"], writes=["vdf"])
        em.add("dve", lambda e: e.tensor_copy(vdix[:, :], vdf[:, :]), reads=["vdf"], writes=["vdix"])
        for v in range(NV):
            n = NPAIR[v]
            em.add("dve", (lambda v, n: lambda e: e.tensor_scalar(
                m1f[0:1, 0:12], itf[0:1, 0:12], float(n - 16), None, ALU.add))(v, n),
                reads=["i12"], writes=["m1f"])
            em.add("dve", (lambda v: lambda e: e.tensor_scalar(
                m1f[0:1, 0:12], m1f[0:1, 0:12], q128r[0:1, v:v + 1], None, ALU.is_lt))(v),
                reads=["m1f", "q128r"], writes=["m1f"])
            em.add("dve", lambda e: e.tensor_copy(m2f[0:1, 0:12], m1f[0:1, 0:12]),
                   reads=["m1f"], writes=["m2f"])
            em.add("pe", (lambda v: lambda e: e.matmul(
                ps_c3[:, 0:12], ones_row_f[:, :], m2f[0:1, 0:12], start=True, stop=True))(v),
                reads=["ones_row_f", "m2f"], writes=["c3"])
            em.add("dve", (lambda v: lambda e: e.tensor_copy(mcol[:, v * 12:(v + 1) * 12], ps_c3[:, 0:12]))(v),
                   reads=["c3"], writes=["mcol"])

        if DEBUG:
            em.dma("pool", "dbg0", lambda e: e.dma_start(d_vdix.ap(), vdix[:, :]), reads=["vdix"], name="dbgvd")
            em.dma("pool", "dbg0", lambda e: e.dma_start(d_kdix.ap(), kdix[:, :]), reads=["kdix"], name="dbgkd")
        em.drain({"ld": "pool", "dbg0": "pool"})
        with nc.Block() as blk:
            em.emit(blk)

    if PHASES >= 2:
        with ExitStack() as ph:
            Ep = ph.enter_context
            wk_sb = Ep(nc.sbuf_tensor("wk_sb", [128, NCH * KVD], BF16))
            wv_sb = Ep(nc.sbuf_tensor("wv_sb", [128, NCH * KVD], BF16))
            x_tm = [Ep(nc.sbuf_tensor(f"x_tm{i}", [128, 8 * D], BF16)) for i in range(2)]
            xfm = [Ep(nc.sbuf_tensor(f"xfm{i}", [128, NCH * 512], BF16)) for i in range(2)]
            kst = [Ep(nc.sbuf_tensor(f"kst{i}", [128, 512], BF16)) for i in range(2)]
            vst = [Ep(nc.sbuf_tensor(f"vst{i}", [128, 512], BF16)) for i in range(2)]
            em = Em(nc, top, "a1")

            em.dma("pool", "wkv", lambda e: e.dma_start(
                wk_sb[:, :].rearrange("p (c n) -> p c n", c=NCH),
                dwk.ap().rearrange("(c p) n -> p c n", p=128)), writes=["wk"])
            em.dma("pool", "wkv", lambda e: e.dma_start(
                wv_sb[:, :].rearrange("p (c n) -> p c n", c=NCH),
                dwv.ap().rearrange("(c p) n -> p c n", p=128)), writes=["wv"])

            em.add("pool", lambda e: e.reg_load(r_g1, sched_i[0:1, 1:2]), reads=["sched"], name="ldxb")

            for st0 in range(16):
                pb = st0 % 2
                if st0 % 2 == 0:
                    ld = st0 // 2
                    lb = ld % 2
                    def xload(e, ld=ld, lb=lb):
                        e.reg_mov(r_g2, ld * 1024 * D)
                        e.reg_add(r_g2, r_g2, r_g1)
                        return e.dma_start(x_tm[lb][:, :].rearrange("p (t c) -> p t c", t=8),
                                           bass.AP(dx, r_g2, [[D, 128], [128 * D, 8], [1, D]]))
                    em.dma("pool", "xin", xload, reads=[], writes=[f"x8_{lb}"], name=f"xl{ld}")
                lb = (st0 // 2) % 2
                for ti in range(4):
                    toff = (st0 % 2) * 4 + ti
                    for half in range(2):
                        for cc in range(8):
                            c = half * 8 + cc
                            em.add("pe", (lambda lb, c, cc, toff: lambda e: e.transpose(
                                ps_tb[:, cc * 128:(cc + 1) * 128],
                                x_tm[lb][:, toff * D + c * 128: toff * D + (c + 1) * 128], id_bf[:, :]))(lb, c, cc, toff),
                                reads=[f"x8_{lb}", "id_bf"], writes=["tb"], name=f"xt{st0}_{ti}")
                        em.add("dve", (lambda pb, ti, half: lambda e: e.tensor_copy(
                            xfm[pb][:, :].rearrange("p (c t) -> p c t", c=NCH)[:, half * 8:(half + 1) * 8, ti * 128:(ti + 1) * 128],
                            ps_tb[:, :].rearrange("p (c t) -> p c t", c=8)))(pb, ti, half),
                            reads=["tb"], writes=[f"xfm{pb}:{ti}"], name=f"xc{st0}_{ti}")
                for g in range(G):
                    kb = g % 2
                    for c in range(NCH):
                        em.add("pe", (lambda pb, g, c, kb: lambda e: e.matmul(
                            (ps_c1 if kb == 0 else ps_c2)[:, :],
                            wk_sb[:, c * KVD + g * 128: c * KVD + (g + 1) * 128],
                            xfm[pb][:, c * 512:(c + 1) * 512],
                            start=(c == 0), stop=(c == NCH - 1)))(pb, g, c, kb),
                            reads=["wk", f"xfm{pb}:0", f"xfm{pb}:1", f"xfm{pb}:2", f"xfm{pb}:3"],
                            writes=["c1" if kb == 0 else "c2"], name=f"kmm{st0}g{g}")
                    em.add("dve", (lambda kb, g: lambda e: e.tensor_copy(kst[kb][:, :], (ps_c1 if kb == 0 else ps_c2)[:, :]))(kb, g),
                           reads=["c1" if kb == 0 else "c2"], writes=[f"kst{kb}"], name=f"kev{st0}g{g}")
                    em.dma("sp", "kout", (lambda kb, g, st0: lambda e: e.dma_start(
                        bass.AP(kscr, g * 128 * KSW + st0 * 512, [[KSW, 128], [1, 512]]), kst[kb][:, :]))(kb, g, st0),
                        reads=[f"kst{kb}"], name=f"kw{st0}g{g}")
                for ti in range(4):
                    vb = ti % 2
                    for c in range(NCH):
                        em.add("pe", (lambda pb, ti, c, vb: lambda e: e.matmul(
                            (ps_s1 if vb == 0 else ps_s2)[:, 0:512],
                            xfm[pb][:, c * 512 + ti * 128: c * 512 + (ti + 1) * 128],
                            wv_sb[:, c * KVD:(c + 1) * KVD],
                            start=(c == 0), stop=(c == NCH - 1)))(pb, ti, c, vb),
                            reads=["wv", f"xfm{pb}:{ti}"], writes=["s1" if vb == 0 else "s2"], name=f"vmm{st0}t{ti}")
                    em.add("dve", (lambda vb: lambda e: e.tensor_copy(vst[vb][:, :], (ps_s1 if vb == 0 else ps_s2)[:, 0:512]))(vb),
                           reads=["s1" if vb == 0 else "s2"], writes=[f"vst{vb}"], name=f"vev{st0}t{ti}")
                    em.dma("act", "vout", (lambda vb, st0, ti: lambda e: e.dma_start(
                        bass.AP(vscr, (st0 * 512 + ti * 128) * KVD, [[KVD, 128], [1, KVD]]), vst[vb][:, :]))(vb, st0, ti),
                        reads=[f"vst{vb}"], name=f"vw{st0}t{ti}")

            em.drain({"wkv": "pool", "xin": "pool", "kout": "sp", "vout": "act"})
            with nc.Block() as blk:
                em.emit(blk)


    if PHASES >= 3:
        with ExitStack() as ph:
            Ep = ph.enter_context
            x_tm2 = Ep(nc.sbuf_tensor("x_tm2", [128, 4 * D], BF16))
            xfq = Ep(nc.sbuf_tensor("xfq", [128, NCH * 512], BF16))
            wcol = [Ep(nc.sbuf_tensor(f"wcol{i}", [128, NCH * 128], BF16)) for i in range(2)]
            qh = [Ep(nc.sbuf_tensor(f"qh{i}", [128, 512], BF16)) for i in range(2)]
            ksb = Ep(nc.sbuf_tensor("ksb", [128, 64 * 128], BF16))
            vsb = Ep(nc.sbuf_tensor("vsb", [128, 64 * 128], BF16))
            pbuf = [Ep(nc.sbuf_tensor(f"pbuf{i}", [128, 1024], BF16)) for i in range(2)]
            ctx_sb = Ep(nc.sbuf_tensor("ctx_sb", [128, H * 512], BF16))
            usb = Ep(nc.sbuf_tensor("usb", [128, NCH * 512], BF16))
            tmpf = Ep(nc.sbuf_tensor("tmpf", [128, 512], F32))
            tmpb = Ep(nc.sbuf_tensor("tmpb", [128, 512], BF16))
            lrow = Ep(nc.sbuf_tensor("lrow", [1, 512], F32))
            lrow_bf = Ep(nc.sbuf_tensor("lrow_bf", [1, 512], BF16))
            mrow = Ep(nc.sbuf_tensor("mrow", [1, 512], F32))
            mrow_bf = Ep(nc.sbuf_tensor("mrow_bf", [1, 512], BF16))
            rrow = Ep(nc.sbuf_tensor("rrow", [1, 512], F32))
            rrow_bf = Ep(nc.sbuf_tensor("rrow_bf", [1, 512], BF16))
            vrow = Ep(nc.sbuf_tensor("vrow", [1, 512], F32))
            em = Em(nc, top, "a2")

            pend_fin = [None]
            for v in range(NV):
                n = NPAIR[v]
                def emit_xq_load(vv):
                    em.add("pool", (lambda vv: lambda e: e.reg_load(r_g1, sched_i[0:1, 2 + vv:3 + vv]))(vv),
                           reads=["sched"], name=f"ldxmy{vv}")
                    def xload2(e):
                        return e.dma_start(x_tm2[:, :].rearrange("p (t c) -> p t c", t=4),
                                           bass.AP(dx, r_g1, [[D, 128], [128 * D, 4], [1, D]]))
                    em.dma("pool", "xq", xload2, writes=["x2"], name=f"xq{vv}")
                if v == 0:
                    emit_xq_load(0)
                for i in range(4):
                    for half in range(2):
                        for cc in range(8):
                            c = half * 8 + cc
                            em.add("pe", (lambda c, cc, i: lambda e: e.transpose(
                                ps_tb[:, cc * 128:(cc + 1) * 128],
                                x_tm2[:, i * D + c * 128: i * D + (c + 1) * 128], id_bf[:, :]))(c, cc, i),
                                reads=["x2", "id_bf"], writes=["tb"], name=f"qt{v}_{i}")
                        em.add("dve", (lambda i, half: lambda e: e.tensor_copy(
                            xfq[:, :].rearrange("p (c t) -> p c t", c=NCH)[:, half * 8:(half + 1) * 8, i * 128:(i + 1) * 128],
                            ps_tb[:, :].rearrange("p (c t) -> p c t", c=8)))(i, half),
                            reads=["tb"], writes=["xfq"], name=f"qc{v}_{i}")

                ctx_regions = []
                for g in range(G):
                    for p in range(4):
                        kcol = (v * 4 + g) * 4 + p
                        vcol = (v * 4 + g) * 4 + p
                        em.dma("pool", "kvd", (lambda p, kcol: lambda e: e.indirect_dma_start(
                            ksb[:, p * 128:(p + 1) * 128], None,
                            bass.AP(kscr, 0, [[128, KVD * (KSW // 128)], [1, 128]]),
                            IndirectOffsetOnAxis(kdix[:, kcol:kcol + 1], 0)))(p, kcol),
                            reads=["kdix"], writes=[f"ks:{p // 16}"], name=f"kd{v}g{g}p{p}")
                        em.dma("pool", "kvd", (lambda p, vcol: lambda e: e.indirect_dma_start(
                            vsb[:, p * 128:(p + 1) * 128], None,
                            bass.AP(vscr, 0, [[128, KSW * 4], [1, 128]]),
                            IndirectOffsetOnAxis(vdix[:, vcol:vcol + 1], 0)))(p, vcol),
                            reads=["vdix"], writes=[f"vs:{p // 16}"], name=f"vd{v}g{g}p{p}")
                    if DEBUG and v == 0 and g == 0:
                        em.dma("pool", "dbg", lambda e: e.dma_start(d_ksb0.ap(), ksb[:, 0:16 * 128]),
                               reads=["ks:0"], name="dbgk")
                        em.dma("pool", "dbg", lambda e: e.dma_start(d_vsb0.ap(), vsb[:, 0:16 * 128]),
                               reads=["vs:0"], name="dbgv")
                    nq4 = (n - 4) // 4
                    for qq in range(4):
                        s0 = 4 + qq * nq4
                        em.dma("sp", "kv", (lambda g, qq, s0, nq4: lambda e: e.dma_start(
                            ksb[:, s0 * 128:(s0 + nq4) * 128],
                            bass.AP(kscr, g * 128 * KSW + qq * nq4 * 128, [[KSW, 128], [1, nq4 * 128]])))(g, qq, s0, nq4),
                            writes=sorted({f"ks:{sl // 16}" for sl in range(s0, s0 + nq4)}), name=f"kb{v}g{g}q{qq}")
                        em.dma("act", "kv2", (lambda g, qq, s0, nq4: lambda e: e.dma_start(
                            vsb[:, s0 * 128:(s0 + nq4) * 128].rearrange("p (s c) -> p s c", s=nq4),
                            bass.AP(vscr, (qq * nq4 * 128) * KVD + g * 128, [[KVD, 128], [128 * KVD, nq4], [1, 128]])))(g, qq, s0, nq4),
                            writes=sorted({f"vs:{sl // 16}" for sl in range(s0, s0 + nq4)}), name=f"vb{v}g{g}q{qq}")

                    for hh in range(4):
                        h = g * 4 + hh
                        qb_ = h % 2
                        cb = [ps_c1, ps_c3][h % 2]
                        creg = ["c1", "c3"][h % 2]
                        em.dma("pool", "wq", (lambda h, qb_: lambda e: e.dma_start(
                            wcol[qb_][:, :].rearrange("p (c m) -> p c m", c=NCH),
                            bass.AP(dwq, h * 128, [[D, 128], [128 * D, NCH], [1, 128]])))(h, qb_),
                            writes=[f"wcol{qb_}"], name=f"wq{h}")
                        for c in range(NCH):
                            em.add("pe", (lambda c, qb_, cb: lambda e: e.matmul(
                                cb[:, :], wcol[qb_][:, c * 128:(c + 1) * 128], xfq[:, c * 512:(c + 1) * 512],
                                start=(c == 0), stop=(c == NCH - 1)))(c, qb_, cb),
                                reads=[f"wcol{qb_}", "xfq"], writes=[creg], name=f"qmm{h}")
                        em.add("dve", (lambda qb_, cb: lambda e: e.tensor_copy(qh[qb_][:, :], cb[:, :]))(qb_, cb),
                               reads=[creg], writes=[f"qh{qb_}"], name=f"qev{h}")

                        M = n // 2

                        def emit_s_exp(m):
                            sb_ps = ps_s1 if (m % 2 == 0) else ps_s2
                            sregs = ["s1a", "s1b"] if (m % 2 == 0) else ["s2a", "s2b"]
                            pb_ = m % 2
                            for u in range(2):
                                p = 2 * m + u
                                em.add("pe", (lambda p, u, sb_ps, qb_: lambda e: e.matmul(
                                    sb_ps[:, u * 512:(u + 1) * 512],
                                    ksb[:, p * 128:(p + 1) * 128], qh[qb_][:, :],
                                    start=True, stop=True))(p, u, sb_ps, qb_),
                                    reads=[f"ks:{p // 16}", f"qh{qb_}"], writes=[sregs[u]], name=f"smm{h}p{p}")
                            em.add("act", (lambda sb_ps, pb_: lambda e: e.activation(
                                pbuf[pb_][:, :], sb_ps[:, :], AF.Exp, bias=c_shift[:, :], scale=c_scale[:, :]))(sb_ps, pb_),
                                reads=sregs + ["c_shift", "c_scale"], writes=[f"pb{pb_}"], name=f"exp{h}m{m}")
                            for u in range(2):
                                p = 2 * m + u
                                if p < 4:
                                    em.add("pool", (lambda p, u, pb_: lambda e: e.affine_select(
                                        out=pbuf[pb_][:, u * 512:(u + 1) * 512], in_=pbuf[pb_][:, u * 512:(u + 1) * 512],
                                        compare_op=ALU.is_ge, fill=0.0, base=-128 * p,
                                        pattern=[[1, 512]], channel_multiplier=-1))(p, u, pb_),
                                        reads=[f"pb{pb_}"], writes=[f"pb{pb_}"], name=f"diag{h}p{p}")
                                if p >= n - 12:
                                    mc = v * 12 + (p - (n - 12))
                                    em.add("dve", (lambda p, u, pb_, mc: lambda e: e.tensor_scalar(
                                        pbuf[pb_][:, u * 512:(u + 1) * 512], pbuf[pb_][:, u * 512:(u + 1) * 512],
                                        mcol[:, mc:mc + 1], None, ALU.mult))(p, u, pb_, mc),
                                        reads=[f"pb{pb_}", "mcol"], writes=[f"pb{pb_}"], name=f"gmask{h}p{p}")

                        def emit_pv(m, cb=cb, creg=creg, h=h, n=n):
                            pb_ = m % 2
                            for u in range(2):
                                p = 2 * m + u
                                em.add("pe", (lambda p, u, pb_, cb: lambda e: e.matmul(
                                    cb[:, :], vsb[:, p * 128:(p + 1) * 128], pbuf[pb_][:, u * 512:(u + 1) * 512],
                                    start=(p == 0), stop=(p == n - 1)))(p, u, pb_, cb),
                                    reads=[f"vs:{p // 16}", f"pb{pb_}"], writes=[creg], name=f"pv{h}p{p}")
                                em.add("pe", (lambda p, u, pb_: lambda e: e.matmul(
                                    ps_c2[0:1, :], ones_col_bf[:, :], pbuf[pb_][:, u * 512:(u + 1) * 512],
                                    start=(p == 0), stop=(p == n - 1)))(p, u, pb_),
                                    reads=["ones_col", f"pb{pb_}"], writes=["c2"], name=f"lmm{h}p{p}")

                        for m in range(M + 1):
                            if m < M:
                                emit_s_exp(m)
                            if m == 1 and pend_fin[0] is not None:
                                pend_fin[0]()
                                pend_fin[0] = None
                            if m >= 1:
                                emit_pv(m - 1)

                        if DEBUG and v == 0 and h == 0:
                            em.dma("pool", "dbg", (lambda qb_: lambda e: e.dma_start(d_q0.ap(), qh[qb_][:, :]))(qb_),
                                   reads=[f"qh{qb_}"], name="dbgq")
                            em.dma("pool", "dbg", lambda e: e.dma_start(d_p0.ap(), pbuf[0][:, :]),
                                   reads=["pb0"], name="dbgp")

                        def fin(h=h, cb=cb, creg=creg):
                            em.add("dve", lambda e: e.reciprocal(lrow[0:1, :], ps_c2[0:1, :]),
                                   reads=["c2"], writes=["lrow"], name=f"li{h}")
                            em.add("dve", lambda e: e.tensor_copy(lrow_bf[0:1, :], lrow[0:1, :]),
                                   reads=["lrow"], writes=["lrow_bf"], name=f"lb{h}")
                            em.add("dve", (lambda cb: lambda e: e.tensor_copy(tmpf[:, :], cb[:, :]))(cb),
                                   reads=[creg], writes=["tmpf"], name=f"cev{h}")
                            em.add("pe", (lambda cb: lambda e: e.matmul(
                                cb[:, :], ones_row_bf[:, :], lrow_bf[0:1, :], start=True, stop=True))(cb),
                                reads=["ones_row", "lrow_bf"], writes=[creg], name=f"lbc{h}")
                            em.add("dve", (lambda h, cb: lambda e: e.tensor_tensor(
                                ctx_sb[:, h * 512:(h + 1) * 512], tmpf[:, :], cb[:, :], ALU.mult))(h, cb),
                                reads=["tmpf", creg], writes=["ctx"], name=f"cscl{h}")
                        pend_fin[0] = fin

                if pend_fin[0] is not None:
                    pend_fin[0]()
                    pend_fin[0] = None
                if DEBUG and v == 0:
                    em.dma("pool", "dbg", lambda e: e.dma_start(d_l0.ap(), lrow[0:1, :]),
                           reads=["lrow"], name="dbgl")
                    em.dma("pool", "dbg", lambda e: e.dma_start(d_ctx0.ap(), ctx_sb[:, :]),
                           reads=["ctx"], name="dbgctx")
                if v + 1 < NV:
                    emit_xq_load(v + 1)
                for dch in range(NCH):
                    wb = dch % 2
                    em.dma("pool", "wo", (lambda dch, wb: lambda e: e.dma_start(
                        wcol[wb][:, :].rearrange("p (c m) -> p c m", c=NCH),
                        bass.AP(dwo, dch * 128, [[D, 128], [128 * D, NCH], [1, 128]])))(dch, wb),
                        writes=[f"wcol{wb}"], name=f"wo{v}_{dch}")
                    for hc in range(NCH):
                        em.add("pe", (lambda dch, hc, wb: lambda e: e.matmul(
                            ps_s1[:, 0:512], wcol[wb][:, hc * 128:(hc + 1) * 128], ctx_sb[:, hc * 512:(hc + 1) * 512],
                            start=(hc == 0), stop=(hc == NCH - 1)))(dch, hc, wb),
                            reads=[f"wcol{wb}", "ctx"], writes=["s1a"], name=f"womm{v}_{dch}")
                    em.add("dve", (lambda dch: lambda e: e.scalar_tensor_tensor(
                        usb[:, dch * 512:(dch + 1) * 512], ps_s1[:, 0:512], bo_sb[:, dch:dch + 1],
                        xfq[:, dch * 512:(dch + 1) * 512], ALU.add, ALU.add))(dch),
                        reads=["s1a", "bo_sb", "xfq"], writes=["usb"], name=f"ures{v}_{dch}")

                for c in range(NCH):
                    em.add("pe", (lambda c: lambda e: e.matmul(
                        ps_s1[0:1, 0:512], ones_col_bf[:, :], usb[:, c * 512:(c + 1) * 512],
                        start=(c == 0), stop=(c == NCH - 1)))(c),
                        reads=["ones_col", "usb"], writes=["s1a"], name=f"lns{v}")
                for c in range(NCH):
                    em.add("dve", (lambda c: lambda e: e.tensor_mul(
                        tmpb[:, :], usb[:, c * 512:(c + 1) * 512], usb[:, c * 512:(c + 1) * 512]))(c),
                        reads=["usb"], writes=["tmpb"], name=f"sq{v}_{c}")
                    em.add("pe", (lambda c: lambda e: e.matmul(
                        ps_s2[0:1, 0:512], ones_col_bf[:, :], tmpb[:, :],
                        start=(c == 0), stop=(c == NCH - 1)))(c),
                        reads=["ones_col", "tmpb"], writes=["s2a"], name=f"sqs{v}")
                em.add("dve", lambda e: e.tensor_scalar(mrow[0:1, :], ps_s1[0:1, 0:512], 1.0 / D, None, ALU.mult),
                       reads=["s1a"], writes=["mrow"], name=f"mean{v}")
                em.add("dve", lambda e: e.tensor_scalar(vrow[0:1, :], ps_s2[0:1, 0:512], 1.0 / D, None, ALU.mult),
                       reads=["s2a"], writes=["vrow"], name=f"msq{v}")
                em.add("dve", lambda e: e.tensor_mul(lrow[0:1, :], mrow[0:1, :], mrow[0:1, :]),
                       reads=["mrow"], writes=["lrow"], name=f"m2{v}")
                em.add("dve", lambda e: e.tensor_sub(vrow[0:1, :], vrow[0:1, :], lrow[0:1, :]),
                       reads=["vrow", "lrow"], writes=["vrow"], name=f"var{v}")
                em.add("act", lambda e: e.activation(vrow[0:1, :], vrow[0:1, :], AF.Ln, bias=c_eps[0:1, :]),
                       reads=["vrow", "c_eps"], writes=["vrow"], name=f"lnv{v}")
                em.add("act", lambda e: e.activation(rrow[0:1, :], vrow[0:1, :], AF.Exp, scale=c_nhalf[0:1, :]),
                       reads=["vrow", "c_nhalf"], writes=["rrow"], name=f"rstd{v}")
                em.add("dve", lambda e: e.tensor_copy(mrow_bf[0:1, :], mrow[0:1, :]),
                       reads=["mrow"], writes=["mrow_bf"], name=f"mbf{v}")
                em.add("dve", lambda e: e.tensor_copy(rrow_bf[0:1, :], rrow[0:1, :]),
                       reads=["rrow"], writes=["rrow_bf"], name=f"rbf{v}")
                em.add("pe", lambda e: e.matmul(ps_c3[:, :], ones_row_bf[:, :], mrow_bf[0:1, :], start=True, stop=True),
                       reads=["ones_row", "mrow_bf"], writes=["c3"], name=f"mb{v}")
                em.add("pe", lambda e: e.matmul(ps_c2[:, :], ones_row_bf[:, :], rrow_bf[0:1, :], start=True, stop=True),
                       reads=["ones_row", "rrow_bf"], writes=["c2"], name=f"rb{v}")
                if DEBUG and v == 0:
                    em.dma("pool", "dbg", lambda e: e.dma_start(d_usb0.ap(), usb[:, :]),
                           reads=["usb"], name="dbgu")
                for c in range(NCH):
                    em.add("dve", (lambda c: lambda e: e.tensor_sub(
                        tmpb[:, :], usb[:, c * 512:(c + 1) * 512], ps_c3[:, :]))(c),
                        reads=["usb", "c3"], writes=["tmpb"], name=f"ap1{v}_{c}")
                    em.add("dve", (lambda c, v: lambda e: e.tensor_tensor(
                        X1[:, c * TOKC + v * 512: c * TOKC + (v + 1) * 512], tmpb[:, :], ps_c2[:, :], ALU.mult))(c, v),
                        reads=["tmpb", "c2"], writes=[f"X1:{v}"], name=f"ap2{v}_{c}")

            if DEBUG:
                em.dma("pool", "dbg", lambda e: e.dma_start(d_x1.ap(), X1[:, :]),
                       reads=["X1:0", "X1:1", "X1:2", "X1:3"], name="dbgx1")
            em.drain({"xq": "pool", "wq": "pool", "wo": "pool", "kv": "sp", "kv2": "act", "kvd": "pool", "dbg": "pool"})
            with nc.Block() as blk:
                em.emit(blk)


    if PHASES >= 4:
        with ExitStack() as ph:
            Ep = ph.enter_context
            Y = Ep(nc.sbuf_tensor("Y", [128, NCH * TAU], F32))
            Hb = Ep(nc.sbuf_tensor("Hb", [128, ICH * TAU], BF16))
            w1c = [Ep(nc.sbuf_tensor(f"w1c{i}", [128, NCH * 128], BF16)) for i in range(2)]
            w3c = [Ep(nc.sbuf_tensor(f"w3c{i}", [128, NCH * 128], BF16)) for i in range(2)]
            w2c = [Ep(nc.sbuf_tensor(f"w2c{i}", [128, ICH * 128], BF16)) for i in range(3)]
            gsc = Ep(nc.sbuf_tensor("gsc", [128, NE], F32))
            gbs = Ep(nc.sbuf_tensor("gbs", [128, NE], F32))
            grp = Ep(nc.sbuf_tensor("grp", [128, 4], F32))
            grp2 = Ep(nc.sbuf_tensor("grp2", [128, 4], F32))
            msc = Ep(nc.sbuf_tensor("msc", [128, NE], F32))
            cur = Ep(nc.sbuf_tensor("cur", [128, NE], F32))
            indt = Ep(nc.sbuf_tensor("indt", [128, NE], F32))
            sc1 = Ep(nc.sbuf_tensor("sc1", [128, 1], F32))
            sc2 = Ep(nc.sbuf_tensor("sc2", [128, 1], F32))
            cw = Ep(nc.sbuf_tensor("cw", [128, 8 * NE], F32))
            cwb16 = Ep(nc.sbuf_tensor("cwb16", [128, NE], BF16))
            cwT = Ep(nc.sbuf_tensor("cwT", [16, 1024], BF16))
            cwB = [Ep(nc.sbuf_tensor(f"cwB{i}", [128, 512], F32)) for i in range(2)]
            silu_t = Ep(nc.sbuf_tensor("silu_t", [128, 512], F32))
            btmp = Ep(nc.sbuf_tensor("btmp", [128, 512], F32))
            outm = Ep(nc.sbuf_tensor("outm", [128, D], F32))
            qtmp = Ep(nc.sbuf_tensor("qtmp", [128, 4], F32))
            srow = Ep(nc.sbuf_tensor("srow", [128, 1], F32))
            qrow = Ep(nc.sbuf_tensor("qrow", [128, 1], F32))
            vr2 = Ep(nc.sbuf_tensor("vr2", [128, 1], F32))
            em = Em(nc, top, "bb")

            for tau in range(2):
                t0 = tau * TAU
                for sub in range(8):
                    tok0 = t0 + sub * 128
                    for c in range(NCH):
                        em.add("pe", (lambda c, tok0: lambda e: e.matmul(
                            ps_c2[:, 0:NE], X1[:, c * TOKC + tok0: c * TOKC + tok0 + 128],
                            gw_fm[:, c * NE:(c + 1) * NE],
                            start=(c == 0), stop=(c == NCH - 1)))(c, tok0),
                            reads=[f"X1:{2 * tau}", f"X1:{2 * tau + 1}"], writes=["c2"], name=f"gmm{tau}_{sub}")
                    em.add("act", lambda e: e.activation(gsc[:, :], ps_c2[:, 0:NE], AF.Sigmoid),
                           reads=["c2"], writes=["gsc"], name=f"sig{tau}_{sub}")
                    em.add("dve", lambda e: e.tensor_add(gbs[:, :], gsc[:, :], ebias_b[:, :]),
                           reads=["gsc", "ebias_b"], writes=["gbs"], name=f"gb{tau}_{sub}")
                    for gg in range(4):
                        em.add("dve", (lambda gg: lambda e: e.reduce_max(
                            grp[:, gg:gg + 1], gbs[:, 4 * gg:4 * (gg + 1)], axis=AX.X))(gg),
                            reads=["gbs"], writes=["grp"], name=f"gmax{tau}_{sub}_{gg}")
                    em.add("dve", lambda e: e.reduce_max(sc1[:, :], grp[:, :], axis=AX.X),
                           reads=["grp"], writes=["sc1"], name=f"m1_{tau}_{sub}")
                    em.add("dve", lambda e: e.tensor_scalar(grp2[:, :], grp[:, :], sc1[:, 0:1], None, ALU.is_equal),
                           reads=["grp", "sc1"], writes=["grp2"], name=f"e1_{tau}_{sub}")
                    em.add("dve", lambda e: e.scalar_tensor_tensor(grp2[:, :], grp2[:, :], -BIG, grp[:, :], ALU.mult, ALU.add),
                           reads=["grp2", "grp"], writes=["grp2"], name=f"g2_{tau}_{sub}")
                    em.add("dve", lambda e: e.reduce_max(sc2[:, :], grp2[:, :], axis=AX.X),
                           reads=["grp2"], writes=["sc2"], name=f"m2_{tau}_{sub}")
                    for gg in range(4):
                        em.add("dve", (lambda gg: lambda e: e.tensor_scalar(
                            sc1[:, :], grp[:, gg:gg + 1], sc2[:, 0:1], None, ALU.is_ge))(gg),
                            reads=["grp", "sc2"], writes=["sc1"], name=f"gi{tau}_{sub}_{gg}")
                        em.add("dve", (lambda gg: lambda e: e.tensor_scalar(
                            sc1[:, :], sc1[:, :], BIG, None, ALU.mult))(gg),
                            reads=["sc1"], writes=["sc1"], name=f"gp{tau}_{sub}_{gg}")
                        em.add("dve", (lambda gg: lambda e: e.tensor_scalar(
                            sc1[:, :], sc1[:, :], -BIG, None, ALU.add))(gg),
                            reads=["sc1"], writes=["sc1"], name=f"gp2{tau}_{sub}_{gg}")
                        em.add("dve", (lambda gg: lambda e: e.tensor_scalar(
                            msc[:, 4 * gg:4 * (gg + 1)], gbs[:, 4 * gg:4 * (gg + 1)], sc1[:, 0:1], None, ALU.add))(gg),
                            reads=["gbs", "sc1"], writes=["msc"], name=f"gm{tau}_{sub}_{gg}")
                    em.add("dve", lambda e: e.tensor_copy(cur[:, :], msc[:, :]),
                           reads=["msc"], writes=["cur"], name=f"cp{tau}_{sub}")
                    for it in range(3):
                        em.add("dve", lambda e: e.reduce_max(sc1[:, :], cur[:, :], axis=AX.X),
                               reads=["cur"], writes=["sc1"], name=f"tm{tau}_{sub}_{it}")
                        em.add("dve", lambda e: e.tensor_scalar(indt[:, :], cur[:, :], sc1[:, 0:1], None, ALU.is_ge),
                               reads=["cur", "sc1"], writes=["indt"], name=f"ti{tau}_{sub}_{it}")
                        em.add("dve", lambda e: e.scalar_tensor_tensor(cur[:, :], indt[:, :], -BIG, cur[:, :], ALU.mult, ALU.add),
                               reads=["indt", "cur"], writes=["cur"], name=f"tc{tau}_{sub}_{it}")
                    em.add("dve", lambda e: e.reduce_max(sc1[:, :], cur[:, :], axis=AX.X),
                           reads=["cur"], writes=["sc1"], name=f"m4_{tau}_{sub}")
                    em.add("dve", lambda e: e.tensor_scalar(indt[:, :], msc[:, :], sc1[:, 0:1], None, ALU.is_ge),
                           reads=["msc", "sc1"], writes=["indt"], name=f"ind{tau}_{sub}")
                    em.add("dve", lambda e: e.tensor_mul(indt[:, :], indt[:, :], gsc[:, :]),
                           reads=["indt", "gsc"], writes=["indt"], name=f"w_{tau}_{sub}")
                    em.add("dve", lambda e: e.reduce_sum(sc1[:, :], indt[:, :], axis=AX.X),
                           reads=["indt"], writes=["sc1"], name=f"ws{tau}_{sub}")
                    em.add("dve", lambda e: e.reciprocal(sc1[:, :], sc1[:, :]),
                           reads=["sc1"], writes=["sc1"], name=f"wr{tau}_{sub}")
                    em.add("dve", (lambda sub: lambda e: e.tensor_scalar(
                        cw[:, sub * NE:(sub + 1) * NE], indt[:, :], sc1[:, 0:1], None, ALU.mult))(sub),
                        reads=["indt", "sc1"], writes=["cw"], name=f"cw{tau}_{sub}")
                    em.add("dve", (lambda sub: lambda e: e.tensor_copy(cwb16[:, :], cw[:, sub * NE:(sub + 1) * NE]))(sub),
                           reads=["cw"], writes=["cwb16"], name=f"cwb{tau}_{sub}")
                    em.add("pe", (lambda sub: lambda e: e.transpose(
                        ps_tb[0:16, sub * 128:(sub + 1) * 128], cwb16[:, 0:NE], id_bf[:, :]))(sub),
                        reads=["cwb16", "id_bf"], writes=["tb"], name=f"cwt{tau}_{sub}")
                em.add("dve", lambda e: e.tensor_copy(cwT[0:16, :], ps_tb[0:16, :]),
                       reads=["tb"], writes=["cwT"], name=f"cwc{tau}")

                for ex in range(NE):
                    for s in range(2):
                        em.add("pe", (lambda ex, s: lambda e: e.matmul(
                            ps_c3[:, :], oneh[:, ex * 128:(ex + 1) * 128], cwT[0:16, s * 512:(s + 1) * 512],
                            start=True, stop=True))(ex, s),
                            reads=["oneh", "cwT"], writes=["c3"], name=f"cb{tau}e{ex}s{s}")
                        em.add("dve", (lambda s: lambda e: e.tensor_copy(cwB[s][:, :], ps_c3[:, :]))(s),
                               reads=["c3"], writes=[f"cwB{s}"], name=f"cbe{tau}e{ex}s{s}")
                    for mch in range(ICH):
                        wb = mch % 2
                        em.dma("pool", "wex", (lambda ex, mch, wb: lambda e: e.dma_start(
                            w1c[wb][:, :].rearrange("p (c m) -> p c m", c=NCH),
                            bass.AP(dw1, ex * D * INTER + mch * 128, [[INTER, 128], [128 * INTER, NCH], [1, 128]])))(ex, mch, wb),
                            writes=[f"w1c{wb}"], name=f"w1{tau}e{ex}m{mch}")
                        em.dma("pool", "wex", (lambda ex, mch, wb: lambda e: e.dma_start(
                            w3c[wb][:, :].rearrange("p (c m) -> p c m", c=NCH),
                            bass.AP(dw3, ex * D * INTER + mch * 128, [[INTER, 128], [128 * INTER, NCH], [1, 128]])))(ex, mch, wb),
                            writes=[f"w3c{wb}"], name=f"w3{tau}e{ex}m{mch}")
                        a_ps = [ps_s1, ps_s2][mch % 2]
                        a_reg = ["s1a", "s1b"] if mch % 2 == 0 else ["s2a", "s2b"]
                        for s in range(2):
                            for c in range(NCH):
                                em.add("pe", (lambda c, s, wb, t0, a_ps: lambda e: e.matmul(
                                    a_ps[:, s * 512:(s + 1) * 512], w1c[wb][:, c * 128:(c + 1) * 128],
                                    X1[:, c * TOKC + t0 + s * 512: c * TOKC + t0 + (s + 1) * 512],
                                    start=(c == 0), stop=(c == NCH - 1)))(c, s, wb, t0, a_ps),
                                    reads=[f"w1c{wb}", f"X1:{2 * tau}", f"X1:{2 * tau + 1}"], writes=[a_reg[s]], name=f"a{tau}e{ex}m{mch}s{s}")
                                em.add("pe", (lambda c, s, wb, t0: lambda e: e.matmul(
                                    [ps_c2, ps_c3][s][:, :], w3c[wb][:, c * 128:(c + 1) * 128],
                                    X1[:, c * TOKC + t0 + s * 512: c * TOKC + t0 + (s + 1) * 512],
                                    start=(c == 0), stop=(c == NCH - 1)))(c, s, wb, t0),
                                    reads=[f"w3c{wb}", f"X1:{2 * tau}", f"X1:{2 * tau + 1}"], writes=[["c2", "c3"][s]], name=f"b{tau}e{ex}m{mch}s{s}")
                            em.add("act", (lambda s, a_ps: lambda e: e.activation(
                                silu_t[:, :], a_ps[:, s * 512:(s + 1) * 512], AF.Silu))(s, a_ps),
                                reads=[a_reg[s]], writes=["silu_t"], name=f"si{tau}e{ex}m{mch}s{s}")
                            em.add("dve", (lambda s: lambda e: e.tensor_tensor(
                                btmp[:, :], cwB[s][:, :], [ps_c2, ps_c3][s][:, :], ALU.mult))(s),
                                reads=[f"cwB{s}", ["c2", "c3"][s]], writes=["btmp"], name=f"bt{tau}e{ex}m{mch}s{s}")
                            em.add("dve", (lambda s, mch: lambda e: e.tensor_tensor(
                                Hb[:, mch * TAU + s * 512: mch * TAU + (s + 1) * 512], silu_t[:, :], btmp[:, :], ALU.mult))(s, mch),
                                reads=["silu_t", "btmp"], writes=[f"Hb:{mch}"], name=f"h{tau}e{ex}m{mch}s{s}")
                    for dch in range(NCH):
                        wb = dch % 3
                        em.dma("pool", "wex", (lambda ex, dch, wb: lambda e: e.dma_start(
                            w2c[wb][:, :].rearrange("p (c m) -> p c m", c=ICH),
                            bass.AP(dw2, ex * INTER * D + dch * 128, [[D, 128], [128 * D, ICH], [1, 128]])))(ex, dch, wb),
                            writes=[f"w2c{wb}"], name=f"w2{tau}e{ex}d{dch}")
                        for s in range(2):
                            yb = (2 * dch + s) % 2
                            y_ps = [ps_c1, ps_s1][yb]
                            y_reg = ["c1", "s1a"][yb]
                            for mch in range(ICH):
                                em.add("pe", (lambda mch, s, wb, y_ps: lambda e: e.matmul(
                                    y_ps[:, 0:512], w2c[wb][:, mch * 128:(mch + 1) * 128],
                                    Hb[:, mch * TAU + s * 512: mch * TAU + (s + 1) * 512],
                                    start=(mch == 0), stop=(mch == ICH - 1)))(mch, s, wb, y_ps),
                                    reads=[f"w2c{wb}", f"Hb:{mch}"], writes=[y_reg], name=f"y{tau}e{ex}d{dch}s{s}")
                            if ex == 0:
                                em.add("dve", (lambda dch, s, y_ps: lambda e: e.tensor_copy(
                                    Y[:, dch * TAU + s * 512: dch * TAU + (s + 1) * 512], y_ps[:, 0:512]))(dch, s, y_ps),
                                    reads=[y_reg], writes=[f"Y:{dch}"], name=f"ye{tau}e{ex}d{dch}s{s}")
                            else:
                                em.add("dve", (lambda dch, s, y_ps: lambda e: e.tensor_tensor(
                                    Y[:, dch * TAU + s * 512: dch * TAU + (s + 1) * 512],
                                    Y[:, dch * TAU + s * 512: dch * TAU + (s + 1) * 512], y_ps[:, 0:512], ALU.add))(dch, s, y_ps),
                                    reads=[y_reg, f"Y:{dch}"], writes=[f"Y:{dch}"], name=f"ya{tau}e{ex}d{dch}s{s}")

                for half in range(2):
                    mlist = list(range(half * 11, min(SCH, half * 11 + 11)))
                    for mi, mch in enumerate(mlist):
                        wb = mi % 2
                        em.dma("pool", "wex", (lambda mch, wb: lambda e: e.dma_start(
                            w1c[wb][:, :].rearrange("p (c m) -> p c m", c=NCH),
                            bass.AP(dsw1, mch * 128, [[SINTER, 128], [128 * SINTER, NCH], [1, 128]])))(mch, wb),
                            writes=[f"w1c{wb}"], name=f"sw1{tau}m{mch}")
                        em.dma("pool", "wex", (lambda mch, wb: lambda e: e.dma_start(
                            w3c[wb][:, :].rearrange("p (c m) -> p c m", c=NCH),
                            bass.AP(dsw3, mch * 128, [[SINTER, 128], [128 * SINTER, NCH], [1, 128]])))(mch, wb),
                            writes=[f"w3c{wb}"], name=f"sw3{tau}m{mch}")
                        a_ps = [ps_s1, ps_s2][mi % 2]
                        a_reg = ["s1a", "s1b"] if mi % 2 == 0 else ["s2a", "s2b"]
                        for s in range(2):
                            for c in range(NCH):
                                em.add("pe", (lambda c, s, wb, t0, a_ps: lambda e: e.matmul(
                                    a_ps[:, s * 512:(s + 1) * 512], w1c[wb][:, c * 128:(c + 1) * 128],
                                    X1[:, c * TOKC + t0 + s * 512: c * TOKC + t0 + (s + 1) * 512],
                                    start=(c == 0), stop=(c == NCH - 1)))(c, s, wb, t0, a_ps),
                                    reads=[f"w1c{wb}", f"X1:{2 * tau}", f"X1:{2 * tau + 1}"], writes=[a_reg[s]], name=f"sa{tau}m{mch}s{s}")
                                em.add("pe", (lambda c, s, wb, t0: lambda e: e.matmul(
                                    [ps_c2, ps_c3][s][:, :], w3c[wb][:, c * 128:(c + 1) * 128],
                                    X1[:, c * TOKC + t0 + s * 512: c * TOKC + t0 + (s + 1) * 512],
                                    start=(c == 0), stop=(c == NCH - 1)))(c, s, wb, t0),
                                    reads=[f"w3c{wb}", f"X1:{2 * tau}", f"X1:{2 * tau + 1}"], writes=[["c2", "c3"][s]], name=f"sb{tau}m{mch}s{s}")
                            em.add("act", (lambda s, a_ps: lambda e: e.activation(
                                silu_t[:, :], a_ps[:, s * 512:(s + 1) * 512], AF.Silu))(s, a_ps),
                                reads=[a_reg[s]], writes=["silu_t"], name=f"ssi{tau}m{mch}s{s}")
                            em.add("dve", (lambda s, mi: lambda e: e.tensor_tensor(
                                Hb[:, mi * TAU + s * 512: mi * TAU + (s + 1) * 512], silu_t[:, :],
                                [ps_c2, ps_c3][s][:, :], ALU.mult))(s, mi),
                                reads=["silu_t", ["c2", "c3"][s]], writes=[f"Hb:{mi}"], name=f"sh{tau}m{mch}s{s}")
                    for dch in range(NCH):
                        wb = dch % 3
                        em.dma("pool", "wex", (lambda dch, wb, half: lambda e: e.dma_start(
                            w2c[wb][:, 0:len(mlist) * 128].rearrange("p (c m) -> p c m", c=len(mlist)),
                            bass.AP(dsw2, (half * 11 * 128) * D + dch * 128, [[D, 128], [128 * D, len(mlist)], [1, 128]])))(dch, wb, half),
                            writes=[f"w2c{wb}"], name=f"sw2{tau}h{half}d{dch}")
                        for s in range(2):
                            yb = (2 * dch + s) % 2
                            y_ps = [ps_c1, ps_s1][yb]
                            y_reg = ["c1", "s1a"][yb]
                            for mi in range(len(mlist)):
                                em.add("pe", (lambda mi, s, wb, y_ps, nm=len(mlist): lambda e: e.matmul(
                                    y_ps[:, 0:512], w2c[wb][:, mi * 128:(mi + 1) * 128],
                                    Hb[:, mi * TAU + s * 512: mi * TAU + (s + 1) * 512],
                                    start=(mi == 0), stop=(mi == nm - 1)))(mi, s, wb, y_ps),
                                    reads=[f"w2c{wb}", f"Hb:{mi}"], writes=[y_reg], name=f"sy{tau}h{half}d{dch}s{s}")
                            em.add("dve", (lambda dch, s, y_ps: lambda e: e.tensor_tensor(
                                Y[:, dch * TAU + s * 512: dch * TAU + (s + 1) * 512],
                                Y[:, dch * TAU + s * 512: dch * TAU + (s + 1) * 512], y_ps[:, 0:512], ALU.add))(dch, s, y_ps),
                                reads=[y_reg, f"Y:{dch}"], writes=[f"Y:{dch}"], name=f"sya{tau}h{half}d{dch}s{s}")

                for c in range(NCH):
                    em.add("dve", (lambda c, t0: lambda e: e.tensor_tensor(
                        Y[:, c * TAU:(c + 1) * TAU], Y[:, c * TAU:(c + 1) * TAU],
                        X1[:, c * TOKC + t0: c * TOKC + t0 + TAU], ALU.add))(c, t0),
                        reads=[f"Y:{c}", f"X1:{2 * tau}", f"X1:{2 * tau + 1}"], writes=[f"Y:{c}"], name=f"yx{tau}_{c}")
                for ti in range(8):
                    for q in range(4):
                        tps_ = [ps_c3, ps_c2][q % 2]
                        treg = ["c3", "c2"][q % 2]
                        for cc in range(4):
                            c = q * 4 + cc
                            em.add("pe", (lambda c, cc, ti, tps_: lambda e: e.transpose(
                                tps_[:, cc * 128:(cc + 1) * 128], Y[:, c * TAU + ti * 128: c * TAU + (ti + 1) * 128],
                                id_f[:, :]))(c, cc, ti, tps_),
                                reads=[f"Y:{c}", "id_f"], writes=[treg], name=f"ot{tau}_{ti}_{c}")
                        em.add("dve", (lambda q, ti, tps_, treg: lambda e: e.tensor_copy(
                            outm[:, q * 512:(q + 1) * 512], tps_[:, :]))(q, ti, tps_, treg),
                            reads=[treg], writes=["outm"], name=f"oc{tau}_{ti}_{q}")
                    em.add("dve", (lambda: lambda e: e.reduce_sum(srow[:, :], outm[:, :], axis=AX.X))(),
                           reads=["outm"], writes=["srow"], name=f"os{tau}_{ti}")
                    for jj in range(4):
                        em.add("dve", (lambda jj: lambda e: e.tensor_mul(
                            silu_t[:, :], outm[:, jj * 512:(jj + 1) * 512], outm[:, jj * 512:(jj + 1) * 512]))(jj),
                            reads=["outm"], writes=["silu_t"], name=f"oq{tau}_{ti}_{jj}")
                        em.add("dve", (lambda jj: lambda e: e.reduce_sum(
                            qtmp[:, jj:jj + 1], silu_t[:, :], axis=AX.X))(jj),
                            reads=["silu_t"], writes=["qtmp"], name=f"oqs{tau}_{ti}_{jj}")
                    em.add("dve", lambda e: e.tensor_add(qtmp[:, 0:1], qtmp[:, 0:1], qtmp[:, 1:2]),
                           reads=["qtmp"], writes=["qtmp"], name=f"oqa{tau}_{ti}")
                    em.add("dve", lambda e: e.tensor_add(qtmp[:, 2:3], qtmp[:, 2:3], qtmp[:, 3:4]),
                           reads=["qtmp"], writes=["qtmp"], name=f"oqb{tau}_{ti}")
                    em.add("dve", lambda e: e.tensor_add(qrow[:, :], qtmp[:, 0:1], qtmp[:, 2:3]),
                           reads=["qtmp"], writes=["qrow"], name=f"oqc{tau}_{ti}")
                    em.add("dve", lambda e: e.tensor_scalar(srow[:, :], srow[:, :], 1.0 / D, None, ALU.mult),
                           reads=["srow"], writes=["srow"], name=f"om{tau}_{ti}")
                    em.add("dve", lambda e: e.tensor_scalar(qrow[:, :], qrow[:, :], 1.0 / D, None, ALU.mult),
                           reads=["qrow"], writes=["qrow"], name=f"oq2{tau}_{ti}")
                    em.add("dve", lambda e: e.tensor_mul(vr2[:, :], srow[:, :], srow[:, :]),
                           reads=["srow"], writes=["vr2"], name=f"ov{tau}_{ti}")
                    em.add("dve", lambda e: e.tensor_sub(vr2[:, :], qrow[:, :], vr2[:, :]),
                           reads=["qrow", "vr2"], writes=["vr2"], name=f"ov2{tau}_{ti}")
                    em.add("act", lambda e: e.activation(vr2[:, :], vr2[:, :], AF.Ln, bias=c_eps[:, :]),
                           reads=["vr2", "c_eps"], writes=["vr2"], name=f"oln{tau}_{ti}")
                    em.add("act", lambda e: e.activation(vr2[:, :], vr2[:, :], AF.Exp, scale=c_nhalf[:, :]),
                           reads=["vr2", "c_nhalf"], writes=["vr2"], name=f"orst{tau}_{ti}")
                    em.add("dve", lambda e: e.tensor_scalar(outm[:, :], outm[:, :], srow[:, 0:1], vr2[:, 0:1], ALU.subtract, ALU.mult),
                           reads=["outm", "srow", "vr2"], writes=["outm"], name=f"oap{tau}_{ti}")
                    em.dma("sp", "out", (lambda tau, ti: lambda e: e.dma_start(
                        bass.AP(dout, (tau * TAU + ti * 128) * D, [[D, 128], [1, D]]), outm[:, :]))(tau, ti),
                        reads=["outm"], name=f"ow{tau}_{ti}")

            em.dma("sp", "out", lambda e: e.dma_start(doid.ap(), sched_i[0:1, 0:1]), reads=["sched"], name="oid")
            em.drain({"wex": "pool", "out": "sp"})
            with nc.Block() as blk:
                em.emit(blk)


    _CACHED["nc"] = nc
    return nc


def kernel(L0_attn_x, L0_attn_Wq, L0_attn_Wk, L0_attn_Wv, L0_attn_Wo, L0_attn_bo,
           L0_ffn_gate_w, L0_ffn_expert_bias, L0_ffn_w1, L0_ffn_w2, L0_ffn_w3,
           L0_ffn_sw1, L0_ffn_sw2, L0_ffn_sw3):
    nc = _build()
    inp = {
        "x": np.ascontiguousarray(L0_attn_x, dtype=np.float32),
        "wq": np.ascontiguousarray(L0_attn_Wq, dtype=np.float32),
        "wk": np.ascontiguousarray(L0_attn_Wk, dtype=np.float32),
        "wv": np.ascontiguousarray(L0_attn_Wv, dtype=np.float32),
        "wo": np.ascontiguousarray(L0_attn_Wo, dtype=np.float32),
        "bo": np.ascontiguousarray(L0_attn_bo, dtype=np.float32),
        "gw": np.ascontiguousarray(L0_ffn_gate_w, dtype=np.float32),
        "eb": np.ascontiguousarray(L0_ffn_expert_bias, dtype=np.float32),
        "w1": np.ascontiguousarray(L0_ffn_w1, dtype=np.float32),
        "w2": np.ascontiguousarray(L0_ffn_w2, dtype=np.float32),
        "w3": np.ascontiguousarray(L0_ffn_w3, dtype=np.float32),
        "sw1": np.ascontiguousarray(L0_ffn_sw1, dtype=np.float32),
        "sw2": np.ascontiguousarray(L0_ffn_sw2, dtype=np.float32),
        "sw3": np.ascontiguousarray(L0_ffn_sw3, dtype=np.float32),
    }
    res = run_bass_kernel_spmd(nc, [dict(inp) for _ in range(8)], core_ids=list(range(8)))
    out = np.empty((B, T, D), dtype=np.float32)
    seen = set()
    for c in range(8):
        r = res.results[c]
        w = int(np.asarray(r["oid"]).reshape(-1)[0])
        assert 0 <= w < 8 and w not in seen, f"bad workload ids: {w}, {seen}"
        seen.add(w)
        j, b = w % 4, w // 4
        om = np.asarray(r["out_my"]).reshape(TOKC, D)
        for v in range(NV):
            q0 = q0tok(w, v)
            out[b, q0:q0 + QB, :] = om[v * QB:(v + 1) * QB, :]
    return out
